# version 02
# baseline (speedup 1.0000x reference)
import numpy as np
import ml_dtypes
import concourse.bass as bass
import concourse.mybir as mybir
from concourse.bass_utils import run_bass_kernel_spmd

F32 = mybir.dt.float32
BF16 = mybir.dt.bfloat16
AF = mybir.ActivationFunctionType
ALU = mybir.AluOpType
AX = mybir.AxisListType

NC = 8
T = 1024
TH = 1026
DM = 4096
DC = 32
W1 = 1024
H = 16
DH = 64
EPS = 1e-5
INV_D = 1.0 / DM

ENGMAP = {"T": "tensor", "V": "vector", "S": "scalar", "Q": "gpsimd", "Y": "sync"}
ENGSEM = {"T": "tp", "V": "vp", "S": "sp", "Q": None, "Y": None}
KD = 4
DMASEMS = [f"q{i}" for i in range(KD)] + [f"y{i}" for i in range(KD)]


class Prog:

    def __init__(self):
        self.ops = []
        self.cnt = {k: 0 for k in ("tp", "vp", "sp", *DMASEMS)}
        self.ev = {}
        self.dq = {"Q": 0, "Y": 0}

    def op(self, eng, fn, waits=(), name=None, inc=True, dma=False):
        wl = []
        for w in waits:
            if w is None:
                continue
            if isinstance(w, str):
                wl.append(self.ev[w])
            else:
                wl.append(w)
        if dma:
            pref = "q" if eng == "Q" else "y"
            sem = f"{pref}{self.dq[eng] % KD}"
            self.dq[eng] += 1
            if self.cnt[sem] > 0:
                wl.append((sem, self.cnt[sem]))
            self.cnt[sem] += 16
            if name is not None:
                self.ev[name] = (sem, self.cnt[sem])
            self.ops.append((eng, fn, wl, (sem, 16)))
            return
        sem = ENGSEM[eng]
        if inc:
            self.cnt[sem] += 1
            if name is not None:
                self.ev[name] = (sem, self.cnt[sem])
        else:
            assert name is None
        self.ops.append((eng, fn, wl, (sem, 1) if inc else None))

    def qsnap(self, eng):
        pref = "q" if eng == "Q" else "y"
        return [(f"{pref}{i}", self.cnt[f"{pref}{i}"]) for i in range(KD)
                if self.cnt[f"{pref}{i}"] > 0]

    def emit(self, block, sems):
        per = {}
        for rec in self.ops:
            per.setdefault(rec[0], []).append(rec)
        for ename, recs in per.items():
            own = ENGSEM[ename]

            def body(eng, recs=recs, own=own):
                maxw = {}
                for (_e, fn, waits, inc) in recs:
                    for (s, v) in waits:
                        if own is not None and s == own:
                            continue
                        if maxw.get(s, 0) < v:
                            eng.wait_ge(sems[s], v)
                            maxw[s] = v
                    inst = fn(eng)
                    if inc is not None:
                        s, a = inc
                        inst.then_inc(sems[s], a)

            getattr(block, ENGMAP[ename])(body)


def build(upto=99, taps=False, agroups=64):
    nc = bass.Bass("TRN2", target_bir_lowering=False)

    def din(name, shape, dt=F32):
        return nc.dram_tensor(name, list(shape), dt, kind="ExternalInput").ap()

    xt = din("xt", (DM, TH))
    cw_d = din("cw", (24, 128, 12288))
    wq_d = din("wqr", (8, 128, 1024))
    wk_d = din("wkr", (8, 128, 1024))
    wv_d = din("wvr", (8, 128, 1024))
    wo_d = din("wor", (8, 128, 1024))
    pj_d = din("pjr", (32, 128, 1024))
    wb_d = din("wbr", (32, 128, 4096))
    a_d = din("ar", (128, 4096))
    b_d = din("br", (128, 4096))
    wr_d = din("wrr", (128, 256))
    bv_d = din("bvr", (1, 1024))
    bias_d = din("biases", (128, 113))
    ident_d = din("identc", (128, 128), BF16)
    mask4_d = din("mask4c", (128, 128), BF16)
    onec_d = din("onecolc", (128, 1), BF16)
    oner_d = din("onerowc", (1, 128), BF16)
    emat_d = din("ematc", (8, 128), BF16)
    out_d = nc.dram_tensor("out", [DM, T], F32, kind="ExternalOutput").ap()
    tapd = {}
    if taps:
        tapd["t1"] = nc.dram_tensor("tap_t1", [128, 32 * 1026], BF16, kind="ExternalOutput").ap()
        tapd["t2"] = nc.dram_tensor("tap_t2", [128, 32 * 1024], BF16, kind="ExternalOutput").ap()
        tapd["rows"] = nc.dram_tensor("tap_rows", [1, 4 * TH], F32, kind="ExternalOutput").ap()
        tapd["lgtk"] = nc.dram_tensor("tap_lgtk", [128, 64], F32, kind="ExternalOutput").ap()
        tapd["comb2"] = nc.dram_tensor("tap_comb2", [128, 64], BF16, kind="ExternalOutput").ap()
        tapd["hw"] = nc.dram_tensor("tap_hw", [128, 1024], BF16, kind="ExternalOutput").ap()
        tapd["isbc"] = nc.dram_tensor("tap_isbc", [128, 1026], BF16, kind="ExternalOutput").ap()
        tapd["att"] = nc.dram_tensor("tap_att", [128, 32], F32, kind="ExternalOutput").ap()
        tapd["SA"] = nc.dram_tensor("tap_SA", [128, 1024], BF16, kind="ExternalOutput").ap()
        tapd["PA"] = nc.dram_tensor("tap_PA", [128, 1024], BF16, kind="ExternalOutput").ap()
        tapd["PB"] = nc.dram_tensor("tap_PB", [128, 512], BF16, kind="ExternalOutput").ap()
        tapd["PnT"] = nc.dram_tensor("tap_PnT", [128, 512], BF16, kind="ExternalOutput").ap()
        tapd["mobc"] = nc.dram_tensor("tap_mobc", [128, 1026], BF16, kind="ExternalOutput").ap()

    from contextlib import ExitStack
    with ExitStack() as st:
        def sb(name, shape, dt):
            return st.enter_context(nc.sbuf_tensor(name, shape, dt))

        T1 = sb("T1", [128, 32 * TH], BF16)
        T2 = sb("T2", [128, 32 * 1024], BF16)
        WA = sb("WA", [128, 12544], BF16)
        x2b = sb("x2b", [128, 2 * TH], BF16)
        RR = sb("RR", [128, 4 * TH], F32)
        rbf = sb("rbf", [128, 2 * TH], BF16)
        isbc = sb("isbc", [128, TH], BF16)
        mobc = sb("mobc", [128, TH], BF16)
        nc._isbc_tap = isbc
        nc._mobc_tap = mobc
        att4 = sb("att4", [128, 16], F32)
        att4b = sb("att4b", [128, 20], F32)
        PA = sb("PA", [128, 2 * 512], BF16)
        SA = sb("SA", [128, 2 * 512], BF16)
        PB = sb("PB", [128, 512], BF16)
        PnT = sb("PnT", [128, 512], BF16)
        xr = sb("xr", [128, 2 * 1024], F32)
        outb = sb("outb", [128, 2 * 1024], F32)
        lgtk = sb("lgtk", [128, 64], F32)
        cw = sb("cwX", [128, 52], F32)
        cwk = sb("cwk", [128, 320], F32)
        comb2 = sb("comb2", [128, 64], BF16)
        wrsb = sb("wrsb", [128, 256], BF16)
        bvbf = rbf[0:1, 0:1024]
        bvbc = isbc[:, 0:1024]
        cbc = mobc[:, 0:1024]
        hwsb = PA[:, 0:1024]
        lgfm = PA[0:8, 0:1024]
        bia = sb("bia", [128, 113], F32)
        ident = sb("ident", [128, 128], BF16)
        mask4 = sb("mask4", [128, 128], BF16)
        onec = sb("onec", [128, 1], BF16)
        oner = sb("oner", [1, 128], BF16)
        emat = sb("emat", [8, 128], BF16)

        def ps(name, shape, dt):
            return st.enter_context(nc.psum_tensor(name, shape, dt))

        pb = [ps(f"pb{i}", [128, 512], F32) for i in range(6)]
        pbt = [ps(f"pbt{i}", [128, 1024], BF16) for i in range(2)]

        sems = {k: st.enter_context(nc.semaphore(k))
                for k in ("tp", "vp", "sp", *DMASEMS)}
        block = st.enter_context(nc.Block())

        p = Prog()
        bank_free = {}

        def bfree(bids):
            return [bank_free[b] for b in bids if b in bank_free]

        def brel(bids, evname):
            for b in bids:
                bank_free[b] = evname

        def vset(name, waits=()):
            p.op("V", lambda e: e.tensor_copy(cw[:, 48:52], cw[:, 44:48]),
                 waits=waits, name=name)

        def sset(name, waits=()):
            p.op("S", lambda e: e.copy(cw[0:1, 48:52], cw[0:1, 44:48]),
                 waits=waits, name=name)

        def x1ap(m, half=None, width=1024):
            if half is None:
                return T1[:, m * 1024: m * 1024 + width]
            return T1[:, m * 1024 + half * 512: m * 1024 + half * 512 + 512]

        def x0b(c, width=TH):
            return T1[:, c * TH: c * TH + width]

        def xl1(dk, lo, width):
            base = dk * TH
            return T1[:, base + lo: base + lo + width]

        p.op("Y", lambda e: e.dma_start(ident[:], ident_d[:]), dma=True)
        p.op("Y", lambda e: e.dma_start(mask4[:], mask4_d[:]), dma=True)
        p.op("Y", lambda e: e.dma_start(onec[:], onec_d[:]), dma=True)
        p.op("Y", lambda e: e.dma_start(oner[:], oner_d[:]), dma=True)
        p.op("Y", lambda e: e.dma_start(emat[:], emat_d[:]), dma=True)
        p.op("Y", lambda e: e.dma_start(bia[:], bias_d[:]), dma=True, name="consts0")
        consts_w = None
        p.op("Q", lambda e: e.dma_start(wrsb[:], wr_d[:]), dma=True, name="wr_ld")
        consts_w = p.qsnap("Y")

        for g in range(8):
            dst = T1[:, g * 4 * TH: (g * 4 + 4) * TH].rearrange("p (c t) -> p c t", c=4)
            src = xt.rearrange("(c p) t -> p c t", p=128)[:, 4 * g: 4 * g + 4, :]
            p.op("Q", lambda e, d=dst, s=src: e.dma_start(d, s), dma=True, name=f"x0b{g}")

        for c in range(DC):
            xc = x0b(c)
            sq = x2b[:, (c % 2) * TH: (c % 2) * TH + TH]
            wv_ = [f"x0b{c // 4}"]
            if c >= 2:
                wv_.append(f"st{c - 2}")
            p.op("V", lambda e, a=sq, b=xc: e.tensor_tensor(a, b, b, ALU.mult),
                 waits=wv_, name=f"sq{c}")
            st_ = (c == 0)
            sp_ = (c == DC - 1)
            wt = consts_w + [f"x0b{c // 4}", f"sq{c}"]
            p.op("T", lambda e, r=xc, s=st_, q=sp_: e.matmul(pb[0][0:1, 0:512], onec[:], r[:, 0:512], start=s, stop=q), waits=wt)
            p.op("T", lambda e, r=xc, s=st_, q=sp_: e.matmul(pb[1][0:1, 0:512], onec[:], r[:, 512:1024], start=s, stop=q))
            p.op("T", lambda e, r=xc, s=st_, q=sp_: e.matmul(pb[2][0:1, 0:2], onec[:], r[:, 1024:1026], start=s, stop=q))
            p.op("T", lambda e, r=sq, s=st_, q=sp_: e.matmul(pb[3][0:1, 0:512], onec[:], r[:, 0:512], start=s, stop=q))
            p.op("T", lambda e, r=sq, s=st_, q=sp_: e.matmul(pb[4][0:1, 0:512], onec[:], r[:, 512:1024], start=s, stop=q))
            p.op("T", lambda e, r=sq, s=st_, q=sp_: e.matmul(pb[5][0:1, 0:2], onec[:], r[:, 1024:1026], start=s, stop=q),
                 name=f"st{c}")

        def ln_post(tag, wid, statev):
            if wid == TH:
                srcs = [(pb[0], 0, 512), (pb[1], 512, 512), (pb[2], 1024, 2)]
                sqs = [(pb[3], 0, 512), (pb[4], 512, 512), (pb[5], 1024, 2)]
            else:
                srcs = [(pb[2], 0, 512), (pb[3], 512, 512)]
                sqs = [(pb[4], 0, 512), (pb[5], 512, 512)]
            rA, rB, rC, rD = 0, TH, 2 * TH, 3 * TH
            for (bk, lo, w) in srcs:
                p.op("V", lambda e, b=bk, l=lo, w=w: e.tensor_scalar_mul(
                    RR[0:1, rA + l: rA + l + w], b[0:1, 0:w], INV_D), waits=[statev])
            for (bk, lo, w) in sqs:
                p.op("V", lambda e, b=bk, l=lo, w=w: e.tensor_scalar_mul(
                    RR[0:1, rB + l: rB + l + w], b[0:1, 0:w], INV_D))
            p.op("V", lambda e, w=wid: e.tensor_tensor(
                RR[0:1, rC: rC + w], RR[0:1, rA: rA + w], RR[0:1, rA: rA + w], ALU.mult))
            p.op("V", lambda e, w=wid: e.tensor_tensor(
                RR[0:1, rD: rD + w], RR[0:1, rB: rB + w], RR[0:1, rC: rC + w], ALU.subtract),
                 name=f"{tag}_var")
            p.op("S", lambda e, w=wid: e.activation(
                RR[0:1, rB: rB + w], RR[0:1, rD: rD + w], AF.Sqrt, bias=bia[0:1, 112:113]),
                 waits=[f"{tag}_var"], name=f"{tag}_sig")
            p.op("V", lambda e, w=wid: e.reciprocal(
                RR[0:1, rC: rC + w], RR[0:1, rB: rB + w]), waits=[f"{tag}_sig"])
            p.op("V", lambda e, w=wid: e.tensor_tensor(
                RR[0:1, rD: rD + w], RR[0:1, rA: rA + w], RR[0:1, rC: rC + w], ALU.mult),
                 name=f"{tag}_rows")
            p.op("S", lambda e, w=wid: e.copy(rbf[0:1, 0:w], RR[0:1, rC: rC + w]), waits=[f"{tag}_rows"])
            p.op("S", lambda e, w=wid: e.copy(rbf[0:1, TH: TH + w], RR[0:1, rD: rD + w]))
            sset(f"{tag}_rbf")
            if wid == TH:
                regs = [(0, 0, 512), (1, 512, 512), (2, 1024, 2)]
                mregs = [(3, 0, 512), (4, 512, 512), (5, 1024, 2)]
            else:
                regs = [(2, 0, 512), (3, 512, 512)]
                mregs = [(4, 0, 512), (5, 512, 512)]
            wt = [f"{tag}_rbf", f"{tag}_rows"]
            for (bi, lo, w) in regs:
                p.op("T", lambda e, b=bi, l=lo, w=w: e.matmul(
                    pb[b][:, 0:w], oner[:], rbf[0:1, l:l + w], start=True, stop=True), waits=wt)
            for (bi, lo, w) in mregs:
                p.op("T", lambda e, b=bi, l=lo, w=w: e.matmul(
                    pb[b][:, 0:w], oner[:], rbf[0:1, TH + l: TH + l + w], start=True, stop=True))
            p.op("T", lambda e: e.nop(), name=f"{tag}_bcmm")
            for (bi, lo, w) in regs:
                p.op("V", lambda e, b=bi, l=lo, w=w: e.tensor_copy(isbc[:, l:l + w], pb[b][:, 0:w]),
                     waits=[f"{tag}_bcmm"])
            for i, (bi, lo, w) in enumerate(mregs):
                p.op("V", lambda e, b=bi, l=lo, w=w: e.tensor_copy(mobc[:, l:l + w], pb[b][:, 0:w]),
                     name=(f"{tag}_bc" if i == len(mregs) - 1 else None))

        ln_post("ln1", TH, "st31")
        brel(["b0", "b1", "b2", "b3", "b4", "b5"], "ln1_bc")
        if upto <= 0:
            return _finish(nc, p, block, sems, taps, tapd, T1, T2, RR, lgtk, comb2, hwsb, out_d, outb)

        for c in range(DC):
            xc = x0b(c)
            tmp = x2b[:, (c % 2) * TH: (c % 2) * TH + TH]
            dst = T1[:, c * TH: (c + 1) * TH]
            p.op("V", lambda e, t=tmp, a=xc: e.tensor_tensor(t, a, isbc[:], ALU.mult),
                 waits=["ln1_bc"])
            p.op("V", lambda e, d=dst, t=tmp: e.tensor_tensor(d, t, mobc[:], ALU.subtract))
            vset(f"xln{c}")
        if upto <= 1:
            return _finish(nc, p, block, sems, taps, tapd, T1, T2, RR, lgtk, comb2, hwsb, out_d, outb)

        for oc in range(24):
            for g in range(4):
                slot = (oc * 4 + g) % 4
                wt = []
                if oc >= 1:
                    wt.append(f"cvslot{oc - 1}_{slot}")
                dst = WA[:, slot * 3072: slot * 3072 + 3072]
                srcd = cw_d[oc, :, g * 3072: (g + 1) * 3072]
                p.op("Q", lambda e, d=dst, s=srcd: e.dma_start(d, s), dma=True,
                     waits=wt, name=f"cw{oc}_{g}")
            pr = ("b0", "b1") if oc % 2 == 0 else ("b2", "b3")
            pA = pb[0] if oc % 2 == 0 else pb[2]
            pB = pb[1] if oc % 2 == 0 else pb[3]
            wt0 = bfree(pr)
            for k in range(96):
                dk, tap = k // 3, k % 3
                slot = k // 24
                g = slot
                lhs = WA[:, ((oc * 4 + g) % 4) * 3072 + (k % 24) * 128:
                         ((oc * 4 + g) % 4) * 3072 + (k % 24) * 128 + 128]
                wt = list(wt0) if k == 0 else []
                wt.append(f"cw{oc}_{g}")
                wt.append(f"xln{dk}")
                st_, sp_ = (k == 0), (k == 95)
                p.op("T", lambda e, l=lhs, d=dk, tp_=tap, s=st_, q=sp_, P=pA:
                     e.matmul(P[:], l, xl1(d, tp_, 512), start=s, stop=q), waits=wt)
                nm = f"cvslot{oc}_{slot}" if k % 24 == 23 else None
                p.op("T", lambda e, l=lhs, d=dk, tp_=tap, s=st_, q=sp_, P=pB:
                     e.matmul(P[:], l, xl1(d, tp_ + 512, 512), start=s, stop=q), name=nm)
                if k == 95:
                    p.op("T", lambda e: e.nop(), name=f"conv{oc}")
            for half, P in ((0, pA), (1, pB)):
                p.op("S", lambda e, o=oc, h=half, P=P: e.add(
                    T2[:, o * 1024 + h * 512: o * 1024 + h * 512 + 512], P[:],
                    bia[:, o:o + 1]),
                    waits=[f"conv{oc}"])
            sset(f"qkvev{oc}")
            brel(pr, f"qkvev{oc}")
        if upto <= 2:
            return _finish(nc, p, block, sems, taps, tapd, T1, T2, RR, lgtk, comb2, hwsb, out_d, outb)

        for k in range(8):
            p.op("Q", lambda e, k=k: e.dma_start(WA[:, 4096 + k * 1024: 4096 + k * 1024 + 1024], wv_d[k]),
                 dma=True, waits=["conv23"], name=f"wvd{k}")

        p.op("Q", lambda e: e.dma_start(bvbf[:], bv_d[:]), dma=True,
             waits=["ln1_bcmm"], name="bv_ld")
        wt = bfree(("b0", "b1"))
        p.op("T", lambda e: e.matmul(pb[0][:], oner[:], bvbf[0:1, 0:512], start=True, stop=True),
             waits=wt + ["bv_ld", "wr_ld"])
        p.op("T", lambda e: e.matmul(pb[1][:], oner[:], bvbf[0:1, 512:1024], start=True, stop=True))
        p.op("T", lambda e: e.nop(), name="bvmm")
        p.op("V", lambda e: e.tensor_copy(bvbc[:, 0:512], pb[0][:]), waits=["bvmm"])
        p.op("V", lambda e: e.tensor_copy(bvbc[:, 512:1024], pb[1][:]))
        vset("bvbc_ok")
        brel(("b0", "b1"), "bvbc_ok")

        pairs = [("b0", "b1"), ("b2", "b3"), ("b4", "b5")]
        pairt = [(pb[0], pb[1]), (pb[2], pb[3]), (pb[4], pb[5])]
        gi = 0

        def pgroup():
            nonlocal gi
            r = (pairs[gi % 3], pairt[gi % 3])
            gi += 1
            return r

        for which, wdcol, evtag, btag in (("q", 0, "qh", 24), ("k", 2048, "kh", 32)):
            wdmat = wq_d if which == "q" else wk_d
            for m in range(8):
                wtd = ["conv23"] if m < 2 else [f"{evtag}_m{m - 2}"]
                p.op("Q", lambda e, m=m, wd=wdmat, col=wdcol: e.dma_start(
                    WA[:, col + (m % 2) * 1024: col + (m % 2) * 1024 + 1024], wd[m]),
                    dma=True, waits=wtd, name=f"w{which}d{m}")
                (prn, (pA, pB)) = pgroup()
                wt = bfree(prn) + [f"w{which}d{m}", "qkvev7" if which == "q" else "qkvev15"]
                for k in range(8):
                    lhs = WA[:, wdcol + (m % 2) * 1024 + k * 128: wdcol + (m % 2) * 1024 + k * 128 + 128]
                    rhsbase = (0 if which == "q" else 8192) + k * 1024
                    st_, sp_ = (k == 0), (k == 7)
                    p.op("T", lambda e, l=lhs, rb=rhsbase, s=st_, q=sp_, P=pA:
                         e.matmul(P[:], l, T2[:, rb: rb + 512], start=s, stop=q),
                         waits=(wt if k == 0 else []))
                    p.op("T", lambda e, l=lhs, rb=rhsbase, s=st_, q=sp_, P=pB:
                         e.matmul(P[:], l, T2[:, rb + 512: rb + 1024], start=s, stop=q))
                    if k == 7:
                        p.op("T", lambda e: e.nop(), name=f"{evtag}_m{m}")
                dstb = (0 if which == "q" else 8192) + m * 1024
                for half, P in ((0, pA), (1, pB)):
                    p.op("S", lambda e, db=dstb, h=half, P=P, m=m, bt=btag: e.add(
                        T1[:, db + h * 512: db + h * 512 + 512], P[:],
                        bia[:, bt + m: bt + m + 1]),
                        waits=[f"{evtag}_m{m}"])
                sset(f"{evtag}ev{m}")
                brel(prn, f"{evtag}ev{m}")

        for tb in range(8):
            (prn, (pA, pB)) = pgroup()
            wt = bfree(prn) + ["qkvev23", "bvbc_ok"] + [f"wvd{k}" for k in range(8)]
            for k in range(8):
                lhs = T2[:, (16 + k) * 1024 + tb * 128: (16 + k) * 1024 + tb * 128 + 128]
                st_, sp_ = (k == 0), (k == 7)
                p.op("T", lambda e, l=lhs, k=k, s=st_, q=sp_, P=pA:
                     e.matmul(P[:], l, WA[:, 4096 + k * 1024: 4096 + k * 1024 + 512], start=s, stop=q),
                     waits=(wt if k == 0 else []))
                p.op("T", lambda e, l=lhs, k=k, s=st_, q=sp_, P=pB:
                     e.matmul(P[:], l, WA[:, 4096 + k * 1024 + 512: 4096 + k * 1024 + 1024], start=s, stop=q))
                if k == 7:
                    p.op("T", lambda e: e.nop(), name=f"vh_tb{tb}")
            for half, P in ((0, pA), (1, pB)):
                p.op("V", lambda e, t=tb, h=half, P=P: e.tensor_tensor(
                    T1[:, 16384 + t * 1024 + h * 512: 16384 + t * 1024 + h * 512 + 512],
                    P[:], bvbc[:, h * 512: h * 512 + 512], ALU.add),
                    waits=[f"vh_tb{tb}"])
            vset(f"vhev{tb}")
            brel(prn, f"vhev{tb}")
        if upto <= 3:
            return _finish(nc, p, block, sems, taps, tapd, T1, T2, RR, lgtk, comb2, hwsb, out_d, outb)

        sc_free = [bank_free.get("b4"), bank_free.get("b5")]
        pv_free = [bank_free.get("b2"), bank_free.get("b3")]
        tr_free = [None, None]
        gidx = 0
        for tb in range(8):
            for hg in range(4):
                if gidx >= agroups:
                    gidx += 1
                    continue
                par = gidx % 2
                pS = pb[4 + par]
                pO = pb[2 + par]
                pT = pbt[par]
                wt = ["qhev7", "khev7", f"vhev{tb}"]
                if sc_free[par] is not None:
                    wt.append(sc_free[par])
                for pr_ in range(2):
                    hc = hg * 2 + pr_
                    qf = T1[:, hc * 1024 + tb * 128: hc * 1024 + tb * 128 + 128]
                    kf = T1[:, 8192 + hc * 1024 + tb * 128: 8192 + hc * 1024 + tb * 128 + 128]
                    p.op("T", lambda e, a=qf, b=kf, r=2 * pr_, P=pS:
                         e.matmul(P[:, r * 128:(r + 1) * 128], a, b, start=True, stop=True),
                         waits=(wt if pr_ == 0 else []))
                    p.op("T", lambda e, a=qf[0:64, :], b=kf[0:64, :], r=2 * pr_ + 1, P=pS:
                         e.matmul(P[:, r * 128:(r + 1) * 128], a, b, start=True, stop=True))
                p.op("T", lambda e: e.nop(), name=f"smm{gidx}")
                for pr_ in range(2):
                    wtd_ = []
                    if pr_ == 0:
                        wtd_.append(f"smm{gidx}")
                        if gidx >= 2:
                            wtd_.append(f"pexp{gidx - 2}")
                    p.op("V", lambda e, pr_=pr_, par=par, P=pS: e.tensor_copy(
                        SA[:, par * 512 + (2 * pr_) * 128: par * 512 + (2 * pr_) * 128 + 128],
                        P[:, (2 * pr_ + 1) * 128: (2 * pr_ + 2) * 128]),
                        waits=wtd_)
                for pr_ in range(2):
                    p.op("V", lambda e, pr_=pr_, par=par, P=pS: e.tensor_tensor(
                        SA[:, par * 512 + (2 * pr_ + 1) * 128: par * 512 + (2 * pr_ + 1) * 128 + 128],
                        P[:, (2 * pr_) * 128: (2 * pr_ + 1) * 128],
                        SA[:, par * 512 + (2 * pr_) * 128: par * 512 + (2 * pr_) * 128 + 128],
                        ALU.subtract),
                        name=(f"sa{gidx}" if pr_ == 1 else None))
                wtm = [f"sa{gidx}"]
                if gidx >= 2:
                    wtm.append(f"pexp{gidx - 2}")
                p.op("V", lambda e, par=par: e.reduce_max(
                    att4[:, par * 8: par * 8 + 4],
                    SA[:, par * 512:(par + 1) * 512].rearrange("p (a b) -> p a b", a=4),
                    axis=AX.X),
                    waits=wtm)
                vset(f"mx{gidx}")
                p.op("S", lambda e, par=par: e.mul(
                    att4[:, par * 8 + 4: par * 8 + 8], att4[:, par * 8: par * 8 + 4],
                    -0.125), waits=[f"mx{gidx}"])
                for r in range(4):
                    p.op("S", lambda e, par=par, r=r: e.activation(
                        PA[:, par * 512 + r * 128: par * 512 + (r + 1) * 128],
                        SA[:, par * 512 + r * 128: par * 512 + (r + 1) * 128], AF.Exp,
                        bias=att4[:, par * 8 + 4 + r: par * 8 + 5 + r], scale=0.125),
                        waits=(),
                        name=(f"pexp{gidx}" if r == 3 else None))
                sc_free[par] = f"sa{gidx}"
                p.op("V", lambda e, par=par: e.tensor_tensor(
                    PB[:, 0:512], PA[:, par * 512:(par + 1) * 512],
                    mask4.rearrange("p (u k) -> p u k", u=1).broadcast_to([128, 4, 128]),
                    ALU.mult), waits=[f"pexp{gidx}"])
                p.op("V", lambda e, par=par: e.reduce_sum(
                    att4b[:, par * 8: par * 8 + 4],
                    PB[:, 0:512].rearrange("p (a b) -> p a b", a=4),
                    axis=AX.X))
                p.op("V", lambda e: e.tensor_copy(att4b[:, 16:20], att4[:, 0:4]))
                p.op("V", lambda e, par=par: e.reciprocal(
                    att4b[:, par * 8 + 4: par * 8 + 8], att4b[:, par * 8: par * 8 + 4]))
                p.op("V", lambda e: e.tensor_copy(att4b[:, 16:20], att4[:, 4:8]))
                for r in range(4):
                    p.op("V", lambda e, par=par, r=r: e.tensor_scalar_mul(
                        PA[:, par * 512 + r * 128: par * 512 + (r + 1) * 128],
                        PB[:, r * 128: (r + 1) * 128],
                        att4b[:, par * 8 + 4 + r: par * 8 + 5 + r]))
                vset(f"pn{gidx}")
                wt = [f"pn{gidx}"]
                if tr_free[par] is not None:
                    wt.append(tr_free[par])
                for r in range(4):
                    p.op("T", lambda e, par=par, r=r, P=pT: e.transpose(
                        P[:, r * 128:(r + 1) * 128],
                        PA[:, par * 512 + r * 128: par * 512 + (r + 1) * 128], ident[:]),
                        waits=(wt if r == 0 else []))
                p.op("T", lambda e: e.nop(), name=f"ptr{gidx}")
                wtp = [f"ptr{gidx}"]
                if gidx >= 1:
                    wtp.append(f"pv{gidx - 1}")
                p.op("S", lambda e, par=par, P=pT: e.copy(
                    PnT[:, 0:512], P[:, 0:512]),
                    waits=wtp)
                sset(f"pnt{gidx}")
                tr_free[par] = f"pnt{gidx}"
                wt = [f"pnt{gidx}"]
                if pv_free[par] is not None:
                    wt.append(pv_free[par])
                for r in range(4):
                    h = hg * 4 + r
                    vl = T1[:, 16384 + tb * 1024 + h * 64: 16384 + tb * 1024 + h * 64 + 64]
                    colr = (r // 2) * 128
                    rowo = (h % 2) * 64
                    tpz = (0, 64) if rowo else None
                    p.op("T", lambda e, v=vl, par=par, r=r, cr=colr, ro=rowo, tz=tpz, P=pO:
                         e.matmul(P[ro:ro + 64, cr:cr + 128], v,
                                  PnT[:, r * 128: (r + 1) * 128],
                                  start=True, stop=True, tile_position=tz),
                         waits=(wt if r == 0 else []))
                p.op("T", lambda e: e.nop(), name=f"pv{gidx}")
                for cc in range(2):
                    hc = hg * 2 + cc
                    p.op("V", lambda e, hc=hc, tb=tb, cc=cc, P=pO: e.tensor_copy(
                        T2[:, hc * 1024 + tb * 128: hc * 1024 + tb * 128 + 128],
                        P[:, cc * 128:(cc + 1) * 128]),
                        waits=[f"pv{gidx}"])
                vset(f"ofm{gidx}")
                pv_free[par] = f"ofm{gidx}"
                gidx += 1
        for b_, e_ in (("b4", sc_free[0]), ("b5", sc_free[1]), ("b2", pv_free[0]),
                       ("b3", pv_free[1]), ("t0", tr_free[0]), ("t1", tr_free[1])):
            if e_ is not None:
                brel((b_,), e_)
        nc._att_taps = (att4, att4b, SA, PA, PB, PnT)
        if upto <= 4:
            return _finish(nc, p, block, sems, taps, tapd, T1, T2, RR, lgtk, comb2, hwsb, out_d, outb)

        gi = 0
        for m in range(8):
            wtd = ["qh_m7"] if m < 2 else [f"wo_m{m - 2}"]
            p.op("Q", lambda e, m=m: e.dma_start(
                WA[:, (m % 2) * 1024: (m % 2) * 1024 + 1024], wo_d[m]),
                dma=True, waits=wtd, name=f"wod{m}")
            (prn, (pA, pB)) = pgroup()
            wt = bfree(prn) + [f"wod{m}", f"ofm{gidx - 1}"]
            for k in range(8):
                lhs = WA[:, (m % 2) * 1024 + k * 128: (m % 2) * 1024 + k * 128 + 128]
                st_, sp_ = (k == 0), (k == 7)
                p.op("T", lambda e, l=lhs, k=k, s=st_, q=sp_, P=pA:
                     e.matmul(P[:], l, T2[:, k * 1024: k * 1024 + 512], start=s, stop=q),
                     waits=(wt if k == 0 else []))
                p.op("T", lambda e, l=lhs, k=k, s=st_, q=sp_, P=pB:
                     e.matmul(P[:], l, T2[:, k * 1024 + 512: k * 1024 + 1024], start=s, stop=q))
                if k == 7:
                    p.op("T", lambda e: e.nop(), name=f"wo_m{m}")
            for half, P in ((0, pA), (1, pB)):
                p.op("S", lambda e, m=m, h=half, P=P: e.add(
                    T2[:, 8192 + m * 1024 + h * 512: 8192 + m * 1024 + h * 512 + 512],
                    P[:], bia[:, 40 + m: 41 + m]),
                    waits=[f"wo_m{m}"])
            sset(f"woev{m}")
            brel(prn, f"woev{m}")
        if upto <= 5:
            return _finish(nc, p, block, sems, taps, tapd, T1, T2, RR, lgtk, comb2, hwsb, out_d, outb)

        p.op("Q", lambda e: e.dma_start(WA[:, 4096: 8192], a_d[:]),
             dma=True, waits=["vh_tb7"], name="a_ld")

        wt_first = bfree(("b0", "b1")) + ["woev7"]
        wt_stats = bfree(("b2", "b3", "b4", "b5"))
        for m in range(32):
            wtd = ["kh_m7"] if m < 2 else [f"proj_m{m - 2}"]
            p.op("Q", lambda e, m=m: e.dma_start(
                WA[:, 2048 + (m % 2) * 1024: 2048 + (m % 2) * 1024 + 1024], pj_d[m]),
                dma=True, waits=wtd, name=f"pjd{m}")
            wtx = [] if m < 2 else [f"x1c{m - 2}"]
            p.op("Y", lambda e, m=m: e.dma_start(
                xr[:, (m % 2) * 1024: (m % 2) * 1024 + 1024],
                xt[m * 128:(m + 1) * 128, 1: 1025]), dma=True, waits=wtx, name=f"xr{m}")
            wt = list(wt_first) if m == 0 else [f"x1c{m - 1}"]
            wt.append(f"pjd{m}")
            for k in range(8):
                lhs = WA[:, 2048 + (m % 2) * 1024 + k * 128: 2048 + (m % 2) * 1024 + k * 128 + 128]
                st_, sp_ = (k == 0), (k == 7)
                p.op("T", lambda e, l=lhs, k=k, s=st_, q=sp_:
                     e.matmul(pb[0][:], l, T2[:, 8192 + k * 1024: 8192 + k * 1024 + 512], start=s, stop=q),
                     waits=(wt if k == 0 else []))
                p.op("T", lambda e, l=lhs, k=k, s=st_, q=sp_:
                     e.matmul(pb[1][:], l, T2[:, 8192 + k * 1024 + 512: 8192 + k * 1024 + 1024], start=s, stop=q))
                if k == 7:
                    p.op("T", lambda e: e.nop(), name=f"proj_m{m}")
            for half in range(2):
                p.op("V", lambda e, m=m, h=half: e.scalar_tensor_tensor(
                    x1ap(m, h), pb[h][:], bia[:, 48 + m: 49 + m],
                    xr[:, (m % 2) * 1024 + h * 512: (m % 2) * 1024 + h * 512 + 512],
                    ALU.add, ALU.add),
                    waits=[f"proj_m{m}", f"xr{m}"])
            vset(f"x1c{m}")
            p.op("V", lambda e, m=m: e.tensor_tensor(
                x2b[:, (m % 2) * TH: (m % 2) * TH + 1024], x1ap(m), x1ap(m), ALU.mult),
                waits=([f"st2_{m - 2}"] if m >= 2 else []))
            vset(f"x1sq{m}")
            st_, sp_ = (m == 0), (m == 31)
            wt2 = (list(wt_stats) if m == 0 else []) + [f"x1sq{m}"]
            p.op("T", lambda e, m=m, s=st_, q=sp_: e.matmul(
                pb[2][0:1, 0:512], onec[:], x1ap(m, 0), start=s, stop=q), waits=wt2)
            p.op("T", lambda e, m=m, s=st_, q=sp_: e.matmul(
                pb[3][0:1, 0:512], onec[:], x1ap(m, 1), start=s, stop=q))
            p.op("T", lambda e, m=m, s=st_, q=sp_: e.matmul(
                pb[4][0:1, 0:512], onec[:],
                x2b[:, (m % 2) * TH: (m % 2) * TH + 512], start=s, stop=q))
            p.op("T", lambda e, m=m, s=st_, q=sp_: e.matmul(
                pb[5][0:1, 0:512], onec[:],
                x2b[:, (m % 2) * TH + 512: (m % 2) * TH + 1024], start=s, stop=q))
            p.op("T", lambda e: e.nop(), name=f"st2_{m}")

        ln_post("ln2", 1024, "st2_31")
        brel(("b2", "b3", "b4", "b5"), "ln2_bc")
        brel(("b0", "b1"), "x1c31")
        if upto <= 6:
            return _finish(nc, p, block, sems, taps, tapd, T1, T2, RR, lgtk, comb2, hwsb, out_d, outb)

        for c in range(DC):
            tmp = x2b[:, (c % 2) * TH: (c % 2) * TH + 1024]
            wt = ["ln2_bc"]
            if c == 0:
                wt.append("proj_m31")
            p.op("V", lambda e, t=tmp, c=c: e.tensor_tensor(t, x1ap(c), isbc[:, 0:1024], ALU.mult),
                 waits=wt)
            p.op("V", lambda e, t=tmp, c=c: e.tensor_tensor(
                T2[:, c * 1024:(c + 1) * 1024], t, mobc[:, 0:1024], ALU.subtract))
            vset(f"xln2_{c}")
        if upto <= 7:
            return _finish(nc, p, block, sems, taps, tapd, T1, T2, RR, lgtk, comb2, hwsb, out_d, outb)

        def emit_ffn_dmas(o):
            wtb = ["vh_tb7"] if o < 2 else [f"base_o{o - 2}"]
            p.op("Q", lambda e, o=o: e.dma_start(
                WA[:, 8192 + (o % 2) * 128: 8192 + (o % 2) * 128 + 128],
                b_d[:, o * 128: (o + 1) * 128]), dma=True, waits=wtb, name=f"bld{o}")
            if o == 0:
                wt = ["vh_tb7"]
            elif o == 1:
                wt = ["proj_m31"]
            else:
                wt = [f"base_o{o - 2}"]
            base_c = 8448 if o % 2 == 0 else 0
            p.op("Q", lambda e, o=o, bc=base_c: e.dma_start(
                WA[:, bc: bc + 4096], wb_d[o]),
                dma=True, waits=wt, name=f"wbd{o}")

        emit_ffn_dmas(0)
        emit_ffn_dmas(1)
        wrtA = bfree(("b0", "b1"))
        wrtH = bfree(("b2", "b3"))
        for k in range(DC):
            st_, sp_ = (k == 0), (k == DC - 1)
            wt = [f"xln2_{k}", "wr_ld", "a_ld"]
            if k == 0:
                wt += wrtA + wrtH
            p.op("T", lambda e, k=k, s=st_, q=sp_: e.matmul(
                pb[0][0:8, 0:512], wrsb[:, k * 8:(k + 1) * 8],
                T2[:, k * 1024: k * 1024 + 512], start=s, stop=q), waits=wt)
            p.op("T", lambda e, k=k, s=st_, q=sp_: e.matmul(
                pb[1][0:8, 0:512], wrsb[:, k * 8:(k + 1) * 8],
                T2[:, k * 1024 + 512: k * 1024 + 1024], start=s, stop=q))
            if sp_:
                p.op("T", lambda e: e.nop(), name="router_mm")
            p.op("T", lambda e, k=k, s=st_, q=sp_: e.matmul(
                pb[2][:], WA[:, 4096 + k * 128: 4096 + k * 128 + 128],
                T2[:, k * 1024: k * 1024 + 512], start=s, stop=q))
            p.op("T", lambda e, k=k, s=st_, q=sp_: e.matmul(
                pb[3][:], WA[:, 4096 + k * 128: 4096 + k * 128 + 128],
                T2[:, k * 1024 + 512: k * 1024 + 1024], start=s, stop=q))
            if sp_:
                p.op("T", lambda e: e.nop(), name="a_mm")

        def base_k_mms(o, pA, pB, wt0):
            for k in range(DC):
                bc = 8448 if o % 2 == 0 else 0
                lhs = WA[:, bc + k * 128: bc + k * 128 + 128]
                st_ = (k == 0)
                p.op("T", lambda e, l=lhs, k=k, s=st_, P=pA:
                     e.matmul(P[:], l, T2[:, k * 1024: k * 1024 + 512], start=s, stop=False),
                     waits=(wt0 if k == 0 else [f"xln2_{k}"]))
                p.op("T", lambda e, l=lhs, k=k, s=st_, P=pB:
                     e.matmul(P[:], l, T2[:, k * 1024 + 512: k * 1024 + 1024], start=s, stop=False))

        def base_lora(o, pA, pB, extra):
            lhs = b_d
            p.op("T", lambda e, o=o, P=pA: e.matmul(
                P[:], WA[:, 8192 + (o % 2) * 128: 8192 + (o % 2) * 128 + 128],
                hwsb[:, 0:512], start=False, stop=True),
                waits=["hw_ok", f"bld{o}"] + extra)
            p.op("T", lambda e, o=o, P=pB: e.matmul(
                P[:], WA[:, 8192 + (o % 2) * 128: 8192 + (o % 2) * 128 + 128],
                hwsb[:, 512:1024], start=False, stop=True))
            p.op("T", lambda e: e.nop(), name=f"base_o{o}")

        wt0 = bfree(("b4", "b5")) + ["wbd0", "xln2_0"]
        base_k_mms(0, pb[4], pb[5], wt0)

        p.op("S", lambda e: e.copy(lgfm[0:8, 0:512], pb[0][0:8, 0:512]), waits=["router_mm"])
        p.op("S", lambda e: e.copy(lgfm[0:8, 512:1024], pb[1][0:8, 0:512]))
        sset("lgfm_ok")
        wt = ["lgfm_ok"]
        if "t0" in bank_free:
            wt.append(bank_free["t0"])
        for tb in range(8):
            p.op("T", lambda e, tb=tb: e.transpose(
                pbt[0][:, tb * 8:(tb + 1) * 8], lgfm[0:8, tb * 128:(tb + 1) * 128],
                ident[0:8, 0:8]),
                waits=(wt if tb == 0 else []))
        p.op("T", lambda e: e.nop(), name="ltr_ok")
        p.op("V", lambda e: e.tensor_copy(lgtk[:], pbt[0][:, 0:64]), waits=["ltr_ok"])
        p.op("V", lambda e: e.tensor_copy(cw[:, 48:52], cw[:, 44:48]), name="lgtk_ok")

        def bc8(ap8):
            return ap8.rearrange("p (e u) -> p e u", u=1).broadcast_to([128, 8, 8])

        cA = cwk[:, 0:64]; cB = cwk[:, 64:128]; cC = cwk[:, 128:192]
        cD = cwk[:, 192:256]; cE = cwk[:, 256:320]
        p.op("V", lambda e: e.reduce_max(cw[:, 0:8], lgtk.rearrange("p (a b) -> p a b", a=8), axis=AX.X),
             waits=["lgtk_ok"])
        p.op("V", lambda e: e.tensor_copy(cw[:, 48:52], lgtk[:, 0:4]))
        p.op("V", lambda e: e.tensor_tensor(cA, lgtk[:], bc8(cw[:, 0:8]), ALU.is_ge))
        p.op("V", lambda e: e.tensor_copy(cw[:, 48:52], cw[:, 44:48]))
        p.op("V", lambda e: e.scalar_tensor_tensor(cB, cA, -1e9, lgtk[:], ALU.mult, ALU.add))
        p.op("V", lambda e: e.tensor_copy(cw[:, 48:52], cw[:, 44:48]))
        p.op("V", lambda e: e.reduce_max(cw[:, 8:16], cB.rearrange("p (a b) -> p a b", a=8), axis=AX.X))
        p.op("V", lambda e: e.tensor_tensor(cC, lgtk[:], bc8(cw[:, 0:8]), ALU.subtract))
        vset("combv1")
        p.op("S", lambda e: e.activation(cD, cC, AF.Exp), waits=["combv1"], name="combs1")
        p.op("V", lambda e: e.tensor_tensor(cE, lgtk[:], bc8(cw[:, 8:16]), ALU.is_ge))
        p.op("V", lambda e: e.tensor_copy(cw[:, 48:52], cw[:, 44:48]))
        p.op("V", lambda e: e.tensor_tensor(cA, cD, cE, ALU.mult), waits=["combs1"])
        p.op("V", lambda e: e.tensor_tensor(cw[:, 16:24], cw[:, 8:16], cw[:, 0:8], ALU.subtract))
        vset("combv2")
        p.op("S", lambda e: e.activation(cw[:, 24:32], cw[:, 16:24], AF.Exp),
             waits=["combv2"])
        sset("combs2")
        p.op("V", lambda e: e.tensor_scalar(cw[:, 32:40], cw[:, 24:32], 1.0, None, ALU.add),
             waits=["combs2"])
        p.op("V", lambda e: e.tensor_copy(cw[:, 48:52], cw[:, 44:48]))
        p.op("V", lambda e: e.reciprocal(cw[:, 40:48], cw[:, 32:40]))
        p.op("V", lambda e: e.tensor_copy(cw[:, 48:52], cw[:, 44:48]))
        p.op("V", lambda e: e.tensor_tensor(comb2[:], cA, bc8(cw[:, 40:48]), ALU.mult))
        vset("comb2_ok")

        wt = ["comb2_ok"]
        if "t1" in bank_free:
            wt.append(bank_free["t1"])
        for tb in range(8):
            p.op("T", lambda e, tb=tb: e.transpose(
                pbt[1][0:8, tb * 128:(tb + 1) * 128], comb2[:, tb * 8:(tb + 1) * 8], ident[:]),
                waits=(wt if tb == 0 else []))
        p.op("T", lambda e: e.nop(), name="ctr_ok")
        p.op("S", lambda e: e.copy(lgfm[0:8, 0:1024], pbt[1][0:8, 0:1024]),
             waits=["ctr_ok"])
        sset("cfm_ok")
        p.op("T", lambda e: e.matmul(pb[0][:], emat[:], lgfm[0:8, 0:512], start=True, stop=True),
             waits=["cfm_ok", "lgfm_ok"])
        p.op("T", lambda e: e.matmul(pb[1][:], emat[:], lgfm[0:8, 512:1024], start=True, stop=True))
        p.op("T", lambda e: e.nop(), name="expand_ok")
        p.op("V", lambda e: e.tensor_copy(cbc[:, 0:512], pb[0][:]), waits=["expand_ok"])
        p.op("V", lambda e: e.tensor_copy(cbc[:, 512:1024], pb[1][:]), name="cbc_ok")
        p.op("V", lambda e: e.tensor_tensor(hwsb[:, 0:512], pb[2][:], cbc[:, 0:512], ALU.mult),
             waits=["cbc_ok", "a_mm"])
        p.op("V", lambda e: e.tensor_tensor(hwsb[:, 512:1024], pb[3][:], cbc[:, 512:1024], ALU.mult))
        vset("hw_ok")

        base_lora(0, pb[4], pb[5], [])
        for half in range(2):
            p.op("V", lambda e, h=half: e.scalar_tensor_tensor(
                outb[:, h * 512: h * 512 + 512], pb[4 + h][:], bia[:, 80: 81],
                x1ap(0, h), ALU.add, ALU.add),
                waits=["base_o0"])
        vset("outev0")
        p.op("Y", lambda e: e.dma_start(out_d[0:128, :], outb[:, 0:1024]),
             dma=True, waits=["outev0"], name="outdma0")

        for o in range(1, 32):
            if o + 1 < 32:
                emit_ffn_dmas(o + 1)
            ev = o % 2
            pA = pb[4] if ev == 0 else pb[2]
            pB = pb[5] if ev == 0 else pb[3]
            wt0 = [f"wbd{o}"]
            if o == 1:
                wt0.append("hw_ok")
            else:
                wt0.append(f"outev{o - 2}")
            base_k_mms(o, pA, pB, wt0)
            base_lora(o, pA, pB, [])
            for half in range(2):
                P = pA if half == 0 else pB
                wt = [f"base_o{o}"]
                if o >= 2:
                    wt.append(f"outdma{o - 2}")
                p.op("V", lambda e, o=o, h=half, P=P: e.scalar_tensor_tensor(
                    outb[:, (o % 2) * 1024 + h * 512: (o % 2) * 1024 + h * 512 + 512],
                    P[:], bia[:, 80 + o: 81 + o], x1ap(o, h), ALU.add, ALU.add),
                    waits=wt)
            vset(f"outev{o}")
            p.op("Y", lambda e, o=o: e.dma_start(
                out_d[o * 128:(o + 1) * 128, :], outb[:, (o % 2) * 1024: (o % 2) * 1024 + 1024]),
                dma=True, waits=[f"outev{o}"], name=f"outdma{o}")

        p.op("Y", lambda e: e.nop(), waits=p.qsnap("Y"), inc=False)

        return _finish(nc, p, block, sems, taps, tapd, T1, T2, RR, lgtk, comb2, hwsb, out_d, outb)


def _finish(nc, p, block, sems, taps, tapd, T1, T2, RR, lgtk, comb2, hwsb, out_d, outb):
    if taps:
        wt = [("tp", p.cnt["tp"]), ("vp", p.cnt["vp"]), ("sp", p.cnt["sp"])]
        wt += p.qsnap("Q") + p.qsnap("Y")
        if "outdma31" not in p.ev:
            p.op("Y", lambda e: e.dma_start(out_d[0:128, :], outb[:, 0:1024]),
                 dma=True, waits=wt)
        p.op("Y", lambda e: e.dma_start(tapd["t1"][:], T1[:]), dma=True, waits=wt)
        p.op("Y", lambda e: e.dma_start(tapd["t2"][:], T2[:]), dma=True)
        p.op("Y", lambda e: e.dma_start(tapd["rows"][:], RR[0:1, :]), dma=True)
        p.op("Y", lambda e: e.dma_start(tapd["lgtk"][:], lgtk[:]), dma=True)
        p.op("Y", lambda e: e.dma_start(tapd["isbc"][:], nc._isbc_tap[:]), dma=True)
        if hasattr(nc, "_att_taps"):
            att4x, att4bx, SAx, PAx, PBx, PnTx = nc._att_taps
            p.op("Y", lambda e: e.dma_start(tapd["att"][:, 0:16], att4x[:]), dma=True)
            p.op("Y", lambda e: e.dma_start(tapd["att"][:, 16:32], att4bx[:, 0:16]), dma=True)
            p.op("Y", lambda e: e.dma_start(tapd["SA"][:], SAx[:]), dma=True)
            p.op("Y", lambda e: e.dma_start(tapd["PA"][:], PAx[:]), dma=True)
            p.op("Y", lambda e: e.dma_start(tapd["PB"][:], PBx[:]), dma=True)
            p.op("Y", lambda e: e.dma_start(tapd["PnT"][:], PnTx[:]), dma=True)
        p.op("Y", lambda e: e.dma_start(tapd["mobc"][:], nc._mobc_tap[:]), dma=True)
        p.op("Y", lambda e: e.dma_start(tapd["comb2"][:], comb2[:]), dma=True)
        p.op("Y", lambda e: e.dma_start(tapd["hw"][:], hwsb[:]), dma=True, name="tapsdone")
        p.op("Y", lambda e: e.nop(), waits=["tapsdone"], inc=False)
    p.emit(block, sems)
    return nc



def _prep_weights(inputs):
    f = np.float32
    conv_w = np.ascontiguousarray(inputs["L0_attn_conv_w"], dtype=f)
    cw = np.ascontiguousarray(
        conv_w.reshape(3, 32, 128, 24, 128).transpose(3, 2, 1, 0, 4)).reshape(24, 128, 12288)

    def proj_repack(w, mk, kk):
        return np.ascontiguousarray(
            w.reshape(kk, 128, mk, 128).transpose(2, 1, 0, 3)).reshape(mk, 128, kk * 128)

    wq = proj_repack(np.asarray(inputs["L0_attn_wq"], f).reshape(1024, 1024), 8, 8)
    wk = proj_repack(np.asarray(inputs["L0_attn_wk"], f).reshape(1024, 1024), 8, 8)
    wv = np.ascontiguousarray(np.asarray(inputs["L0_attn_wv"], f).reshape(8, 128, 1024))
    wo = proj_repack(np.asarray(inputs["L0_attn_wo"], f).reshape(1024, 1024), 8, 8)
    pj = proj_repack(np.asarray(inputs["L0_attn_proj_w"], f), 32, 8)
    wb = np.ascontiguousarray(
        np.asarray(inputs["L0_ffn_W_base"], f).reshape(32, 128, 32, 128).transpose(0, 3, 2, 1)
    ).reshape(32, 128, 4096)
    A2 = np.asarray(inputs["L0_ffn_A"], f).reshape(128, 4096)
    ar = np.ascontiguousarray(A2.T.reshape(32, 128, 128).transpose(1, 0, 2)).reshape(128, 4096)
    br = np.ascontiguousarray(np.asarray(inputs["L0_ffn_B"], f).transpose(0, 2, 1)).reshape(128, 4096)
    wrr = np.ascontiguousarray(
        np.asarray(inputs["L0_ffn_W_router"], f).T.reshape(32, 128, 8).transpose(1, 0, 2)
    ).reshape(128, 256)
    bvr = np.ascontiguousarray(np.asarray(inputs["L0_attn_bv"], f).reshape(1, 1024))

    biases = np.zeros((128, 113), f)
    biases[:, 112] = 1e-5
    biases[:, 0:24] = np.asarray(inputs["L0_attn_conv_b"], f).reshape(24, 128).T
    biases[:, 24:32] = np.asarray(inputs["L0_attn_bq"], f).reshape(8, 128).T
    biases[:, 32:40] = np.asarray(inputs["L0_attn_bk"], f).reshape(8, 128).T
    biases[:, 40:48] = np.asarray(inputs["L0_attn_bo"], f).reshape(8, 128).T
    biases[:, 48:80] = np.asarray(inputs["L0_attn_proj_b"], f).reshape(32, 128).T
    biases[:, 80:112] = np.asarray(inputs["L0_ffn_b_base"], f).reshape(32, 128).T

    bf = ml_dtypes.bfloat16
    ident = np.eye(128, dtype=bf)
    mask4 = np.tril(np.ones((128, 128), np.float32)).astype(bf)
    onec = np.ones((128, 1), bf)
    oner = np.ones((1, 128), bf)
    emat = np.zeros((8, 128), np.float32)
    for e_ in range(8):
        emat[e_, e_ * 16:(e_ + 1) * 16] = 2.0
    emat = emat.astype(bf)
    return dict(cw=cw, wqr=wq, wkr=wk, wvr=wv, wor=wo, pjr=pj, wbr=wb, ar=ar, br=br,
                wrr=wrr, bvr=bvr, biases=biases, identc=ident, mask4c=mask4,
                onecolc=onec, onerowc=oner, ematc=emat)


def _prep_x(inputs):
    f = np.float32
    x = np.asarray(inputs["L0_attn_inputs"], f)
    xts = []
    for c in range(NC):
        s, q0 = c // 4, (c % 4) * 1024
        xt = np.zeros((DM, TH), f)
        xt[:, 1:1025] = np.ascontiguousarray(x[s, q0:q0 + 1024, :].T)
        if c % 4 > 0:
            xt[:, 0] = x[s, q0 - 1, :]
        if c % 4 < 3:
            xt[:, 1025] = x[s, q0 + 1024, :]
        xts.append(np.ascontiguousarray(xt))
    return xts


def kernel(**inputs):
    w = _prep_weights(inputs)
    xts = _prep_x(inputs)
    nc = build()
    in_maps = [{**w, "xt": xts[c]} for c in range(NC)]
    res = run_bass_kernel_spmd(nc, in_maps, core_ids=list(range(NC)))
    out = np.empty((2, 4096, 4096), np.float32)
    for c in range(NC):
        s, q0 = c // 4, (c % 4) * 1024
        out[s, q0:q0 + 1024, :] = np.asarray(res.results[c]["out"]).T
    return out


# version 03
# speedup vs baseline: 1.0266x; 1.0266x over previous
import numpy as np
import ml_dtypes
import concourse.bass as bass
import concourse.mybir as mybir
from concourse.bass_utils import run_bass_kernel_spmd

F32 = mybir.dt.float32
BF16 = mybir.dt.bfloat16
AF = mybir.ActivationFunctionType
ALU = mybir.AluOpType
AX = mybir.AxisListType

NC = 8
T = 1024
TH = 1026
DM = 4096
DC = 32
W1 = 1024
H = 16
DH = 64
EPS = 1e-5
INV_D = 1.0 / DM

ENGMAP = {"T": "tensor", "V": "vector", "S": "scalar", "Q": "gpsimd", "Y": "sync"}
ENGSEM = {"T": "tp", "V": "vp", "S": "sp", "Q": None, "Y": None}
KD = 4
DMASEMS = [f"q{i}" for i in range(KD)] + [f"y{i}" for i in range(KD)]


class Prog:

    def __init__(self):
        self.ops = []
        self.cnt = {k: 0 for k in ("tp", "vp", "sp", *DMASEMS)}
        self.ev = {}
        self.dq = {"Q": 0, "Y": 0}

    def op(self, eng, fn, waits=(), name=None, inc=True, dma=False):
        wl = []
        for w in waits:
            if w is None:
                continue
            if isinstance(w, str):
                wl.append(self.ev[w])
            else:
                wl.append(w)
        if dma:
            pref = "q" if eng == "Q" else "y"
            sem = f"{pref}{self.dq[eng] % KD}"
            self.dq[eng] += 1
            if self.cnt[sem] > 0:
                wl.append((sem, self.cnt[sem]))
            self.cnt[sem] += 16
            if name is not None:
                self.ev[name] = (sem, self.cnt[sem])
            self.ops.append((eng, fn, wl, (sem, 16)))
            return
        sem = ENGSEM[eng]
        if inc:
            self.cnt[sem] += 1
            if name is not None:
                self.ev[name] = (sem, self.cnt[sem])
        else:
            assert name is None
        self.ops.append((eng, fn, wl, (sem, 1) if inc else None))

    def qsnap(self, eng):
        pref = "q" if eng == "Q" else "y"
        return [(f"{pref}{i}", self.cnt[f"{pref}{i}"]) for i in range(KD)
                if self.cnt[f"{pref}{i}"] > 0]

    def emit(self, block, sems):
        per = {}
        for rec in self.ops:
            per.setdefault(rec[0], []).append(rec)
        for ename, recs in per.items():
            own = ENGSEM[ename]

            def body(eng, recs=recs, own=own):
                maxw = {}
                for (_e, fn, waits, inc) in recs:
                    for (s, v) in waits:
                        if own is not None and s == own:
                            continue
                        if maxw.get(s, 0) < v:
                            eng.wait_ge(sems[s], v)
                            maxw[s] = v
                    inst = fn(eng)
                    if inc is not None:
                        s, a = inc
                        inst.then_inc(sems[s], a)

            getattr(block, ENGMAP[ename])(body)


def build(upto=99, taps=False, agroups=64):
    nc = bass.Bass("TRN2", target_bir_lowering=False)

    def din(name, shape, dt=F32):
        return nc.dram_tensor(name, list(shape), dt, kind="ExternalInput").ap()

    xt = din("xt", (DM, TH))
    cw_d = din("cw", (24, 128, 12288))
    wq_d = din("wqr", (8, 128, 1024))
    wk_d = din("wkr", (8, 128, 1024))
    wv_d = din("wvr", (8, 128, 1024))
    wo_d = din("wor", (8, 128, 1024))
    pj_d = din("pjr", (32, 128, 1024))
    wb_d = din("wbr", (32, 128, 4096))
    a_d = din("ar", (128, 4096))
    b_d = din("br", (128, 4096))
    wr_d = din("wrr", (128, 256))
    bv_d = din("bvr", (1, 1024))
    bias_d = din("biases", (128, 113))
    ident_d = din("identc", (128, 128), BF16)
    mask4_d = din("mask4c", (128, 128), BF16)
    onec_d = din("onecolc", (128, 1), BF16)
    oner_d = din("onerowc", (1, 128), BF16)
    emat_d = din("ematc", (8, 128), BF16)
    out_d = nc.dram_tensor("out", [DM, T], F32, kind="ExternalOutput").ap()
    tapd = {}
    if taps:
        tapd["t1"] = nc.dram_tensor("tap_t1", [128, 32 * 1026], BF16, kind="ExternalOutput").ap()
        tapd["t2"] = nc.dram_tensor("tap_t2", [128, 32 * 1024], BF16, kind="ExternalOutput").ap()
        tapd["rows"] = nc.dram_tensor("tap_rows", [1, 4 * TH], F32, kind="ExternalOutput").ap()
        tapd["lgtk"] = nc.dram_tensor("tap_lgtk", [128, 64], F32, kind="ExternalOutput").ap()
        tapd["comb2"] = nc.dram_tensor("tap_comb2", [128, 64], BF16, kind="ExternalOutput").ap()
        tapd["hw"] = nc.dram_tensor("tap_hw", [128, 1024], BF16, kind="ExternalOutput").ap()
        tapd["isbc"] = nc.dram_tensor("tap_isbc", [128, 1026], BF16, kind="ExternalOutput").ap()
        tapd["att"] = nc.dram_tensor("tap_att", [128, 32], F32, kind="ExternalOutput").ap()
        tapd["SA"] = nc.dram_tensor("tap_SA", [128, 1024], BF16, kind="ExternalOutput").ap()
        tapd["PA"] = nc.dram_tensor("tap_PA", [128, 1024], BF16, kind="ExternalOutput").ap()
        tapd["PB"] = nc.dram_tensor("tap_PB", [128, 512], BF16, kind="ExternalOutput").ap()
        tapd["PnT"] = nc.dram_tensor("tap_PnT", [128, 512], BF16, kind="ExternalOutput").ap()
        tapd["mobc"] = nc.dram_tensor("tap_mobc", [128, 1026], BF16, kind="ExternalOutput").ap()

    from contextlib import ExitStack
    with ExitStack() as st:
        def sb(name, shape, dt):
            return st.enter_context(nc.sbuf_tensor(name, shape, dt))

        T1 = sb("T1", [128, 32 * TH], BF16)
        T2 = sb("T2", [128, 32 * 1024], BF16)
        WA = sb("WA", [128, 12544], BF16)
        x2b = sb("x2b", [128, 2 * TH], BF16)
        RR = sb("RR", [128, 4 * TH], F32)
        rbf = sb("rbf", [128, 2 * TH], BF16)
        isbc = sb("isbc", [128, TH], BF16)
        mobc = sb("mobc", [128, TH], BF16)
        nc._isbc_tap = isbc
        nc._mobc_tap = mobc
        att4 = sb("att4", [128, 16], F32)
        att4b = sb("att4b", [128, 20], F32)
        PA = sb("PA", [128, 2 * 512], BF16)
        SA = sb("SA", [128, 2 * 512], BF16)
        PB = sb("PB", [128, 512], BF16)
        PnT = sb("PnT", [128, 512], BF16)
        xr = sb("xr", [128, 2 * 1024], F32)
        outb = sb("outb", [128, 2 * 1024], F32)
        lgtk = sb("lgtk", [128, 64], F32)
        cw = sb("cwX", [128, 52], F32)
        cwk = sb("cwk", [128, 320], F32)
        comb2 = sb("comb2", [128, 64], BF16)
        wrsb = sb("wrsb", [128, 256], BF16)
        bvbf = rbf[0:1, 0:1024]
        bvbc = isbc[:, 0:1024]
        cbc = mobc[:, 0:1024]
        hwsb = PA[:, 0:1024]
        lgfm = PA[0:8, 0:1024]
        bia = sb("bia", [128, 113], F32)
        ident = sb("ident", [128, 128], BF16)
        mask4 = sb("mask4", [128, 128], BF16)
        onec = sb("onec", [128, 1], BF16)
        oner = sb("oner", [1, 128], BF16)
        emat = sb("emat", [8, 128], BF16)

        def ps(name, shape, dt):
            return st.enter_context(nc.psum_tensor(name, shape, dt))

        pb = [ps(f"pb{i}", [128, 512], F32) for i in range(6)]
        pbt = [ps(f"pbt{i}", [128, 1024], BF16) for i in range(2)]

        sems = {k: st.enter_context(nc.semaphore(k))
                for k in ("tp", "vp", "sp", *DMASEMS)}
        block = st.enter_context(nc.Block())

        p = Prog()
        bank_free = {}

        def bfree(bids):
            return [bank_free[b] for b in bids if b in bank_free]

        def brel(bids, evname):
            for b in bids:
                bank_free[b] = evname

        def vset(name, waits=()):
            p.op("V", lambda e: e.tensor_copy(cw[:, 48:52], cw[:, 44:48]),
                 waits=waits, name=name)

        def sset(name, waits=()):
            p.op("S", lambda e: e.copy(cw[0:1, 48:52], cw[0:1, 44:48]),
                 waits=waits, name=name)

        def x1ap(m, half=None, width=1024):
            if half is None:
                return T1[:, m * 1024: m * 1024 + width]
            return T1[:, m * 1024 + half * 512: m * 1024 + half * 512 + 512]

        def x0b(c, width=TH):
            return T1[:, c * TH: c * TH + width]

        def xl1(dk, lo, width):
            base = dk * TH
            return T1[:, base + lo: base + lo + width]

        p.op("Y", lambda e: e.dma_start(ident[:], ident_d[:]), dma=True)
        p.op("Y", lambda e: e.dma_start(mask4[:], mask4_d[:]), dma=True)
        p.op("Y", lambda e: e.dma_start(onec[:], onec_d[:]), dma=True)
        p.op("Y", lambda e: e.dma_start(oner[:], oner_d[:]), dma=True)
        p.op("Y", lambda e: e.dma_start(emat[:], emat_d[:]), dma=True)
        p.op("Y", lambda e: e.dma_start(bia[:], bias_d[:]), dma=True, name="consts0")
        consts_w = None
        p.op("Q", lambda e: e.dma_start(wrsb[:], wr_d[:]), dma=True, name="wr_ld")
        consts_w = p.qsnap("Y")

        for g in range(8):
            dst = T1[:, g * 4 * TH: (g * 4 + 4) * TH].rearrange("p (c t) -> p c t", c=4)
            src = xt.rearrange("(c p) t -> p c t", p=128)[:, 4 * g: 4 * g + 4, :]
            p.op("Q", lambda e, d=dst, s=src: e.dma_start(d, s), dma=True, name=f"x0b{g}")

        for c in range(DC):
            xc = x0b(c)
            sq = x2b[:, (c % 2) * TH: (c % 2) * TH + TH]
            wv_ = [f"x0b{c // 4}"]
            if c >= 2:
                wv_.append(f"st{c - 2}")
            p.op("V", lambda e, a=sq, b=xc: e.tensor_tensor(a, b, b, ALU.mult),
                 waits=wv_, name=f"sq{c}")
            st_ = (c == 0)
            sp_ = (c == DC - 1)
            wt = consts_w + [f"x0b{c // 4}", f"sq{c}"]
            p.op("T", lambda e, r=xc, s=st_, q=sp_: e.matmul(pb[0][0:1, 0:512], onec[:], r[:, 0:512], start=s, stop=q), waits=wt)
            p.op("T", lambda e, r=xc, s=st_, q=sp_: e.matmul(pb[1][0:1, 0:512], onec[:], r[:, 512:1024], start=s, stop=q))
            p.op("T", lambda e, r=xc, s=st_, q=sp_: e.matmul(pb[2][0:1, 0:2], onec[:], r[:, 1024:1026], start=s, stop=q))
            p.op("T", lambda e, r=sq, s=st_, q=sp_: e.matmul(pb[3][0:1, 0:512], onec[:], r[:, 0:512], start=s, stop=q))
            p.op("T", lambda e, r=sq, s=st_, q=sp_: e.matmul(pb[4][0:1, 0:512], onec[:], r[:, 512:1024], start=s, stop=q))
            p.op("T", lambda e, r=sq, s=st_, q=sp_: e.matmul(pb[5][0:1, 0:2], onec[:], r[:, 1024:1026], start=s, stop=q),
                 name=f"st{c}")

        def ln_post(tag, wid, statev):
            if wid == TH:
                srcs = [(pb[0], 0, 512), (pb[1], 512, 512), (pb[2], 1024, 2)]
                sqs = [(pb[3], 0, 512), (pb[4], 512, 512), (pb[5], 1024, 2)]
            else:
                srcs = [(pb[2], 0, 512), (pb[3], 512, 512)]
                sqs = [(pb[4], 0, 512), (pb[5], 512, 512)]
            rA, rB, rC, rD = 0, TH, 2 * TH, 3 * TH
            for (bk, lo, w) in srcs:
                p.op("V", lambda e, b=bk, l=lo, w=w: e.tensor_scalar_mul(
                    RR[0:1, rA + l: rA + l + w], b[0:1, 0:w], INV_D), waits=[statev])
            for (bk, lo, w) in sqs:
                p.op("V", lambda e, b=bk, l=lo, w=w: e.tensor_scalar_mul(
                    RR[0:1, rB + l: rB + l + w], b[0:1, 0:w], INV_D))
            p.op("V", lambda e, w=wid: e.tensor_tensor(
                RR[0:1, rC: rC + w], RR[0:1, rA: rA + w], RR[0:1, rA: rA + w], ALU.mult))
            p.op("V", lambda e, w=wid: e.tensor_tensor(
                RR[0:1, rD: rD + w], RR[0:1, rB: rB + w], RR[0:1, rC: rC + w], ALU.subtract),
                 name=f"{tag}_var")
            p.op("S", lambda e, w=wid: e.activation(
                RR[0:1, rB: rB + w], RR[0:1, rD: rD + w], AF.Sqrt, bias=bia[0:1, 112:113]),
                 waits=[f"{tag}_var"], name=f"{tag}_sig")
            p.op("V", lambda e, w=wid: e.reciprocal(
                RR[0:1, rC: rC + w], RR[0:1, rB: rB + w]), waits=[f"{tag}_sig"])
            p.op("V", lambda e, w=wid: e.tensor_tensor(
                RR[0:1, rD: rD + w], RR[0:1, rA: rA + w], RR[0:1, rC: rC + w], ALU.mult),
                 name=f"{tag}_rows")
            p.op("S", lambda e, w=wid: e.copy(rbf[0:1, 0:w], RR[0:1, rC: rC + w]), waits=[f"{tag}_rows"])
            p.op("S", lambda e, w=wid: e.copy(rbf[0:1, TH: TH + w], RR[0:1, rD: rD + w]))
            sset(f"{tag}_rbf")
            if wid == TH:
                regs = [(0, 0, 512), (1, 512, 512), (2, 1024, 2)]
                mregs = [(3, 0, 512), (4, 512, 512), (5, 1024, 2)]
            else:
                regs = [(2, 0, 512), (3, 512, 512)]
                mregs = [(4, 0, 512), (5, 512, 512)]
            wt = [f"{tag}_rbf", f"{tag}_rows"]
            for (bi, lo, w) in regs:
                p.op("T", lambda e, b=bi, l=lo, w=w: e.matmul(
                    pb[b][:, 0:w], oner[:], rbf[0:1, l:l + w], start=True, stop=True), waits=wt)
            for (bi, lo, w) in mregs:
                p.op("T", lambda e, b=bi, l=lo, w=w: e.matmul(
                    pb[b][:, 0:w], oner[:], rbf[0:1, TH + l: TH + l + w], start=True, stop=True))
            p.op("T", lambda e: e.nop(), name=f"{tag}_bcmm")
            for (bi, lo, w) in regs:
                p.op("V", lambda e, b=bi, l=lo, w=w: e.tensor_copy(isbc[:, l:l + w], pb[b][:, 0:w]),
                     waits=[f"{tag}_bcmm"])
            for i, (bi, lo, w) in enumerate(mregs):
                p.op("V", lambda e, b=bi, l=lo, w=w: e.tensor_copy(mobc[:, l:l + w], pb[b][:, 0:w]),
                     name=(f"{tag}_bc" if i == len(mregs) - 1 else None))

        ln_post("ln1", TH, "st31")
        brel(["b0", "b1", "b2", "b3", "b4", "b5"], "ln1_bc")
        if upto <= 0:
            return _finish(nc, p, block, sems, taps, tapd, T1, T2, RR, lgtk, comb2, hwsb, out_d, outb)

        for c in range(DC):
            xc = x0b(c)
            tmp = x2b[:, (c % 2) * TH: (c % 2) * TH + TH]
            dst = T1[:, c * TH: (c + 1) * TH]
            p.op("V", lambda e, t=tmp, a=xc: e.tensor_tensor(t, a, isbc[:], ALU.mult),
                 waits=["ln1_bc"])
            p.op("V", lambda e, d=dst, t=tmp: e.tensor_tensor(d, t, mobc[:], ALU.subtract))
            vset(f"xln{c}")
        if upto <= 1:
            return _finish(nc, p, block, sems, taps, tapd, T1, T2, RR, lgtk, comb2, hwsb, out_d, outb)

        for oc in range(24):
            for g in range(4):
                slot = (oc * 4 + g) % 4
                wt = []
                if oc >= 1:
                    wt.append(f"cvslot{oc - 1}_{slot}")
                dst = WA[:, slot * 3072: slot * 3072 + 3072]
                srcd = cw_d[oc, :, g * 3072: (g + 1) * 3072]
                p.op("Q", lambda e, d=dst, s=srcd: e.dma_start(d, s), dma=True,
                     waits=wt, name=f"cw{oc}_{g}")
            pr = ("b0", "b1") if oc % 2 == 0 else ("b2", "b3")
            pA = pb[0] if oc % 2 == 0 else pb[2]
            pB = pb[1] if oc % 2 == 0 else pb[3]
            wt0 = bfree(pr)
            for k in range(96):
                dk, tap = k // 3, k % 3
                slot = k // 24
                g = slot
                lhs = WA[:, ((oc * 4 + g) % 4) * 3072 + (k % 24) * 128:
                         ((oc * 4 + g) % 4) * 3072 + (k % 24) * 128 + 128]
                wt = list(wt0) if k == 0 else []
                wt.append(f"cw{oc}_{g}")
                wt.append(f"xln{dk}")
                st_, sp_ = (k == 0), (k == 95)
                p.op("T", lambda e, l=lhs, d=dk, tp_=tap, s=st_, q=sp_, P=pA:
                     e.matmul(P[:], l, xl1(d, tp_, 512), start=s, stop=q), waits=wt)
                nm = f"cvslot{oc}_{slot}" if k % 24 == 23 else None
                p.op("T", lambda e, l=lhs, d=dk, tp_=tap, s=st_, q=sp_, P=pB:
                     e.matmul(P[:], l, xl1(d, tp_ + 512, 512), start=s, stop=q), name=nm)
                if k == 95:
                    p.op("T", lambda e: e.nop(), name=f"conv{oc}")
            for half, P in ((0, pA), (1, pB)):
                p.op("S", lambda e, o=oc, h=half, P=P: e.add(
                    T2[:, o * 1024 + h * 512: o * 1024 + h * 512 + 512], P[:],
                    bia[:, o:o + 1]),
                    waits=[f"conv{oc}"])
            sset(f"qkvev{oc}")
            brel(pr, f"qkvev{oc}")
        if upto <= 2:
            return _finish(nc, p, block, sems, taps, tapd, T1, T2, RR, lgtk, comb2, hwsb, out_d, outb)

        p.op("Q", lambda e: e.dma_start(bvbf[:], bv_d[:]), dma=True,
             waits=["ln1_bcmm"], name="bv_ld")
        for k in range(8):
            p.op("Q", lambda e, k=k: e.dma_start(WA[:, 4096 + k * 1024: 4096 + k * 1024 + 1024], wv_d[k]),
                 dma=True, waits=["conv23"], name=f"wvd{k}")

        wt = bfree(("b0", "b1"))
        p.op("T", lambda e: e.matmul(pb[0][:], oner[:], bvbf[0:1, 0:512], start=True, stop=True),
             waits=wt + ["bv_ld", "wr_ld"])
        p.op("T", lambda e: e.matmul(pb[1][:], oner[:], bvbf[0:1, 512:1024], start=True, stop=True))
        p.op("T", lambda e: e.nop(), name="bvmm")
        p.op("V", lambda e: e.tensor_copy(bvbc[:, 0:512], pb[0][:]), waits=["bvmm"])
        p.op("V", lambda e: e.tensor_copy(bvbc[:, 512:1024], pb[1][:]))
        vset("bvbc_ok")
        brel(("b0", "b1"), "bvbc_ok")

        pairs = [("b0", "b1"), ("b2", "b3"), ("b4", "b5")]
        pairt = [(pb[0], pb[1]), (pb[2], pb[3]), (pb[4], pb[5])]
        gi = 0

        def pgroup():
            nonlocal gi
            r = (pairs[gi % 3], pairt[gi % 3])
            gi += 1
            return r

        for which, wdcol, evtag, btag in (("q", 0, "qh", 24), ("k", 2048, "kh", 32)):
            wdmat = wq_d if which == "q" else wk_d
            for m in range(8):
                wtd = ["conv23"] if m < 2 else [f"{evtag}_m{m - 2}"]
                p.op("Q", lambda e, m=m, wd=wdmat, col=wdcol: e.dma_start(
                    WA[:, col + (m % 2) * 1024: col + (m % 2) * 1024 + 1024], wd[m]),
                    dma=True, waits=wtd, name=f"w{which}d{m}")
                (prn, (pA, pB)) = pgroup()
                wt = bfree(prn) + [f"w{which}d{m}", "qkvev7" if which == "q" else "qkvev15"]
                for k in range(8):
                    lhs = WA[:, wdcol + (m % 2) * 1024 + k * 128: wdcol + (m % 2) * 1024 + k * 128 + 128]
                    rhsbase = (0 if which == "q" else 8192) + k * 1024
                    st_, sp_ = (k == 0), (k == 7)
                    p.op("T", lambda e, l=lhs, rb=rhsbase, s=st_, q=sp_, P=pA:
                         e.matmul(P[:], l, T2[:, rb: rb + 512], start=s, stop=q),
                         waits=(wt if k == 0 else []))
                    p.op("T", lambda e, l=lhs, rb=rhsbase, s=st_, q=sp_, P=pB:
                         e.matmul(P[:], l, T2[:, rb + 512: rb + 1024], start=s, stop=q))
                    if k == 7:
                        p.op("T", lambda e: e.nop(), name=f"{evtag}_m{m}")
                dstb = (0 if which == "q" else 8192) + m * 1024
                for half, P in ((0, pA), (1, pB)):
                    p.op("S", lambda e, db=dstb, h=half, P=P, m=m, bt=btag: e.add(
                        T1[:, db + h * 512: db + h * 512 + 512], P[:],
                        bia[:, bt + m: bt + m + 1]),
                        waits=[f"{evtag}_m{m}"])
                sset(f"{evtag}ev{m}")
                brel(prn, f"{evtag}ev{m}")

        for tb in range(8):
            (prn, (pA, pB)) = pgroup()
            wt = bfree(prn) + ["qkvev23", "bvbc_ok"] + [f"wvd{k}" for k in range(8)]
            for k in range(8):
                lhs = T2[:, (16 + k) * 1024 + tb * 128: (16 + k) * 1024 + tb * 128 + 128]
                st_, sp_ = (k == 0), (k == 7)
                p.op("T", lambda e, l=lhs, k=k, s=st_, q=sp_, P=pA:
                     e.matmul(P[:], l, WA[:, 4096 + k * 1024: 4096 + k * 1024 + 512], start=s, stop=q),
                     waits=(wt if k == 0 else []))
                p.op("T", lambda e, l=lhs, k=k, s=st_, q=sp_, P=pB:
                     e.matmul(P[:], l, WA[:, 4096 + k * 1024 + 512: 4096 + k * 1024 + 1024], start=s, stop=q))
                if k == 7:
                    p.op("T", lambda e: e.nop(), name=f"vh_tb{tb}")
            for half, P in ((0, pA), (1, pB)):
                p.op("V", lambda e, t=tb, h=half, P=P: e.tensor_tensor(
                    T1[:, 16384 + t * 1024 + h * 512: 16384 + t * 1024 + h * 512 + 512],
                    P[:], bvbc[:, h * 512: h * 512 + 512], ALU.add),
                    waits=[f"vh_tb{tb}"])
            vset(f"vhev{tb}")
            brel(prn, f"vhev{tb}")
        if upto <= 3:
            return _finish(nc, p, block, sems, taps, tapd, T1, T2, RR, lgtk, comb2, hwsb, out_d, outb)

        NG = min(32, agroups)
        sc_free = [bank_free.get("b4"), bank_free.get("b5")]
        pv_free = [bank_free.get("b2"), bank_free.get("b3")]
        tr_free = [None, None]

        def att_A(g):
            tb, hg, par = g // 4, g % 4, g % 2
            pS = pb[4 + par]
            wt = ["qhev7", "khev7", f"vhev{tb}"]
            if sc_free[par] is not None:
                wt.append(sc_free[par])
            for pr_ in range(2):
                hc = hg * 2 + pr_
                qf = T1[:, hc * 1024 + tb * 128: hc * 1024 + tb * 128 + 128]
                kf = T1[:, 8192 + hc * 1024 + tb * 128: 8192 + hc * 1024 + tb * 128 + 128]
                p.op("T", lambda e, a=qf, b=kf, r=2 * pr_, P=pS:
                     e.matmul(P[:, r * 128:(r + 1) * 128], a, b, start=True, stop=True),
                     waits=(wt if pr_ == 0 else []))
                p.op("T", lambda e, a=qf[0:64, :], b=kf[0:64, :], r=2 * pr_ + 1, P=pS:
                     e.matmul(P[:, r * 128:(r + 1) * 128], a, b, start=True, stop=True))
            p.op("T", lambda e: e.nop(), name=f"smm{g}")
            wts = [f"smm{g}"]
            if g >= 2:
                wts.append(f"pexp{g - 2}")
            for pr_ in range(2):
                p.op("S", lambda e, pr_=pr_, par=par, P=pS: e.copy(
                    SA[:, par * 512 + (2 * pr_) * 128: par * 512 + (2 * pr_) * 128 + 128],
                    P[:, (2 * pr_ + 1) * 128: (2 * pr_ + 2) * 128]),
                    waits=(wts if pr_ == 0 else []))
            sset(f"saev{g}")
            for pr_ in range(2):
                p.op("V", lambda e, pr_=pr_, par=par, P=pS: e.tensor_tensor(
                    SA[:, par * 512 + (2 * pr_ + 1) * 128: par * 512 + (2 * pr_ + 1) * 128 + 128],
                    P[:, (2 * pr_) * 128: (2 * pr_ + 1) * 128],
                    SA[:, par * 512 + (2 * pr_) * 128: par * 512 + (2 * pr_) * 128 + 128],
                    ALU.subtract),
                    waits=([f"saev{g}"] if pr_ == 0 else []),
                    name=(f"sa{g}" if pr_ == 1 else None))
            sc_free[par] = f"sa{g}"
            wtm = []
            if g >= 2:
                wtm.append(f"pexp{g - 2}")
            p.op("V", lambda e, par=par: e.reduce_max(
                att4[:, par * 8: par * 8 + 4],
                SA[:, par * 512:(par + 1) * 512].rearrange("p (a b) -> p a b", a=4),
                axis=AX.X), waits=wtm)
            vset(f"mx{g}")
            p.op("S", lambda e, par=par: e.mul(
                att4[:, par * 8 + 4: par * 8 + 8], att4[:, par * 8: par * 8 + 4], -0.125),
                waits=[f"mx{g}"])
            wte = [f"ptr{g - 2}"] if g >= 2 else []
            for r in range(4):
                p.op("S", lambda e, par=par, r=r: e.activation(
                    PA[:, par * 512 + r * 128: par * 512 + (r + 1) * 128],
                    SA[:, par * 512 + r * 128: par * 512 + (r + 1) * 128], AF.Exp,
                    bias=att4[:, par * 8 + 4 + r: par * 8 + 5 + r], scale=0.125),
                    waits=(wte if r == 0 else []))
            sset(f"pexp{g}")

        def att_C(g):
            par = g % 2
            p.op("V", lambda e, par=par: e.tensor_tensor(
                PB[:, 0:512], PA[:, par * 512:(par + 1) * 512],
                mask4.rearrange("p (u k) -> p u k", u=1).broadcast_to([128, 4, 128]),
                ALU.mult), waits=[f"pexp{g}"])
            p.op("V", lambda e, par=par: e.reduce_sum(
                att4b[:, par * 8: par * 8 + 4],
                PB[:, 0:512].rearrange("p (a b) -> p a b", a=4), axis=AX.X))
            p.op("V", lambda e: e.tensor_copy(att4b[:, 16:20], att4[:, 0:4]))
            p.op("V", lambda e, par=par: e.reciprocal(
                att4b[:, par * 8 + 4: par * 8 + 8], att4b[:, par * 8: par * 8 + 4]))
            p.op("V", lambda e: e.tensor_copy(att4b[:, 16:20], att4[:, 4:8]))
            sset(f"recq{g}")
            for r in range(4):
                p.op("V", lambda e, par=par, r=r: e.tensor_scalar_mul(
                    PA[:, par * 512 + r * 128: par * 512 + (r + 1) * 128],
                    PB[:, r * 128: (r + 1) * 128],
                    att4b[:, par * 8 + 4 + r: par * 8 + 5 + r]))
            vset(f"pnv{g}")

        def att_D(g):
            tb, hg, par = g // 4, g % 4, g % 2
            pO = pb[2 + par]
            pT = pbt[par]
            wt = [f"pnv{g}"]
            if tr_free[par] is not None:
                wt.append(tr_free[par])
            for r in range(4):
                p.op("T", lambda e, par=par, r=r, P=pT: e.transpose(
                    P[:, r * 128:(r + 1) * 128],
                    PA[:, par * 512 + r * 128: par * 512 + (r + 1) * 128], ident[:]),
                    waits=(wt if r == 0 else []))
            p.op("T", lambda e: e.nop(), name=f"ptr{g}")
            wtp = [f"ptr{g}"]
            if g >= 1:
                wtp.append(f"pv{g - 1}")
            p.op("S", lambda e, P=pT: e.copy(PnT[:, 0:512], P[:, 0:512]), waits=wtp)
            sset(f"pnt{g}")
            tr_free[par] = f"pnt{g}"
            wtv = [f"pnt{g}"]
            if pv_free[par] is not None:
                wtv.append(pv_free[par])
            for r in range(4):
                h = hg * 4 + r
                vl = T1[:, 16384 + tb * 1024 + h * 64: 16384 + tb * 1024 + h * 64 + 64]
                colr = (r // 2) * 128
                rowo = (h % 2) * 64
                tz = (0, 64) if rowo else None
                p.op("T", lambda e, v=vl, r=r, cr=colr, ro=rowo, tz=tz, P=pO:
                     e.matmul(P[ro:ro + 64, cr:cr + 128], v,
                              PnT[:, r * 128: (r + 1) * 128],
                              start=True, stop=True, tile_position=tz),
                     waits=(wtv if r == 0 else []))
            p.op("T", lambda e: e.nop(), name=f"pv{g}")
            for cc in range(2):
                hc = hg * 2 + cc
                p.op("S", lambda e, hc=hc, tb=tb, cc=cc, P=pO: e.copy(
                    T2[:, hc * 1024 + tb * 128: hc * 1024 + tb * 128 + 128],
                    P[:, cc * 128:(cc + 1) * 128]),
                    waits=([f"pv{g}"] if cc == 0 else []))
            sset(f"ofm{g}")
            pv_free[par] = f"ofm{g}"

        for it in range(NG + 1):
            if it < NG:
                att_A(it)
            if 0 <= it - 1 < NG:
                att_C(it - 1)
                att_D(it - 1)
        gidx = NG
        for b_, e_ in (("b4", sc_free[0]), ("b5", sc_free[1]), ("b2", pv_free[0]),
                       ("b3", pv_free[1]), ("t0", tr_free[0]), ("t1", tr_free[1])):
            if e_ is not None:
                brel((b_,), e_)
        nc._att_taps = (att4, att4b, SA, PA, PB, PnT)
        if upto <= 4:
            return _finish(nc, p, block, sems, taps, tapd, T1, T2, RR, lgtk, comb2, hwsb, out_d, outb)

        gi = 0
        for m in range(8):
            wtd = ["qh_m7"] if m < 2 else [f"wo_m{m - 2}"]
            p.op("Q", lambda e, m=m: e.dma_start(
                WA[:, (m % 2) * 1024: (m % 2) * 1024 + 1024], wo_d[m]),
                dma=True, waits=wtd, name=f"wod{m}")
            (prn, (pA, pB)) = pgroup()
            wt = bfree(prn) + [f"wod{m}", f"ofm{gidx - 1}"]
            for k in range(8):
                lhs = WA[:, (m % 2) * 1024 + k * 128: (m % 2) * 1024 + k * 128 + 128]
                st_, sp_ = (k == 0), (k == 7)
                p.op("T", lambda e, l=lhs, k=k, s=st_, q=sp_, P=pA:
                     e.matmul(P[:], l, T2[:, k * 1024: k * 1024 + 512], start=s, stop=q),
                     waits=(wt if k == 0 else []))
                p.op("T", lambda e, l=lhs, k=k, s=st_, q=sp_, P=pB:
                     e.matmul(P[:], l, T2[:, k * 1024 + 512: k * 1024 + 1024], start=s, stop=q))
                if k == 7:
                    p.op("T", lambda e: e.nop(), name=f"wo_m{m}")
            for half, P in ((0, pA), (1, pB)):
                p.op("S", lambda e, m=m, h=half, P=P: e.add(
                    T2[:, 8192 + m * 1024 + h * 512: 8192 + m * 1024 + h * 512 + 512],
                    P[:], bia[:, 40 + m: 41 + m]),
                    waits=[f"wo_m{m}"])
            sset(f"woev{m}")
            brel(prn, f"woev{m}")
        if upto <= 5:
            return _finish(nc, p, block, sems, taps, tapd, T1, T2, RR, lgtk, comb2, hwsb, out_d, outb)

        p.op("Q", lambda e: e.dma_start(WA[:, 4096: 8192], a_d[:]),
             dma=True, waits=["vh_tb7"], name="a_ld")

        wt_first = bfree(("b0", "b1")) + ["woev7"]
        wt_stats = bfree(("b2", "b3", "b4", "b5"))
        for m in range(32):
            wtd = ["kh_m7"] if m < 2 else [f"proj_m{m - 2}"]
            p.op("Q", lambda e, m=m: e.dma_start(
                WA[:, 2048 + (m % 2) * 1024: 2048 + (m % 2) * 1024 + 1024], pj_d[m]),
                dma=True, waits=wtd, name=f"pjd{m}")
            wtx = [] if m < 2 else [f"x1c{m - 2}"]
            p.op("Y", lambda e, m=m: e.dma_start(
                xr[:, (m % 2) * 1024: (m % 2) * 1024 + 1024],
                xt[m * 128:(m + 1) * 128, 1: 1025]), dma=True, waits=wtx, name=f"xr{m}")
            wt = list(wt_first) if m == 0 else [f"x1ch0_{m - 1}"]
            wt.append(f"pjd{m}")
            for k in range(8):
                lhs = WA[:, 2048 + (m % 2) * 1024 + k * 128: 2048 + (m % 2) * 1024 + k * 128 + 128]
                st_, sp_ = (k == 0), (k == 7)
                p.op("T", lambda e, l=lhs, k=k, s=st_, q=sp_:
                     e.matmul(pb[0][:], l, T2[:, 8192 + k * 1024: 8192 + k * 1024 + 512], start=s, stop=q),
                     waits=(wt if k == 0 else []))
                p.op("T", lambda e, l=lhs, k=k, s=st_, q=sp_:
                     e.matmul(pb[1][:], l, T2[:, 8192 + k * 1024 + 512: 8192 + k * 1024 + 1024], start=s, stop=q),
                     waits=([f"x1c{m - 1}"] if (k == 0 and m >= 1) else []))
                if k == 7:
                    p.op("T", lambda e: e.nop(), name=f"proj_m{m}")
            for half in range(2):
                p.op("V", lambda e, m=m, h=half: e.scalar_tensor_tensor(
                    x1ap(m, h), pb[h][:], bia[:, 48 + m: 49 + m],
                    xr[:, (m % 2) * 1024 + h * 512: (m % 2) * 1024 + h * 512 + 512],
                    ALU.add, ALU.add),
                    waits=[f"proj_m{m}", f"xr{m}"],
                    name=(f"x1ch0_{m}" if half == 0 else None))
            vset(f"x1c{m}")
            p.op("V", lambda e, m=m: e.tensor_tensor(
                x2b[:, (m % 2) * TH: (m % 2) * TH + 1024], x1ap(m), x1ap(m), ALU.mult),
                waits=([f"st2_{m - 2}"] if m >= 2 else []))
            vset(f"x1sq{m}")
            st_, sp_ = (m == 0), (m == 31)
            wt2 = (list(wt_stats) if m == 0 else []) + [f"x1sq{m}"]
            p.op("T", lambda e, m=m, s=st_, q=sp_: e.matmul(
                pb[2][0:1, 0:512], onec[:], x1ap(m, 0), start=s, stop=q), waits=wt2)
            p.op("T", lambda e, m=m, s=st_, q=sp_: e.matmul(
                pb[3][0:1, 0:512], onec[:], x1ap(m, 1), start=s, stop=q))
            p.op("T", lambda e, m=m, s=st_, q=sp_: e.matmul(
                pb[4][0:1, 0:512], onec[:],
                x2b[:, (m % 2) * TH: (m % 2) * TH + 512], start=s, stop=q))
            p.op("T", lambda e, m=m, s=st_, q=sp_: e.matmul(
                pb[5][0:1, 0:512], onec[:],
                x2b[:, (m % 2) * TH + 512: (m % 2) * TH + 1024], start=s, stop=q))
            p.op("T", lambda e: e.nop(), name=f"st2_{m}")

        ln_post("ln2", 1024, "st2_31")
        brel(("b2", "b3", "b4", "b5"), "ln2_bc")
        brel(("b0", "b1"), "x1c31")
        if upto <= 6:
            return _finish(nc, p, block, sems, taps, tapd, T1, T2, RR, lgtk, comb2, hwsb, out_d, outb)

        for c in range(DC):
            tmp = x2b[:, (c % 2) * TH: (c % 2) * TH + 1024]
            wt = ["ln2_bc"]
            if c == 0:
                wt.append("proj_m31")
            p.op("V", lambda e, t=tmp, c=c: e.tensor_tensor(t, x1ap(c), isbc[:, 0:1024], ALU.mult),
                 waits=wt)
            p.op("V", lambda e, t=tmp, c=c: e.tensor_tensor(
                T2[:, c * 1024:(c + 1) * 1024], t, mobc[:, 0:1024], ALU.subtract))
            vset(f"xln2_{c}")
        if upto <= 7:
            return _finish(nc, p, block, sems, taps, tapd, T1, T2, RR, lgtk, comb2, hwsb, out_d, outb)

        def emit_ffn_dmas(o):
            wtb = ["vh_tb7"] if o < 2 else [f"base_o{o - 2}"]
            p.op("Q", lambda e, o=o: e.dma_start(
                WA[:, 8192 + (o % 2) * 128: 8192 + (o % 2) * 128 + 128],
                b_d[:, o * 128: (o + 1) * 128]), dma=True, waits=wtb, name=f"bld{o}")
            if o == 0:
                wt = ["vh_tb7"]
            elif o == 1:
                wt = ["proj_m31"]
            else:
                wt = [f"base_o{o - 2}"]
            base_c = 8448 if o % 2 == 0 else 0
            p.op("Q", lambda e, o=o, bc=base_c: e.dma_start(
                WA[:, bc: bc + 4096], wb_d[o]),
                dma=True, waits=wt, name=f"wbd{o}")

        emit_ffn_dmas(0)
        emit_ffn_dmas(1)
        wrtA = bfree(("b0", "b1"))
        wrtH = bfree(("b2", "b3"))
        for k in range(DC):
            st_, sp_ = (k == 0), (k == DC - 1)
            wt = [f"xln2_{k}", "wr_ld", "a_ld"]
            if k == 0:
                wt += wrtA + wrtH
            p.op("T", lambda e, k=k, s=st_, q=sp_: e.matmul(
                pb[0][0:8, 0:512], wrsb[:, k * 8:(k + 1) * 8],
                T2[:, k * 1024: k * 1024 + 512], start=s, stop=q), waits=wt)
            p.op("T", lambda e, k=k, s=st_, q=sp_: e.matmul(
                pb[1][0:8, 0:512], wrsb[:, k * 8:(k + 1) * 8],
                T2[:, k * 1024 + 512: k * 1024 + 1024], start=s, stop=q))
            if sp_:
                p.op("T", lambda e: e.nop(), name="router_mm")
            p.op("T", lambda e, k=k, s=st_, q=sp_: e.matmul(
                pb[2][:], WA[:, 4096 + k * 128: 4096 + k * 128 + 128],
                T2[:, k * 1024: k * 1024 + 512], start=s, stop=q))
            p.op("T", lambda e, k=k, s=st_, q=sp_: e.matmul(
                pb[3][:], WA[:, 4096 + k * 128: 4096 + k * 128 + 128],
                T2[:, k * 1024 + 512: k * 1024 + 1024], start=s, stop=q))
            if sp_:
                p.op("T", lambda e: e.nop(), name="a_mm")

        def base_k_mms(o, pA, pB, wt0):
            for k in range(DC):
                bc = 8448 if o % 2 == 0 else 0
                lhs = WA[:, bc + k * 128: bc + k * 128 + 128]
                st_ = (k == 0)
                p.op("T", lambda e, l=lhs, k=k, s=st_, P=pA:
                     e.matmul(P[:], l, T2[:, k * 1024: k * 1024 + 512], start=s, stop=False),
                     waits=(wt0 if k == 0 else [f"xln2_{k}"]))
                p.op("T", lambda e, l=lhs, k=k, s=st_, P=pB:
                     e.matmul(P[:], l, T2[:, k * 1024 + 512: k * 1024 + 1024], start=s, stop=False))

        def base_lora(o, pA, pB, extra):
            lhs = b_d
            p.op("T", lambda e, o=o, P=pA: e.matmul(
                P[:], WA[:, 8192 + (o % 2) * 128: 8192 + (o % 2) * 128 + 128],
                hwsb[:, 0:512], start=False, stop=True),
                waits=["hw_ok", f"bld{o}"] + extra)
            p.op("T", lambda e, o=o, P=pB: e.matmul(
                P[:], WA[:, 8192 + (o % 2) * 128: 8192 + (o % 2) * 128 + 128],
                hwsb[:, 512:1024], start=False, stop=True))
            p.op("T", lambda e: e.nop(), name=f"base_o{o}")

        wt0 = bfree(("b4", "b5")) + ["wbd0", "xln2_0"]
        base_k_mms(0, pb[4], pb[5], wt0)

        p.op("S", lambda e: e.copy(lgfm[0:8, 0:512], pb[0][0:8, 0:512]), waits=["router_mm"])
        p.op("S", lambda e: e.copy(lgfm[0:8, 512:1024], pb[1][0:8, 0:512]))
        sset("lgfm_ok")
        wt = ["lgfm_ok"]
        if "t0" in bank_free:
            wt.append(bank_free["t0"])
        for tb in range(8):
            p.op("T", lambda e, tb=tb: e.transpose(
                pbt[0][:, tb * 8:(tb + 1) * 8], lgfm[0:8, tb * 128:(tb + 1) * 128],
                ident[0:8, 0:8]),
                waits=(wt if tb == 0 else []))
        p.op("T", lambda e: e.nop(), name="ltr_ok")
        p.op("V", lambda e: e.tensor_copy(lgtk[:], pbt[0][:, 0:64]), waits=["ltr_ok"])
        p.op("V", lambda e: e.tensor_copy(cw[:, 48:52], cw[:, 44:48]), name="lgtk_ok")

        def bc8(ap8):
            return ap8.rearrange("p (e u) -> p e u", u=1).broadcast_to([128, 8, 8])

        cA = cwk[:, 0:64]; cB = cwk[:, 64:128]; cC = cwk[:, 128:192]
        cD = cwk[:, 192:256]; cE = cwk[:, 256:320]
        p.op("V", lambda e: e.reduce_max(cw[:, 0:8], lgtk.rearrange("p (a b) -> p a b", a=8), axis=AX.X),
             waits=["lgtk_ok"])
        p.op("V", lambda e: e.tensor_copy(cw[:, 48:52], lgtk[:, 0:4]))
        p.op("V", lambda e: e.tensor_tensor(cA, lgtk[:], bc8(cw[:, 0:8]), ALU.is_ge))
        p.op("V", lambda e: e.tensor_copy(cw[:, 48:52], cw[:, 44:48]))
        p.op("V", lambda e: e.scalar_tensor_tensor(cB, cA, -1e9, lgtk[:], ALU.mult, ALU.add))
        p.op("V", lambda e: e.tensor_copy(cw[:, 48:52], cw[:, 44:48]))
        p.op("V", lambda e: e.reduce_max(cw[:, 8:16], cB.rearrange("p (a b) -> p a b", a=8), axis=AX.X))
        p.op("V", lambda e: e.tensor_tensor(cC, lgtk[:], bc8(cw[:, 0:8]), ALU.subtract))
        vset("combv1")
        p.op("S", lambda e: e.activation(cD, cC, AF.Exp), waits=["combv1"], name="combs1")
        p.op("V", lambda e: e.tensor_tensor(cE, lgtk[:], bc8(cw[:, 8:16]), ALU.is_ge))
        p.op("V", lambda e: e.tensor_copy(cw[:, 48:52], cw[:, 44:48]))
        p.op("V", lambda e: e.tensor_tensor(cA, cD, cE, ALU.mult), waits=["combs1"])
        p.op("V", lambda e: e.tensor_tensor(cw[:, 16:24], cw[:, 8:16], cw[:, 0:8], ALU.subtract))
        vset("combv2")
        p.op("S", lambda e: e.activation(cw[:, 24:32], cw[:, 16:24], AF.Exp),
             waits=["combv2"])
        sset("combs2")
        p.op("V", lambda e: e.tensor_scalar(cw[:, 32:40], cw[:, 24:32], 1.0, None, ALU.add),
             waits=["combs2"])
        p.op("V", lambda e: e.tensor_copy(cw[:, 48:52], cw[:, 44:48]))
        p.op("V", lambda e: e.reciprocal(cw[:, 40:48], cw[:, 32:40]))
        p.op("V", lambda e: e.tensor_copy(cw[:, 48:52], cw[:, 44:48]))
        p.op("V", lambda e: e.tensor_tensor(comb2[:], cA, bc8(cw[:, 40:48]), ALU.mult))
        vset("comb2_ok")

        wt = ["comb2_ok"]
        if "t1" in bank_free:
            wt.append(bank_free["t1"])
        for tb in range(8):
            p.op("T", lambda e, tb=tb: e.transpose(
                pbt[1][0:8, tb * 128:(tb + 1) * 128], comb2[:, tb * 8:(tb + 1) * 8], ident[:]),
                waits=(wt if tb == 0 else []))
        p.op("T", lambda e: e.nop(), name="ctr_ok")
        p.op("S", lambda e: e.copy(lgfm[0:8, 0:1024], pbt[1][0:8, 0:1024]),
             waits=["ctr_ok"])
        sset("cfm_ok")
        p.op("T", lambda e: e.matmul(pb[0][:], emat[:], lgfm[0:8, 0:512], start=True, stop=True),
             waits=["cfm_ok", "lgfm_ok"])
        p.op("T", lambda e: e.matmul(pb[1][:], emat[:], lgfm[0:8, 512:1024], start=True, stop=True))
        p.op("T", lambda e: e.nop(), name="expand_ok")
        p.op("V", lambda e: e.tensor_copy(cbc[:, 0:512], pb[0][:]), waits=["expand_ok"])
        p.op("V", lambda e: e.tensor_copy(cbc[:, 512:1024], pb[1][:]), name="cbc_ok")
        p.op("V", lambda e: e.tensor_tensor(hwsb[:, 0:512], pb[2][:], cbc[:, 0:512], ALU.mult),
             waits=["cbc_ok", "a_mm"])
        p.op("V", lambda e: e.tensor_tensor(hwsb[:, 512:1024], pb[3][:], cbc[:, 512:1024], ALU.mult))
        vset("hw_ok")

        base_lora(0, pb[4], pb[5], [])
        for half in range(2):
            p.op("V", lambda e, h=half: e.scalar_tensor_tensor(
                outb[:, h * 512: h * 512 + 512], pb[4 + h][:], bia[:, 80: 81],
                x1ap(0, h), ALU.add, ALU.add),
                waits=["base_o0"])
        vset("outev0")
        p.op("Y", lambda e: e.dma_start(out_d[0:128, :], outb[:, 0:1024]),
             dma=True, waits=["outev0"], name="outdma0")

        for o in range(1, 32):
            if o + 1 < 32:
                emit_ffn_dmas(o + 1)
            ev = o % 2
            pA = pb[4] if ev == 0 else pb[2]
            pB = pb[5] if ev == 0 else pb[3]
            wt0 = [f"wbd{o}"]
            if o == 1:
                wt0.append("hw_ok")
            else:
                wt0.append(f"outev{o - 2}")
            base_k_mms(o, pA, pB, wt0)
            base_lora(o, pA, pB, [])
            for half in range(2):
                P = pA if half == 0 else pB
                wt = [f"base_o{o}"]
                if o >= 2:
                    wt.append(f"outdma{o - 2}")
                p.op("V", lambda e, o=o, h=half, P=P: e.scalar_tensor_tensor(
                    outb[:, (o % 2) * 1024 + h * 512: (o % 2) * 1024 + h * 512 + 512],
                    P[:], bia[:, 80 + o: 81 + o], x1ap(o, h), ALU.add, ALU.add),
                    waits=wt)
            vset(f"outev{o}")
            p.op("Y", lambda e, o=o: e.dma_start(
                out_d[o * 128:(o + 1) * 128, :], outb[:, (o % 2) * 1024: (o % 2) * 1024 + 1024]),
                dma=True, waits=[f"outev{o}"], name=f"outdma{o}")

        p.op("Y", lambda e: e.nop(), waits=p.qsnap("Y"), inc=False)

        return _finish(nc, p, block, sems, taps, tapd, T1, T2, RR, lgtk, comb2, hwsb, out_d, outb)


def _finish(nc, p, block, sems, taps, tapd, T1, T2, RR, lgtk, comb2, hwsb, out_d, outb):
    if taps:
        wt = [("tp", p.cnt["tp"]), ("vp", p.cnt["vp"]), ("sp", p.cnt["sp"])]
        wt += p.qsnap("Q") + p.qsnap("Y")
        if "outdma31" not in p.ev:
            p.op("Y", lambda e: e.dma_start(out_d[0:128, :], outb[:, 0:1024]),
                 dma=True, waits=wt)
        p.op("Y", lambda e: e.dma_start(tapd["t1"][:], T1[:]), dma=True, waits=wt)
        p.op("Y", lambda e: e.dma_start(tapd["t2"][:], T2[:]), dma=True)
        p.op("Y", lambda e: e.dma_start(tapd["rows"][:], RR[0:1, :]), dma=True)
        p.op("Y", lambda e: e.dma_start(tapd["lgtk"][:], lgtk[:]), dma=True)
        p.op("Y", lambda e: e.dma_start(tapd["isbc"][:], nc._isbc_tap[:]), dma=True)
        if hasattr(nc, "_att_taps"):
            att4x, att4bx, SAx, PAx, PBx, PnTx = nc._att_taps
            p.op("Y", lambda e: e.dma_start(tapd["att"][:, 0:16], att4x[:]), dma=True)
            p.op("Y", lambda e: e.dma_start(tapd["att"][:, 16:32], att4bx[:, 0:16]), dma=True)
            p.op("Y", lambda e: e.dma_start(tapd["SA"][:], SAx[:]), dma=True)
            p.op("Y", lambda e: e.dma_start(tapd["PA"][:], PAx[:]), dma=True)
            p.op("Y", lambda e: e.dma_start(tapd["PB"][:], PBx[:]), dma=True)
            p.op("Y", lambda e: e.dma_start(tapd["PnT"][:], PnTx[:]), dma=True)
        p.op("Y", lambda e: e.dma_start(tapd["mobc"][:], nc._mobc_tap[:]), dma=True)
        p.op("Y", lambda e: e.dma_start(tapd["comb2"][:], comb2[:]), dma=True)
        p.op("Y", lambda e: e.dma_start(tapd["hw"][:], hwsb[:]), dma=True, name="tapsdone")
        p.op("Y", lambda e: e.nop(), waits=["tapsdone"], inc=False)
    p.emit(block, sems)
    return nc



def _prep_weights(inputs):
    f = np.float32
    conv_w = np.ascontiguousarray(inputs["L0_attn_conv_w"], dtype=f)
    cw = np.ascontiguousarray(
        conv_w.reshape(3, 32, 128, 24, 128).transpose(3, 2, 1, 0, 4)).reshape(24, 128, 12288)

    def proj_repack(w, mk, kk):
        return np.ascontiguousarray(
            w.reshape(kk, 128, mk, 128).transpose(2, 1, 0, 3)).reshape(mk, 128, kk * 128)

    wq = proj_repack(np.asarray(inputs["L0_attn_wq"], f).reshape(1024, 1024), 8, 8)
    wk = proj_repack(np.asarray(inputs["L0_attn_wk"], f).reshape(1024, 1024), 8, 8)
    wv = np.ascontiguousarray(np.asarray(inputs["L0_attn_wv"], f).reshape(8, 128, 1024))
    wo = proj_repack(np.asarray(inputs["L0_attn_wo"], f).reshape(1024, 1024), 8, 8)
    pj = proj_repack(np.asarray(inputs["L0_attn_proj_w"], f), 32, 8)
    wb = np.ascontiguousarray(
        np.asarray(inputs["L0_ffn_W_base"], f).reshape(32, 128, 32, 128).transpose(0, 3, 2, 1)
    ).reshape(32, 128, 4096)
    A2 = np.asarray(inputs["L0_ffn_A"], f).reshape(128, 4096)
    ar = np.ascontiguousarray(A2.T.reshape(32, 128, 128).transpose(1, 0, 2)).reshape(128, 4096)
    br = np.ascontiguousarray(np.asarray(inputs["L0_ffn_B"], f).transpose(0, 2, 1)).reshape(128, 4096)
    wrr = np.ascontiguousarray(
        np.asarray(inputs["L0_ffn_W_router"], f).T.reshape(32, 128, 8).transpose(1, 0, 2)
    ).reshape(128, 256)
    bvr = np.ascontiguousarray(np.asarray(inputs["L0_attn_bv"], f).reshape(1, 1024))

    biases = np.zeros((128, 113), f)
    biases[:, 112] = 1e-5
    biases[:, 0:24] = np.asarray(inputs["L0_attn_conv_b"], f).reshape(24, 128).T
    biases[:, 24:32] = np.asarray(inputs["L0_attn_bq"], f).reshape(8, 128).T
    biases[:, 32:40] = np.asarray(inputs["L0_attn_bk"], f).reshape(8, 128).T
    biases[:, 40:48] = np.asarray(inputs["L0_attn_bo"], f).reshape(8, 128).T
    biases[:, 48:80] = np.asarray(inputs["L0_attn_proj_b"], f).reshape(32, 128).T
    biases[:, 80:112] = np.asarray(inputs["L0_ffn_b_base"], f).reshape(32, 128).T

    bf = ml_dtypes.bfloat16
    ident = np.eye(128, dtype=bf)
    mask4 = np.tril(np.ones((128, 128), np.float32)).astype(bf)
    onec = np.ones((128, 1), bf)
    oner = np.ones((1, 128), bf)
    emat = np.zeros((8, 128), np.float32)
    for e_ in range(8):
        emat[e_, e_ * 16:(e_ + 1) * 16] = 2.0
    emat = emat.astype(bf)
    return dict(cw=cw, wqr=wq, wkr=wk, wvr=wv, wor=wo, pjr=pj, wbr=wb, ar=ar, br=br,
                wrr=wrr, bvr=bvr, biases=biases, identc=ident, mask4c=mask4,
                onecolc=onec, onerowc=oner, ematc=emat)


def _prep_x(inputs):
    f = np.float32
    x = np.asarray(inputs["L0_attn_inputs"], f)
    xts = []
    for c in range(NC):
        s, q0 = c // 4, (c % 4) * 1024
        xt = np.zeros((DM, TH), f)
        xt[:, 1:1025] = np.ascontiguousarray(x[s, q0:q0 + 1024, :].T)
        if c % 4 > 0:
            xt[:, 0] = x[s, q0 - 1, :]
        if c % 4 < 3:
            xt[:, 1025] = x[s, q0 + 1024, :]
        xts.append(np.ascontiguousarray(xt))
    return xts


def kernel(**inputs):
    w = _prep_weights(inputs)
    xts = _prep_x(inputs)
    nc = build()
    in_maps = [{**w, "xt": xts[c]} for c in range(NC)]
    res = run_bass_kernel_spmd(nc, in_maps, core_ids=list(range(NC)))
    out = np.empty((2, 4096, 4096), np.float32)
    for c in range(NC):
        s, q0 = c // 4, (c % 4) * 1024
        out[s, q0:q0 + 1024, :] = np.asarray(res.results[c]["out"]).T
    return out


# version 05
# speedup vs baseline: 1.0502x; 1.0230x over previous
import numpy as np
import ml_dtypes
import concourse.bass as bass
import concourse.mybir as mybir
from concourse.bass_utils import run_bass_kernel_spmd

F32 = mybir.dt.float32
BF16 = mybir.dt.bfloat16
AF = mybir.ActivationFunctionType
ALU = mybir.AluOpType
AX = mybir.AxisListType

NC = 8
T = 1024
TH = 1026
DM = 4096
DC = 32
W1 = 1024
H = 16
DH = 64
EPS = 1e-5
INV_D = 1.0 / DM

ENGMAP = {"T": "tensor", "V": "vector", "S": "scalar", "Q": "gpsimd", "Y": "sync"}
ENGSEM = {"T": "tp", "V": "vp", "S": "sp", "Q": None, "Y": None}
KD = 4
DMASEMS = [f"q{i}" for i in range(KD)] + [f"y{i}" for i in range(KD)]


class Prog:

    def __init__(self):
        self.ops = []
        self.cnt = {k: 0 for k in ("tp", "vp", "sp", *DMASEMS)}
        self.ev = {}
        self.dq = {"Q": 0, "Y": 0}

    def op(self, eng, fn, waits=(), name=None, inc=True, dma=False):
        wl = []
        for w in waits:
            if w is None:
                continue
            if isinstance(w, str):
                wl.append(self.ev[w])
            else:
                wl.append(w)
        if dma:
            pref = "q" if eng == "Q" else "y"
            sem = f"{pref}{self.dq[eng] % KD}"
            self.dq[eng] += 1
            if self.cnt[sem] > 0:
                wl.append((sem, self.cnt[sem]))
            self.cnt[sem] += 16
            if name is not None:
                self.ev[name] = (sem, self.cnt[sem])
            self.ops.append((eng, fn, wl, (sem, 16)))
            return
        sem = ENGSEM[eng]
        if inc:
            self.cnt[sem] += 1
            if name is not None:
                self.ev[name] = (sem, self.cnt[sem])
        else:
            assert name is None
        self.ops.append((eng, fn, wl, (sem, 1) if inc else None))

    def qsnap(self, eng):
        pref = "q" if eng == "Q" else "y"
        return [(f"{pref}{i}", self.cnt[f"{pref}{i}"]) for i in range(KD)
                if self.cnt[f"{pref}{i}"] > 0]

    def emit(self, block, sems):
        per = {}
        for rec in self.ops:
            per.setdefault(rec[0], []).append(rec)
        for ename, recs in per.items():
            own = ENGSEM[ename]

            def body(eng, recs=recs, own=own):
                maxw = {}
                for (_e, fn, waits, inc) in recs:
                    for (s, v) in waits:
                        if own is not None and s == own:
                            continue
                        if maxw.get(s, 0) < v:
                            eng.wait_ge(sems[s], v)
                            maxw[s] = v
                    inst = fn(eng)
                    if inc is not None:
                        s, a = inc
                        inst.then_inc(sems[s], a)

            getattr(block, ENGMAP[ename])(body)


def build(upto=99, taps=False, agroups=64):
    nc = bass.Bass("TRN2", target_bir_lowering=False)

    def din(name, shape, dt=F32):
        return nc.dram_tensor(name, list(shape), dt, kind="ExternalInput").ap()

    xt = din("xt", (DM, TH))
    cw_d = din("cw", (24, 128, 12288))
    wq_d = din("wqr", (8, 128, 1024))
    wk_d = din("wkr", (8, 128, 1024))
    wv_d = din("wvr", (8, 128, 1024))
    wo_d = din("wor", (8, 128, 1024))
    pj_d = din("pjr", (32, 128, 1024))
    wb_d = din("wbr", (32, 128, 4096))
    a_d = din("ar", (128, 4096))
    b_d = din("br", (128, 4096))
    wr_d = din("wrr", (128, 256))
    bv_d = din("bvr", (1, 1024))
    bias_d = din("biases", (128, 113))
    ident_d = din("identc", (128, 128), BF16)
    mask4_d = din("mask4c", (128, 128), BF16)
    onec_d = din("onecolc", (128, 1), BF16)
    oner_d = din("onerowc", (1, 128), BF16)
    emat_d = din("ematc", (8, 128), BF16)
    out_d = nc.dram_tensor("out", [DM, T], F32, kind="ExternalOutput").ap()
    tapd = {}
    if taps:
        tapd["t1"] = nc.dram_tensor("tap_t1", [128, 32 * 1026], BF16, kind="ExternalOutput").ap()
        tapd["t2"] = nc.dram_tensor("tap_t2", [128, 32 * 1024], BF16, kind="ExternalOutput").ap()
        tapd["rows"] = nc.dram_tensor("tap_rows", [1, 4 * TH], F32, kind="ExternalOutput").ap()
        tapd["lgtk"] = nc.dram_tensor("tap_lgtk", [128, 64], F32, kind="ExternalOutput").ap()
        tapd["comb2"] = nc.dram_tensor("tap_comb2", [128, 64], BF16, kind="ExternalOutput").ap()
        tapd["hw"] = nc.dram_tensor("tap_hw", [128, 1024], BF16, kind="ExternalOutput").ap()
        tapd["isbc"] = nc.dram_tensor("tap_isbc", [128, 1026], BF16, kind="ExternalOutput").ap()
        tapd["att"] = nc.dram_tensor("tap_att", [128, 32], F32, kind="ExternalOutput").ap()
        tapd["SA"] = nc.dram_tensor("tap_SA", [128, 1024], BF16, kind="ExternalOutput").ap()
        tapd["PA"] = nc.dram_tensor("tap_PA", [128, 1024], BF16, kind="ExternalOutput").ap()
        tapd["PB"] = nc.dram_tensor("tap_PB", [128, 512], BF16, kind="ExternalOutput").ap()
        tapd["PnT"] = nc.dram_tensor("tap_PnT", [128, 512], BF16, kind="ExternalOutput").ap()
        tapd["mobc"] = nc.dram_tensor("tap_mobc", [128, 1026], BF16, kind="ExternalOutput").ap()

    from contextlib import ExitStack
    with ExitStack() as st:
        def sb(name, shape, dt):
            return st.enter_context(nc.sbuf_tensor(name, shape, dt))

        T1 = sb("T1", [128, 32 * TH], BF16)
        T2 = sb("T2", [128, 32 * 1024], BF16)
        WA = sb("WA", [128, 12544], BF16)
        x2b = sb("x2b", [128, 2 * TH], BF16)
        RR = sb("RR", [128, 4 * TH], F32)
        rbf = sb("rbf", [128, 2 * TH], BF16)
        isbc = sb("isbc", [128, TH], BF16)
        mobc = sb("mobc", [128, TH], BF16)
        nc._isbc_tap = isbc
        nc._mobc_tap = mobc
        att4 = sb("att4", [128, 16], F32)
        att4b = sb("att4b", [128, 20], F32)
        PA = sb("PA", [128, 2 * 512], BF16)
        SA = sb("SA", [128, 2 * 512], BF16)
        PB = sb("PB", [128, 512], BF16)
        PnT = sb("PnT", [128, 512], BF16)
        xr = sb("xr", [128, 2 * 1024], F32)
        outb = sb("outb", [128, 2 * 1024], F32)
        lgtk = sb("lgtk", [128, 64], F32)
        cw = sb("cwX", [128, 52], F32)
        cwk = sb("cwk", [128, 320], F32)
        comb2 = sb("comb2", [128, 64], BF16)
        wrsb = sb("wrsb", [128, 256], BF16)
        bvbf = rbf[0:1, 0:1024]
        bvbc = isbc[:, 0:1024]
        cbc = mobc[:, 0:1024]
        hwsb = PA[:, 0:1024]
        lgfm = PA[0:8, 0:1024]
        bia = sb("bia", [128, 113], F32)
        ident = sb("ident", [128, 128], BF16)
        mask4 = sb("mask4", [128, 128], BF16)
        onec = sb("onec", [128, 1], BF16)
        oner = sb("oner", [1, 128], BF16)
        emat = sb("emat", [8, 128], BF16)

        def ps(name, shape, dt):
            return st.enter_context(nc.psum_tensor(name, shape, dt))

        pb = [ps(f"pb{i}", [128, 512], F32) for i in range(6)]
        pbt = [ps(f"pbt{i}", [128, 1024], BF16) for i in range(2)]

        sems = {k: st.enter_context(nc.semaphore(k))
                for k in ("tp", "vp", "sp", *DMASEMS)}
        block = st.enter_context(nc.Block())

        p = Prog()
        bank_free = {}

        def bfree(bids):
            return [bank_free[b] for b in bids if b in bank_free]

        def brel(bids, evname):
            for b in bids:
                bank_free[b] = evname

        def vset(name, waits=()):
            p.op("V", lambda e: e.tensor_copy(cw[:, 48:52], cw[:, 44:48]),
                 waits=waits, name=name)

        def sset(name, waits=()):
            p.op("S", lambda e: e.copy(cw[0:1, 48:52], cw[0:1, 44:48]),
                 waits=waits, name=name)

        def x1ap(m, half=None, width=1024):
            if half is None:
                return T1[:, m * 1024: m * 1024 + width]
            return T1[:, m * 1024 + half * 512: m * 1024 + half * 512 + 512]

        def x0b(c, width=TH):
            return T1[:, c * TH: c * TH + width]

        def xl1(dk, lo, width):
            base = dk * TH
            return T1[:, base + lo: base + lo + width]

        p.op("Y", lambda e: e.dma_start(ident[:], ident_d[:]), dma=True)
        p.op("Y", lambda e: e.dma_start(mask4[:], mask4_d[:]), dma=True)
        p.op("Y", lambda e: e.dma_start(onec[:], onec_d[:]), dma=True)
        p.op("Y", lambda e: e.dma_start(oner[:], oner_d[:]), dma=True)
        p.op("Y", lambda e: e.dma_start(emat[:], emat_d[:]), dma=True)
        p.op("Y", lambda e: e.dma_start(bia[:], bias_d[:]), dma=True, name="consts0")
        consts_w = None
        p.op("Q", lambda e: e.dma_start(wrsb[:], wr_d[:]), dma=True, name="wr_ld")
        consts_w = p.qsnap("Y")

        for g in range(8):
            dst = T1[:, g * 4 * TH: (g * 4 + 4) * TH].rearrange("p (c t) -> p c t", c=4)
            src = xt.rearrange("(c p) t -> p c t", p=128)[:, 4 * g: 4 * g + 4, :]
            p.op("Q", lambda e, d=dst, s=src: e.dma_start(d, s), dma=True, name=f"x0b{g}")

        for c in range(DC):
            xc = x0b(c)
            sq = x2b[:, (c % 2) * TH: (c % 2) * TH + TH]
            wv_ = [f"x0b{c // 4}"]
            if c >= 2:
                wv_.append(f"st{c - 2}")
            p.op("V", lambda e, a=sq, b=xc: e.tensor_tensor(a, b, b, ALU.mult),
                 waits=wv_, name=f"sq{c}")
            st_ = (c == 0)
            sp_ = (c == DC - 1)
            wt = consts_w + [f"x0b{c // 4}", f"sq{c}"]
            p.op("T", lambda e, r=xc, s=st_, q=sp_: e.matmul(pb[0][0:1, 0:512], onec[:], r[:, 0:512], start=s, stop=q), waits=wt)
            p.op("T", lambda e, r=xc, s=st_, q=sp_: e.matmul(pb[1][0:1, 0:512], onec[:], r[:, 512:1024], start=s, stop=q))
            p.op("T", lambda e, r=xc, s=st_, q=sp_: e.matmul(pb[2][0:1, 0:2], onec[:], r[:, 1024:1026], start=s, stop=q))
            p.op("T", lambda e, r=sq, s=st_, q=sp_: e.matmul(pb[3][0:1, 0:512], onec[:], r[:, 0:512], start=s, stop=q))
            p.op("T", lambda e, r=sq, s=st_, q=sp_: e.matmul(pb[4][0:1, 0:512], onec[:], r[:, 512:1024], start=s, stop=q))
            p.op("T", lambda e, r=sq, s=st_, q=sp_: e.matmul(pb[5][0:1, 0:2], onec[:], r[:, 1024:1026], start=s, stop=q),
                 name=f"st{c}")

        def ln_post(tag, wid, statev):
            if wid == TH:
                srcs = [(pb[0], 0, 512), (pb[1], 512, 512), (pb[2], 1024, 2)]
                sqs = [(pb[3], 0, 512), (pb[4], 512, 512), (pb[5], 1024, 2)]
            else:
                srcs = [(pb[2], 0, 512), (pb[3], 512, 512)]
                sqs = [(pb[4], 0, 512), (pb[5], 512, 512)]
            rA, rB, rC, rD = 0, TH, 2 * TH, 3 * TH
            for (bk, lo, w) in srcs:
                p.op("V", lambda e, b=bk, l=lo, w=w: e.tensor_scalar_mul(
                    RR[0:1, rA + l: rA + l + w], b[0:1, 0:w], INV_D), waits=[statev])
            for (bk, lo, w) in sqs:
                p.op("V", lambda e, b=bk, l=lo, w=w: e.tensor_scalar_mul(
                    RR[0:1, rB + l: rB + l + w], b[0:1, 0:w], INV_D))
            p.op("V", lambda e, w=wid: e.tensor_tensor(
                RR[0:1, rC: rC + w], RR[0:1, rA: rA + w], RR[0:1, rA: rA + w], ALU.mult))
            p.op("V", lambda e, w=wid: e.tensor_tensor(
                RR[0:1, rD: rD + w], RR[0:1, rB: rB + w], RR[0:1, rC: rC + w], ALU.subtract),
                 name=f"{tag}_var")
            p.op("S", lambda e, w=wid: e.activation(
                RR[0:1, rB: rB + w], RR[0:1, rD: rD + w], AF.Sqrt, bias=bia[0:1, 112:113]),
                 waits=[f"{tag}_var"], name=f"{tag}_sig")
            p.op("V", lambda e, w=wid: e.reciprocal(
                RR[0:1, rC: rC + w], RR[0:1, rB: rB + w]), waits=[f"{tag}_sig"])
            p.op("V", lambda e, w=wid: e.tensor_tensor(
                RR[0:1, rD: rD + w], RR[0:1, rA: rA + w], RR[0:1, rC: rC + w], ALU.mult),
                 name=f"{tag}_rows")
            p.op("S", lambda e, w=wid: e.copy(rbf[0:1, 0:w], RR[0:1, rC: rC + w]), waits=[f"{tag}_rows"])
            p.op("S", lambda e, w=wid: e.copy(rbf[0:1, TH: TH + w], RR[0:1, rD: rD + w]))
            sset(f"{tag}_rbf")
            if wid == TH:
                regs = [(0, 0, 512), (1, 512, 512), (2, 1024, 2)]
                mregs = [(3, 0, 512), (4, 512, 512), (5, 1024, 2)]
            else:
                regs = [(2, 0, 512), (3, 512, 512)]
                mregs = [(4, 0, 512), (5, 512, 512)]
            wt = [f"{tag}_rbf", f"{tag}_rows"]
            for (bi, lo, w) in regs:
                p.op("T", lambda e, b=bi, l=lo, w=w: e.matmul(
                    pb[b][:, 0:w], oner[:], rbf[0:1, l:l + w], start=True, stop=True), waits=wt)
            for (bi, lo, w) in mregs:
                p.op("T", lambda e, b=bi, l=lo, w=w: e.matmul(
                    pb[b][:, 0:w], oner[:], rbf[0:1, TH + l: TH + l + w], start=True, stop=True))
            p.op("T", lambda e: e.nop(), name=f"{tag}_bcmm")
            for (bi, lo, w) in regs:
                p.op("V", lambda e, b=bi, l=lo, w=w: e.tensor_copy(isbc[:, l:l + w], pb[b][:, 0:w]),
                     waits=[f"{tag}_bcmm"])
            for i, (bi, lo, w) in enumerate(mregs):
                p.op("V", lambda e, b=bi, l=lo, w=w: e.tensor_copy(mobc[:, l:l + w], pb[b][:, 0:w]),
                     name=(f"{tag}_bc" if i == len(mregs) - 1 else None))

        ln_post("ln1", TH, "st31")
        brel(["b0", "b1", "b2", "b3", "b4", "b5"], "ln1_bc")
        if upto <= 0:
            return _finish(nc, p, block, sems, taps, tapd, T1, T2, RR, lgtk, comb2, hwsb, out_d, outb)

        for c in range(DC):
            xc = x0b(c)
            tmp = x2b[:, (c % 2) * TH: (c % 2) * TH + TH]
            dst = T1[:, c * TH: (c + 1) * TH]
            p.op("V", lambda e, t=tmp, a=xc: e.tensor_tensor(t, a, isbc[:], ALU.mult),
                 waits=["ln1_bc"])
            p.op("V", lambda e, d=dst, t=tmp: e.tensor_tensor(d, t, mobc[:], ALU.subtract))
            vset(f"xln{c}")
        if upto <= 1:
            return _finish(nc, p, block, sems, taps, tapd, T1, T2, RR, lgtk, comb2, hwsb, out_d, outb)

        for oc in range(24):
            for g in range(4):
                slot = (oc * 4 + g) % 4
                wt = []
                if oc >= 1:
                    wt.append(f"cvslot{oc - 1}_{slot}")
                dst = WA[:, slot * 3072: slot * 3072 + 3072]
                srcd = cw_d[oc, :, g * 3072: (g + 1) * 3072]
                p.op("Q", lambda e, d=dst, s=srcd: e.dma_start(d, s), dma=True,
                     waits=wt, name=f"cw{oc}_{g}")
            pr = ("b0", "b1") if oc % 2 == 0 else ("b2", "b3")
            pA = pb[0] if oc % 2 == 0 else pb[2]
            pB = pb[1] if oc % 2 == 0 else pb[3]
            wt0 = bfree(pr)
            for k in range(96):
                dk, tap = k // 3, k % 3
                slot = k // 24
                g = slot
                lhs = WA[:, ((oc * 4 + g) % 4) * 3072 + (k % 24) * 128:
                         ((oc * 4 + g) % 4) * 3072 + (k % 24) * 128 + 128]
                wt = list(wt0) if k == 0 else []
                wt.append(f"cw{oc}_{g}")
                wt.append(f"xln{dk}")
                st_, sp_ = (k == 0), (k == 95)
                p.op("T", lambda e, l=lhs, d=dk, tp_=tap, s=st_, q=sp_, P=pA:
                     e.matmul(P[:], l, xl1(d, tp_, 512), start=s, stop=q), waits=wt)
                nm = f"cvslot{oc}_{slot}" if k % 24 == 23 else None
                p.op("T", lambda e, l=lhs, d=dk, tp_=tap, s=st_, q=sp_, P=pB:
                     e.matmul(P[:], l, xl1(d, tp_ + 512, 512), start=s, stop=q), name=nm)
                if k == 95:
                    p.op("T", lambda e: e.nop(), name=f"conv{oc}")
            for half, P in ((0, pA), (1, pB)):
                p.op("S", lambda e, o=oc, h=half, P=P: e.add(
                    T2[:, o * 1024 + h * 512: o * 1024 + h * 512 + 512], P[:],
                    bia[:, o:o + 1]),
                    waits=[f"conv{oc}"])
            sset(f"qkvev{oc}")
            brel(pr, f"qkvev{oc}")
        if upto <= 2:
            return _finish(nc, p, block, sems, taps, tapd, T1, T2, RR, lgtk, comb2, hwsb, out_d, outb)

        p.op("Q", lambda e: e.dma_start(bvbf[:], bv_d[:]), dma=True,
             waits=["ln1_bcmm"], name="bv_ld")

        wt = bfree(("b0", "b1"))
        p.op("T", lambda e: e.matmul(pb[0][:], oner[:], bvbf[0:1, 0:512], start=True, stop=True),
             waits=wt + ["bv_ld", "wr_ld"])
        p.op("T", lambda e: e.matmul(pb[1][:], oner[:], bvbf[0:1, 512:1024], start=True, stop=True))
        p.op("T", lambda e: e.nop(), name="bvmm")
        p.op("V", lambda e: e.tensor_copy(bvbc[:, 0:512], pb[0][:]), waits=["bvmm"])
        p.op("V", lambda e: e.tensor_copy(bvbc[:, 512:1024], pb[1][:]))
        vset("bvbc_ok")
        brel(("b0", "b1"), "bvbc_ok")

        pairs = [("b0", "b1"), ("b2", "b3"), ("b4", "b5")]
        pairt = [(pb[0], pb[1]), (pb[2], pb[3]), (pb[4], pb[5])]
        gi = 0

        def pgroup():
            nonlocal gi
            r = (pairs[gi % 3], pairt[gi % 3])
            gi += 1
            return r

        for which, wdcol, evtag, btag in (("q", 0, "qh", 24), ("k", 2048, "kh", 32)):
            wdmat = wq_d if which == "q" else wk_d
            for m in range(8):
                wtd = ["conv23"] if m < 2 else [f"{evtag}_m{m - 2}"]
                p.op("Q", lambda e, m=m, wd=wdmat, col=wdcol: e.dma_start(
                    WA[:, col + (m % 2) * 1024: col + (m % 2) * 1024 + 1024], wd[m]),
                    dma=True, waits=wtd, name=f"w{which}d{m}")
                (prn, (pA, pB)) = pgroup()
                wt = bfree(prn) + [f"w{which}d{m}", "qkvev7" if which == "q" else "qkvev15"]
                for k in range(8):
                    lhs = WA[:, wdcol + (m % 2) * 1024 + k * 128: wdcol + (m % 2) * 1024 + k * 128 + 128]
                    rhsbase = (0 if which == "q" else 8192) + k * 1024
                    st_, sp_ = (k == 0), (k == 7)
                    p.op("T", lambda e, l=lhs, rb=rhsbase, s=st_, q=sp_, P=pA:
                         e.matmul(P[:], l, T2[:, rb: rb + 512], start=s, stop=q),
                         waits=(wt if k == 0 else []))
                    p.op("T", lambda e, l=lhs, rb=rhsbase, s=st_, q=sp_, P=pB:
                         e.matmul(P[:], l, T2[:, rb + 512: rb + 1024], start=s, stop=q))
                    if k == 7:
                        p.op("T", lambda e: e.nop(), name=f"{evtag}_m{m}")
                dstb = (0 if which == "q" else 8192) + m * 1024
                for half, P in ((0, pA), (1, pB)):
                    p.op("S", lambda e, db=dstb, h=half, P=P, m=m, bt=btag: e.add(
                        T1[:, db + h * 512: db + h * 512 + 512], P[:],
                        bia[:, bt + m: bt + m + 1]),
                        waits=[f"{evtag}_m{m}"])
                sset(f"{evtag}ev{m}")
                brel(prn, f"{evtag}ev{m}")

        for k in range(8):
            p.op("Q", lambda e, k=k: e.dma_start(WA[:, 4096 + k * 1024: 4096 + k * 1024 + 1024], wv_d[k]),
                 dma=True, waits=["conv23"], name=f"wvd{k}")
        for tb in range(8):
            (prn, (pA, pB)) = pgroup()
            wt = bfree(prn) + ["qkvev23", "bvbc_ok"] + [f"wvd{k}" for k in range(8)]
            for k in range(8):
                lhs = T2[:, (16 + k) * 1024 + tb * 128: (16 + k) * 1024 + tb * 128 + 128]
                st_, sp_ = (k == 0), (k == 7)
                p.op("T", lambda e, l=lhs, k=k, s=st_, q=sp_, P=pA:
                     e.matmul(P[:], l, WA[:, 4096 + k * 1024: 4096 + k * 1024 + 512], start=s, stop=q),
                     waits=(wt if k == 0 else []))
                p.op("T", lambda e, l=lhs, k=k, s=st_, q=sp_, P=pB:
                     e.matmul(P[:], l, WA[:, 4096 + k * 1024 + 512: 4096 + k * 1024 + 1024], start=s, stop=q))
                if k == 7:
                    p.op("T", lambda e: e.nop(), name=f"vh_tb{tb}")
            for half, P in ((0, pA), (1, pB)):
                p.op("V", lambda e, t=tb, h=half, P=P: e.tensor_tensor(
                    T1[:, 16384 + t * 1024 + h * 512: 16384 + t * 1024 + h * 512 + 512],
                    P[:], bvbc[:, h * 512: h * 512 + 512], ALU.add),
                    waits=[f"vh_tb{tb}"])
            vset(f"vhev{tb}")
            brel(prn, f"vhev{tb}")
        if upto <= 3:
            return _finish(nc, p, block, sems, taps, tapd, T1, T2, RR, lgtk, comb2, hwsb, out_d, outb)

        NG = min(32, agroups)
        sc_free = [bank_free.get("b4"), bank_free.get("b5")]
        pv_free = [bank_free.get("b2"), bank_free.get("b3")]
        tr_free = [None, None]

        def att_A(g):
            tb, hg, par = g // 4, g % 4, g % 2
            pS = pb[4 + par]
            wt = ["qhev7", "khev7", f"vhev{tb}"]
            if sc_free[par] is not None:
                wt.append(sc_free[par])
            for pr_ in range(2):
                hc = hg * 2 + pr_
                qf = T1[:, hc * 1024 + tb * 128: hc * 1024 + tb * 128 + 128]
                kf = T1[:, 8192 + hc * 1024 + tb * 128: 8192 + hc * 1024 + tb * 128 + 128]
                p.op("T", lambda e, a=qf, b=kf, r=2 * pr_, P=pS:
                     e.matmul(P[:, r * 128:(r + 1) * 128], a, b, start=True, stop=True),
                     waits=(wt if pr_ == 0 else []))
                p.op("T", lambda e, a=qf[0:64, :], b=kf[0:64, :], r=2 * pr_ + 1, P=pS:
                     e.matmul(P[:, r * 128:(r + 1) * 128], a, b, start=True, stop=True))
            p.op("T", lambda e: e.nop(), name=f"smm{g}")
            wts = [f"smm{g}"]
            if g >= 2:
                wts.append(f"pexp{g - 2}")
            for pr_ in range(2):
                p.op("V", lambda e, pr_=pr_, par=par, P=pS: e.tensor_copy(
                    SA[:, par * 512 + (2 * pr_) * 128: par * 512 + (2 * pr_) * 128 + 128],
                    P[:, (2 * pr_ + 1) * 128: (2 * pr_ + 2) * 128]),
                    waits=(wts if pr_ == 0 else []))
            for pr_ in range(2):
                p.op("V", lambda e, pr_=pr_, par=par, P=pS: e.tensor_tensor(
                    SA[:, par * 512 + (2 * pr_ + 1) * 128: par * 512 + (2 * pr_ + 1) * 128 + 128],
                    P[:, (2 * pr_) * 128: (2 * pr_ + 1) * 128],
                    SA[:, par * 512 + (2 * pr_) * 128: par * 512 + (2 * pr_) * 128 + 128],
                    ALU.subtract),
                    name=(f"sa{g}" if pr_ == 1 else None))
            sc_free[par] = f"sa{g}"
            wtm = []
            if g >= 2:
                wtm.append(f"pexp{g - 2}")
            p.op("V", lambda e, par=par: e.reduce_max(
                att4[:, par * 8: par * 8 + 4],
                SA[:, par * 512:(par + 1) * 512].rearrange("p (a b) -> p a b", a=4),
                axis=AX.X), waits=wtm)
            vset(f"mx{g}")
            p.op("S", lambda e, par=par: e.mul(
                att4[:, par * 8 + 4: par * 8 + 8], att4[:, par * 8: par * 8 + 4], -0.125),
                waits=[f"mx{g}"])
            wte = [f"ptr{g - 2}"] if g >= 2 else []
            for r in range(4):
                p.op("S", lambda e, par=par, r=r: e.activation(
                    PA[:, par * 512 + r * 128: par * 512 + (r + 1) * 128],
                    SA[:, par * 512 + r * 128: par * 512 + (r + 1) * 128], AF.Exp,
                    bias=att4[:, par * 8 + 4 + r: par * 8 + 5 + r], scale=0.125),
                    waits=(wte if r == 0 else []))
            sset(f"pexp{g}")

        def att_C(g):
            par = g % 2
            p.op("V", lambda e, par=par: e.tensor_tensor(
                PB[:, 0:512], PA[:, par * 512:(par + 1) * 512],
                mask4.rearrange("p (u k) -> p u k", u=1).broadcast_to([128, 4, 128]),
                ALU.mult), waits=[f"pexp{g}"])
            p.op("V", lambda e, par=par: e.reduce_sum(
                att4b[:, par * 8: par * 8 + 4],
                PB[:, 0:512].rearrange("p (a b) -> p a b", a=4), axis=AX.X))
            p.op("V", lambda e: e.tensor_copy(att4b[:, 16:20], att4[:, 0:4]))
            p.op("V", lambda e, par=par: e.reciprocal(
                att4b[:, par * 8 + 4: par * 8 + 8], att4b[:, par * 8: par * 8 + 4]))
            p.op("V", lambda e: e.tensor_copy(att4b[:, 16:20], att4[:, 4:8]))
            sset(f"recq{g}")
            for r in range(4):
                p.op("V", lambda e, par=par, r=r: e.tensor_scalar_mul(
                    PA[:, par * 512 + r * 128: par * 512 + (r + 1) * 128],
                    PB[:, r * 128: (r + 1) * 128],
                    att4b[:, par * 8 + 4 + r: par * 8 + 5 + r]))
            vset(f"pnv{g}")

        def att_D(g):
            tb, hg, par = g // 4, g % 4, g % 2
            pO = pb[2 + par]
            pT = pbt[par]
            wt = [f"pnv{g}"]
            if tr_free[par] is not None:
                wt.append(tr_free[par])
            for r in range(4):
                p.op("T", lambda e, par=par, r=r, P=pT: e.transpose(
                    P[:, r * 128:(r + 1) * 128],
                    PA[:, par * 512 + r * 128: par * 512 + (r + 1) * 128], ident[:]),
                    waits=(wt if r == 0 else []))
            p.op("T", lambda e: e.nop(), name=f"ptr{g}")
            wtp = [f"ptr{g}"]
            if g >= 1:
                wtp.append(f"pv{g - 1}")
            p.op("S", lambda e, P=pT: e.copy(PnT[:, 0:512], P[:, 0:512]), waits=wtp)
            sset(f"pnt{g}")
            tr_free[par] = f"pnt{g}"
            wtv = [f"pnt{g}"]
            if pv_free[par] is not None:
                wtv.append(pv_free[par])
            for r in range(4):
                h = hg * 4 + r
                vl = T1[:, 16384 + tb * 1024 + h * 64: 16384 + tb * 1024 + h * 64 + 64]
                colr = (r // 2) * 128
                rowo = (h % 2) * 64
                tz = (0, 64) if rowo else None
                p.op("T", lambda e, v=vl, r=r, cr=colr, ro=rowo, tz=tz, P=pO:
                     e.matmul(P[ro:ro + 64, cr:cr + 128], v,
                              PnT[:, r * 128: (r + 1) * 128],
                              start=True, stop=True, tile_position=tz),
                     waits=(wtv if r == 0 else []))
            p.op("T", lambda e: e.nop(), name=f"pv{g}")
            for cc in range(2):
                hc = hg * 2 + cc
                p.op("S", lambda e, hc=hc, tb=tb, cc=cc, P=pO: e.copy(
                    T2[:, hc * 1024 + tb * 128: hc * 1024 + tb * 128 + 128],
                    P[:, cc * 128:(cc + 1) * 128]),
                    waits=([f"pv{g}"] if cc == 0 else []))
            sset(f"ofm{g}")
            pv_free[par] = f"ofm{g}"

        for it in range(NG + 2):
            if 0 <= it - 2 < NG:
                att_D(it - 2)
            if 0 <= it - 1 < NG:
                att_C(it - 1)
            if it < NG:
                att_A(it)
        gidx = NG
        for b_, e_ in (("b4", sc_free[0]), ("b5", sc_free[1]), ("b2", pv_free[0]),
                       ("b3", pv_free[1]), ("t0", tr_free[0]), ("t1", tr_free[1])):
            if e_ is not None:
                brel((b_,), e_)
        nc._att_taps = (att4, att4b, SA, PA, PB, PnT)
        if upto <= 4:
            return _finish(nc, p, block, sems, taps, tapd, T1, T2, RR, lgtk, comb2, hwsb, out_d, outb)

        gi = 0
        for m in range(8):
            wtd = ["qh_m7"] if m < 2 else [f"wo_m{m - 2}"]
            p.op("Q", lambda e, m=m: e.dma_start(
                WA[:, (m % 2) * 1024: (m % 2) * 1024 + 1024], wo_d[m]),
                dma=True, waits=wtd, name=f"wod{m}")
            (prn, (pA, pB)) = pgroup()
            wt = bfree(prn) + [f"wod{m}", f"ofm{gidx - 1}"]
            for k in range(8):
                lhs = WA[:, (m % 2) * 1024 + k * 128: (m % 2) * 1024 + k * 128 + 128]
                st_, sp_ = (k == 0), (k == 7)
                p.op("T", lambda e, l=lhs, k=k, s=st_, q=sp_, P=pA:
                     e.matmul(P[:], l, T2[:, k * 1024: k * 1024 + 512], start=s, stop=q),
                     waits=(wt if k == 0 else []))
                p.op("T", lambda e, l=lhs, k=k, s=st_, q=sp_, P=pB:
                     e.matmul(P[:], l, T2[:, k * 1024 + 512: k * 1024 + 1024], start=s, stop=q))
                if k == 7:
                    p.op("T", lambda e: e.nop(), name=f"wo_m{m}")
            for half, P in ((0, pA), (1, pB)):
                p.op("S", lambda e, m=m, h=half, P=P: e.add(
                    T2[:, 8192 + m * 1024 + h * 512: 8192 + m * 1024 + h * 512 + 512],
                    P[:], bia[:, 40 + m: 41 + m]),
                    waits=[f"wo_m{m}"])
            sset(f"woev{m}")
            brel(prn, f"woev{m}")
        if upto <= 5:
            return _finish(nc, p, block, sems, taps, tapd, T1, T2, RR, lgtk, comb2, hwsb, out_d, outb)

        p.op("Q", lambda e: e.dma_start(WA[:, 4096: 8192], a_d[:]),
             dma=True, waits=["vh_tb7"], name="a_ld")

        wt_first = bfree(("b0", "b1")) + ["woev7"]
        wt_stats = bfree(("b2", "b3", "b4", "b5"))
        for m in range(32):
            wtd = ["kh_m7"] if m < 2 else [f"proj_m{m - 2}"]
            p.op("Q", lambda e, m=m: e.dma_start(
                WA[:, 2048 + (m % 2) * 1024: 2048 + (m % 2) * 1024 + 1024], pj_d[m]),
                dma=True, waits=wtd, name=f"pjd{m}")
            wtx = [] if m < 2 else [f"x1c{m - 2}"]
            p.op("Y", lambda e, m=m: e.dma_start(
                xr[:, (m % 2) * 1024: (m % 2) * 1024 + 1024],
                xt[m * 128:(m + 1) * 128, 1: 1025]), dma=True, waits=wtx, name=f"xr{m}")
            wt = list(wt_first) if m == 0 else [f"x1ch0_{m - 1}"]
            wt.append(f"pjd{m}")
            for k in range(8):
                lhs = WA[:, 2048 + (m % 2) * 1024 + k * 128: 2048 + (m % 2) * 1024 + k * 128 + 128]
                st_, sp_ = (k == 0), (k == 7)
                p.op("T", lambda e, l=lhs, k=k, s=st_, q=sp_:
                     e.matmul(pb[0][:], l, T2[:, 8192 + k * 1024: 8192 + k * 1024 + 512], start=s, stop=q),
                     waits=(wt if k == 0 else []))
                p.op("T", lambda e, l=lhs, k=k, s=st_, q=sp_:
                     e.matmul(pb[1][:], l, T2[:, 8192 + k * 1024 + 512: 8192 + k * 1024 + 1024], start=s, stop=q),
                     waits=([f"x1c{m - 1}"] if (k == 0 and m >= 1) else []))
                if k == 7:
                    p.op("T", lambda e: e.nop(), name=f"proj_m{m}")
            for half in range(2):
                p.op("V", lambda e, m=m, h=half: e.scalar_tensor_tensor(
                    x1ap(m, h), pb[h][:], bia[:, 48 + m: 49 + m],
                    xr[:, (m % 2) * 1024 + h * 512: (m % 2) * 1024 + h * 512 + 512],
                    ALU.add, ALU.add),
                    waits=[f"proj_m{m}", f"xr{m}"],
                    name=(f"x1ch0_{m}" if half == 0 else None))
            vset(f"x1c{m}")
            p.op("V", lambda e, m=m: e.tensor_tensor(
                x2b[:, (m % 2) * TH: (m % 2) * TH + 1024], x1ap(m), x1ap(m), ALU.mult),
                waits=([f"st2_{m - 2}"] if m >= 2 else []))
            vset(f"x1sq{m}")
            st_, sp_ = (m == 0), (m == 31)
            wt2 = (list(wt_stats) if m == 0 else []) + [f"x1sq{m}"]
            p.op("T", lambda e, m=m, s=st_, q=sp_: e.matmul(
                pb[2][0:1, 0:512], onec[:], x1ap(m, 0), start=s, stop=q), waits=wt2)
            p.op("T", lambda e, m=m, s=st_, q=sp_: e.matmul(
                pb[3][0:1, 0:512], onec[:], x1ap(m, 1), start=s, stop=q))
            p.op("T", lambda e, m=m, s=st_, q=sp_: e.matmul(
                pb[4][0:1, 0:512], onec[:],
                x2b[:, (m % 2) * TH: (m % 2) * TH + 512], start=s, stop=q))
            p.op("T", lambda e, m=m, s=st_, q=sp_: e.matmul(
                pb[5][0:1, 0:512], onec[:],
                x2b[:, (m % 2) * TH + 512: (m % 2) * TH + 1024], start=s, stop=q))
            p.op("T", lambda e: e.nop(), name=f"st2_{m}")

        ln_post("ln2", 1024, "st2_31")
        brel(("b2", "b3", "b4", "b5"), "ln2_bc")
        brel(("b0", "b1"), "x1c31")
        if upto <= 6:
            return _finish(nc, p, block, sems, taps, tapd, T1, T2, RR, lgtk, comb2, hwsb, out_d, outb)

        for c in range(DC):
            tmp = x2b[:, (c % 2) * TH: (c % 2) * TH + 1024]
            wt = ["ln2_bc"]
            if c == 0:
                wt.append("proj_m31")
            p.op("V", lambda e, t=tmp, c=c: e.tensor_tensor(t, x1ap(c), isbc[:, 0:1024], ALU.mult),
                 waits=wt)
            p.op("V", lambda e, t=tmp, c=c: e.tensor_tensor(
                T2[:, c * 1024:(c + 1) * 1024], t, mobc[:, 0:1024], ALU.subtract))
            vset(f"xln2_{c}")
        if upto <= 7:
            return _finish(nc, p, block, sems, taps, tapd, T1, T2, RR, lgtk, comb2, hwsb, out_d, outb)

        def emit_ffn_dmas(o):
            wtb = ["vh_tb7"] if o < 2 else [f"base_o{o - 2}"]
            p.op("Q", lambda e, o=o: e.dma_start(
                WA[:, 8192 + (o % 2) * 128: 8192 + (o % 2) * 128 + 128],
                b_d[:, o * 128: (o + 1) * 128]), dma=True, waits=wtb, name=f"bld{o}")
            if o == 0:
                wt = ["vh_tb7"]
            elif o == 1:
                wt = ["proj_m31"]
            else:
                wt = [f"base_o{o - 2}"]
            base_c = 8448 if o % 2 == 0 else 0
            p.op("Q", lambda e, o=o, bc=base_c: e.dma_start(
                WA[:, bc: bc + 4096], wb_d[o]),
                dma=True, waits=wt, name=f"wbd{o}")

        emit_ffn_dmas(0)
        emit_ffn_dmas(1)
        wrtA = bfree(("b0", "b1"))
        wrtH = bfree(("b2", "b3"))
        for k in range(DC):
            st_, sp_ = (k == 0), (k == DC - 1)
            wt = [f"xln2_{k}", "wr_ld", "a_ld"]
            if k == 0:
                wt += wrtA + wrtH
            p.op("T", lambda e, k=k, s=st_, q=sp_: e.matmul(
                pb[0][0:8, 0:512], wrsb[:, k * 8:(k + 1) * 8],
                T2[:, k * 1024: k * 1024 + 512], start=s, stop=q), waits=wt)
            p.op("T", lambda e, k=k, s=st_, q=sp_: e.matmul(
                pb[1][0:8, 0:512], wrsb[:, k * 8:(k + 1) * 8],
                T2[:, k * 1024 + 512: k * 1024 + 1024], start=s, stop=q))
            if sp_:
                p.op("T", lambda e: e.nop(), name="router_mm")
            p.op("T", lambda e, k=k, s=st_, q=sp_: e.matmul(
                pb[2][:], WA[:, 4096 + k * 128: 4096 + k * 128 + 128],
                T2[:, k * 1024: k * 1024 + 512], start=s, stop=q))
            p.op("T", lambda e, k=k, s=st_, q=sp_: e.matmul(
                pb[3][:], WA[:, 4096 + k * 128: 4096 + k * 128 + 128],
                T2[:, k * 1024 + 512: k * 1024 + 1024], start=s, stop=q))
            if sp_:
                p.op("T", lambda e: e.nop(), name="a_mm")

        def base_k_mms(o, pA, pB, wt0):
            for k in range(DC):
                bc = 8448 if o % 2 == 0 else 0
                lhs = WA[:, bc + k * 128: bc + k * 128 + 128]
                st_ = (k == 0)
                p.op("T", lambda e, l=lhs, k=k, s=st_, P=pA:
                     e.matmul(P[:], l, T2[:, k * 1024: k * 1024 + 512], start=s, stop=False),
                     waits=(wt0 if k == 0 else [f"xln2_{k}"]))
                p.op("T", lambda e, l=lhs, k=k, s=st_, P=pB:
                     e.matmul(P[:], l, T2[:, k * 1024 + 512: k * 1024 + 1024], start=s, stop=False))

        def base_lora(o, pA, pB, extra):
            lhs = b_d
            p.op("T", lambda e, o=o, P=pA: e.matmul(
                P[:], WA[:, 8192 + (o % 2) * 128: 8192 + (o % 2) * 128 + 128],
                hwsb[:, 0:512], start=False, stop=True),
                waits=["hw_ok", f"bld{o}"] + extra)
            p.op("T", lambda e, o=o, P=pB: e.matmul(
                P[:], WA[:, 8192 + (o % 2) * 128: 8192 + (o % 2) * 128 + 128],
                hwsb[:, 512:1024], start=False, stop=True))
            p.op("T", lambda e: e.nop(), name=f"base_o{o}")

        wt0 = bfree(("b4", "b5")) + ["wbd0", "xln2_0"]
        base_k_mms(0, pb[4], pb[5], wt0)

        p.op("S", lambda e: e.copy(lgfm[0:8, 0:512], pb[0][0:8, 0:512]), waits=["router_mm"])
        p.op("S", lambda e: e.copy(lgfm[0:8, 512:1024], pb[1][0:8, 0:512]))
        sset("lgfm_ok")
        wt = ["lgfm_ok"]
        if "t0" in bank_free:
            wt.append(bank_free["t0"])
        for tb in range(8):
            p.op("T", lambda e, tb=tb: e.transpose(
                pbt[0][:, tb * 8:(tb + 1) * 8], lgfm[0:8, tb * 128:(tb + 1) * 128],
                ident[0:8, 0:8]),
                waits=(wt if tb == 0 else []))
        p.op("T", lambda e: e.nop(), name="ltr_ok")
        p.op("V", lambda e: e.tensor_copy(lgtk[:], pbt[0][:, 0:64]), waits=["ltr_ok"])
        p.op("V", lambda e: e.tensor_copy(cw[:, 48:52], cw[:, 44:48]), name="lgtk_ok")

        def bc8(ap8):
            return ap8.rearrange("p (e u) -> p e u", u=1).broadcast_to([128, 8, 8])

        cA = cwk[:, 0:64]; cB = cwk[:, 64:128]; cC = cwk[:, 128:192]
        cD = cwk[:, 192:256]; cE = cwk[:, 256:320]
        p.op("V", lambda e: e.reduce_max(cw[:, 0:8], lgtk.rearrange("p (a b) -> p a b", a=8), axis=AX.X),
             waits=["lgtk_ok"])
        p.op("V", lambda e: e.tensor_copy(cw[:, 48:52], lgtk[:, 0:4]))
        p.op("V", lambda e: e.tensor_tensor(cA, lgtk[:], bc8(cw[:, 0:8]), ALU.is_ge))
        p.op("V", lambda e: e.tensor_copy(cw[:, 48:52], cw[:, 44:48]))
        p.op("V", lambda e: e.scalar_tensor_tensor(cB, cA, -1e9, lgtk[:], ALU.mult, ALU.add))
        p.op("V", lambda e: e.tensor_copy(cw[:, 48:52], cw[:, 44:48]))
        p.op("V", lambda e: e.reduce_max(cw[:, 8:16], cB.rearrange("p (a b) -> p a b", a=8), axis=AX.X))
        p.op("V", lambda e: e.tensor_tensor(cC, lgtk[:], bc8(cw[:, 0:8]), ALU.subtract))
        vset("combv1")
        p.op("S", lambda e: e.activation(cD, cC, AF.Exp), waits=["combv1"], name="combs1")
        p.op("V", lambda e: e.tensor_tensor(cE, lgtk[:], bc8(cw[:, 8:16]), ALU.is_ge))
        p.op("V", lambda e: e.tensor_copy(cw[:, 48:52], cw[:, 44:48]))
        p.op("V", lambda e: e.tensor_tensor(cA, cD, cE, ALU.mult), waits=["combs1"])
        p.op("V", lambda e: e.tensor_tensor(cw[:, 16:24], cw[:, 8:16], cw[:, 0:8], ALU.subtract))
        vset("combv2")
        p.op("S", lambda e: e.activation(cw[:, 24:32], cw[:, 16:24], AF.Exp),
             waits=["combv2"])
        sset("combs2")
        p.op("V", lambda e: e.tensor_scalar(cw[:, 32:40], cw[:, 24:32], 1.0, None, ALU.add),
             waits=["combs2"])
        p.op("V", lambda e: e.tensor_copy(cw[:, 48:52], cw[:, 44:48]))
        p.op("V", lambda e: e.reciprocal(cw[:, 40:48], cw[:, 32:40]))
        p.op("V", lambda e: e.tensor_copy(cw[:, 48:52], cw[:, 44:48]))
        p.op("V", lambda e: e.tensor_tensor(comb2[:], cA, bc8(cw[:, 40:48]), ALU.mult))
        vset("comb2_ok")

        wt = ["comb2_ok"]
        if "t1" in bank_free:
            wt.append(bank_free["t1"])
        for tb in range(8):
            p.op("T", lambda e, tb=tb: e.transpose(
                pbt[1][0:8, tb * 128:(tb + 1) * 128], comb2[:, tb * 8:(tb + 1) * 8], ident[:]),
                waits=(wt if tb == 0 else []))
        p.op("T", lambda e: e.nop(), name="ctr_ok")
        p.op("S", lambda e: e.copy(lgfm[0:8, 0:1024], pbt[1][0:8, 0:1024]),
             waits=["ctr_ok"])
        sset("cfm_ok")
        p.op("T", lambda e: e.matmul(pb[0][:], emat[:], lgfm[0:8, 0:512], start=True, stop=True),
             waits=["cfm_ok", "lgfm_ok"])
        p.op("T", lambda e: e.matmul(pb[1][:], emat[:], lgfm[0:8, 512:1024], start=True, stop=True))
        p.op("T", lambda e: e.nop(), name="expand_ok")
        p.op("V", lambda e: e.tensor_copy(cbc[:, 0:512], pb[0][:]), waits=["expand_ok"])
        p.op("V", lambda e: e.tensor_copy(cbc[:, 512:1024], pb[1][:]), name="cbc_ok")
        p.op("V", lambda e: e.tensor_tensor(hwsb[:, 0:512], pb[2][:], cbc[:, 0:512], ALU.mult),
             waits=["cbc_ok", "a_mm"])
        p.op("V", lambda e: e.tensor_tensor(hwsb[:, 512:1024], pb[3][:], cbc[:, 512:1024], ALU.mult))
        vset("hw_ok")

        base_lora(0, pb[4], pb[5], [])
        for half in range(2):
            p.op("V", lambda e, h=half: e.scalar_tensor_tensor(
                outb[:, h * 512: h * 512 + 512], pb[4 + h][:], bia[:, 80: 81],
                x1ap(0, h), ALU.add, ALU.add),
                waits=["base_o0"])
        vset("outev0")
        p.op("Y", lambda e: e.dma_start(out_d[0:128, :], outb[:, 0:1024]),
             dma=True, waits=["outev0"], name="outdma0")

        for o in range(1, 32):
            if o + 1 < 32:
                emit_ffn_dmas(o + 1)
            ev = o % 2
            pA = pb[4] if ev == 0 else pb[2]
            pB = pb[5] if ev == 0 else pb[3]
            wt0 = [f"wbd{o}"]
            if o == 1:
                wt0.append("hw_ok")
            else:
                wt0.append(f"outev{o - 2}")
            base_k_mms(o, pA, pB, wt0)
            base_lora(o, pA, pB, [])
            for half in range(2):
                P = pA if half == 0 else pB
                wt = [f"base_o{o}"]
                if o >= 2:
                    wt.append(f"outdma{o - 2}")
                p.op("V", lambda e, o=o, h=half, P=P: e.scalar_tensor_tensor(
                    outb[:, (o % 2) * 1024 + h * 512: (o % 2) * 1024 + h * 512 + 512],
                    P[:], bia[:, 80 + o: 81 + o], x1ap(o, h), ALU.add, ALU.add),
                    waits=wt)
            vset(f"outev{o}")
            p.op("Y", lambda e, o=o: e.dma_start(
                out_d[o * 128:(o + 1) * 128, :], outb[:, (o % 2) * 1024: (o % 2) * 1024 + 1024]),
                dma=True, waits=[f"outev{o}"], name=f"outdma{o}")

        p.op("Y", lambda e: e.nop(), waits=p.qsnap("Y"), inc=False)

        return _finish(nc, p, block, sems, taps, tapd, T1, T2, RR, lgtk, comb2, hwsb, out_d, outb)


def _finish(nc, p, block, sems, taps, tapd, T1, T2, RR, lgtk, comb2, hwsb, out_d, outb):
    if taps:
        wt = [("tp", p.cnt["tp"]), ("vp", p.cnt["vp"]), ("sp", p.cnt["sp"])]
        wt += p.qsnap("Q") + p.qsnap("Y")
        if "outdma31" not in p.ev:
            p.op("Y", lambda e: e.dma_start(out_d[0:128, :], outb[:, 0:1024]),
                 dma=True, waits=wt)
        p.op("Y", lambda e: e.dma_start(tapd["t1"][:], T1[:]), dma=True, waits=wt)
        p.op("Y", lambda e: e.dma_start(tapd["t2"][:], T2[:]), dma=True)
        p.op("Y", lambda e: e.dma_start(tapd["rows"][:], RR[0:1, :]), dma=True)
        p.op("Y", lambda e: e.dma_start(tapd["lgtk"][:], lgtk[:]), dma=True)
        p.op("Y", lambda e: e.dma_start(tapd["isbc"][:], nc._isbc_tap[:]), dma=True)
        if hasattr(nc, "_att_taps"):
            att4x, att4bx, SAx, PAx, PBx, PnTx = nc._att_taps
            p.op("Y", lambda e: e.dma_start(tapd["att"][:, 0:16], att4x[:]), dma=True)
            p.op("Y", lambda e: e.dma_start(tapd["att"][:, 16:32], att4bx[:, 0:16]), dma=True)
            p.op("Y", lambda e: e.dma_start(tapd["SA"][:], SAx[:]), dma=True)
            p.op("Y", lambda e: e.dma_start(tapd["PA"][:], PAx[:]), dma=True)
            p.op("Y", lambda e: e.dma_start(tapd["PB"][:], PBx[:]), dma=True)
            p.op("Y", lambda e: e.dma_start(tapd["PnT"][:], PnTx[:]), dma=True)
        p.op("Y", lambda e: e.dma_start(tapd["mobc"][:], nc._mobc_tap[:]), dma=True)
        p.op("Y", lambda e: e.dma_start(tapd["comb2"][:], comb2[:]), dma=True)
        p.op("Y", lambda e: e.dma_start(tapd["hw"][:], hwsb[:]), dma=True, name="tapsdone")
        p.op("Y", lambda e: e.nop(), waits=["tapsdone"], inc=False)
    p.emit(block, sems)
    return nc



def _prep_weights(inputs):
    f = np.float32
    conv_w = np.ascontiguousarray(inputs["L0_attn_conv_w"], dtype=f)
    cw = np.ascontiguousarray(
        conv_w.reshape(3, 32, 128, 24, 128).transpose(3, 2, 1, 0, 4)).reshape(24, 128, 12288)

    def proj_repack(w, mk, kk):
        return np.ascontiguousarray(
            w.reshape(kk, 128, mk, 128).transpose(2, 1, 0, 3)).reshape(mk, 128, kk * 128)

    wq = proj_repack(np.asarray(inputs["L0_attn_wq"], f).reshape(1024, 1024), 8, 8)
    wk = proj_repack(np.asarray(inputs["L0_attn_wk"], f).reshape(1024, 1024), 8, 8)
    wv = np.ascontiguousarray(np.asarray(inputs["L0_attn_wv"], f).reshape(8, 128, 1024))
    wo = proj_repack(np.asarray(inputs["L0_attn_wo"], f).reshape(1024, 1024), 8, 8)
    pj = proj_repack(np.asarray(inputs["L0_attn_proj_w"], f), 32, 8)
    wb = np.ascontiguousarray(
        np.asarray(inputs["L0_ffn_W_base"], f).reshape(32, 128, 32, 128).transpose(0, 3, 2, 1)
    ).reshape(32, 128, 4096)
    A2 = np.asarray(inputs["L0_ffn_A"], f).reshape(128, 4096)
    ar = np.ascontiguousarray(A2.T.reshape(32, 128, 128).transpose(1, 0, 2)).reshape(128, 4096)
    br = np.ascontiguousarray(np.asarray(inputs["L0_ffn_B"], f).transpose(0, 2, 1)).reshape(128, 4096)
    wrr = np.ascontiguousarray(
        np.asarray(inputs["L0_ffn_W_router"], f).T.reshape(32, 128, 8).transpose(1, 0, 2)
    ).reshape(128, 256)
    bvr = np.ascontiguousarray(np.asarray(inputs["L0_attn_bv"], f).reshape(1, 1024))

    biases = np.zeros((128, 113), f)
    biases[:, 112] = 1e-5
    biases[:, 0:24] = np.asarray(inputs["L0_attn_conv_b"], f).reshape(24, 128).T
    biases[:, 24:32] = np.asarray(inputs["L0_attn_bq"], f).reshape(8, 128).T
    biases[:, 32:40] = np.asarray(inputs["L0_attn_bk"], f).reshape(8, 128).T
    biases[:, 40:48] = np.asarray(inputs["L0_attn_bo"], f).reshape(8, 128).T
    biases[:, 48:80] = np.asarray(inputs["L0_attn_proj_b"], f).reshape(32, 128).T
    biases[:, 80:112] = np.asarray(inputs["L0_ffn_b_base"], f).reshape(32, 128).T

    bf = ml_dtypes.bfloat16
    ident = np.eye(128, dtype=bf)
    mask4 = np.tril(np.ones((128, 128), np.float32)).astype(bf)
    onec = np.ones((128, 1), bf)
    oner = np.ones((1, 128), bf)
    emat = np.zeros((8, 128), np.float32)
    for e_ in range(8):
        emat[e_, e_ * 16:(e_ + 1) * 16] = 2.0
    emat = emat.astype(bf)
    return dict(cw=cw, wqr=wq, wkr=wk, wvr=wv, wor=wo, pjr=pj, wbr=wb, ar=ar, br=br,
                wrr=wrr, bvr=bvr, biases=biases, identc=ident, mask4c=mask4,
                onecolc=onec, onerowc=oner, ematc=emat)


def _prep_x(inputs):
    f = np.float32
    x = np.asarray(inputs["L0_attn_inputs"], f)
    xts = []
    for c in range(NC):
        s, q0 = c // 4, (c % 4) * 1024
        xt = np.zeros((DM, TH), f)
        xt[:, 1:1025] = np.ascontiguousarray(x[s, q0:q0 + 1024, :].T)
        if c % 4 > 0:
            xt[:, 0] = x[s, q0 - 1, :]
        if c % 4 < 3:
            xt[:, 1025] = x[s, q0 + 1024, :]
        xts.append(np.ascontiguousarray(xt))
    return xts


def kernel(**inputs):
    w = _prep_weights(inputs)
    xts = _prep_x(inputs)
    nc = build()
    in_maps = [{**w, "xt": xts[c]} for c in range(NC)]
    res = run_bass_kernel_spmd(nc, in_maps, core_ids=list(range(NC)))
    out = np.empty((2, 4096, 4096), np.float32)
    for c in range(NC):
        s, q0 = c // 4, (c % 4) * 1024
        out[s, q0:q0 + 1024, :] = np.asarray(res.results[c]["out"]).T
    return out


# version 08
# speedup vs baseline: 1.0826x; 1.0309x over previous
import numpy as np
import ml_dtypes
import concourse.bass as bass
import concourse.mybir as mybir
from concourse.bass_utils import run_bass_kernel_spmd

F32 = mybir.dt.float32
BF16 = mybir.dt.bfloat16
AF = mybir.ActivationFunctionType
ALU = mybir.AluOpType
AX = mybir.AxisListType

NC = 8
T = 1024
TH = 1026
DM = 4096
DC = 32
W1 = 1024
H = 16
DH = 64
EPS = 1e-5
INV_D = 1.0 / DM

ENGMAP = {"T": "tensor", "V": "vector", "S": "scalar", "Q": "gpsimd", "Y": "sync"}
ENGSEM = {"T": "tp", "V": "vp", "S": "sp", "Q": None, "Y": None}
KD = 4
DMASEMS = [f"q{i}" for i in range(KD)] + [f"y{i}" for i in range(KD)]


class Prog:

    def __init__(self):
        self.ops = []
        self.cnt = {k: 0 for k in ("tp", "vp", "sp", *DMASEMS)}
        self.ev = {}
        self.dq = {"Q": 0, "Y": 0}

    def op(self, eng, fn, waits=(), name=None, inc=True, dma=False):
        wl = []
        for w in waits:
            if w is None:
                continue
            if isinstance(w, str):
                wl.append(self.ev[w])
            else:
                wl.append(w)
        if dma:
            pref = "q" if eng == "Q" else "y"
            sem = f"{pref}{self.dq[eng] % KD}"
            self.dq[eng] += 1
            if self.cnt[sem] > 0:
                wl.append((sem, self.cnt[sem]))
            self.cnt[sem] += 16
            if name is not None:
                self.ev[name] = (sem, self.cnt[sem])
            self.ops.append((eng, fn, wl, (sem, 16)))
            return
        sem = ENGSEM[eng]
        if inc:
            self.cnt[sem] += 1
            if name is not None:
                self.ev[name] = (sem, self.cnt[sem])
        else:
            assert name is None
        self.ops.append((eng, fn, wl, (sem, 1) if inc else None))

    def qsnap(self, eng):
        pref = "q" if eng == "Q" else "y"
        return [(f"{pref}{i}", self.cnt[f"{pref}{i}"]) for i in range(KD)
                if self.cnt[f"{pref}{i}"] > 0]

    def emit(self, block, sems):
        per = {}
        for rec in self.ops:
            per.setdefault(rec[0], []).append(rec)
        for ename, recs in per.items():
            own = ENGSEM[ename]

            def body(eng, recs=recs, own=own):
                maxw = {}
                for (_e, fn, waits, inc) in recs:
                    for (s, v) in waits:
                        if own is not None and s == own:
                            continue
                        if maxw.get(s, 0) < v:
                            eng.wait_ge(sems[s], v)
                            maxw[s] = v
                    inst = fn(eng)
                    if inc is not None:
                        s, a = inc
                        inst.then_inc(sems[s], a)

            getattr(block, ENGMAP[ename])(body)


def build(upto=99, taps=False, agroups=64):
    nc = bass.Bass("TRN2", target_bir_lowering=False)

    def din(name, shape, dt=F32):
        return nc.dram_tensor(name, list(shape), dt, kind="ExternalInput").ap()

    xt = din("xt", (DM, TH))
    cw_d = din("cw", (24, 128, 12288))
    wq_d = din("wqr", (8, 128, 1024))
    wk_d = din("wkr", (8, 128, 1024))
    wv_d = din("wvr", (8, 128, 1024))
    wo_d = din("wor", (8, 128, 1024))
    pj_d = din("pjr", (32, 128, 1024))
    wb_d = din("wbr", (32, 128, 4096))
    a_d = din("ar", (128, 4096))
    b_d = din("br", (128, 4096))
    wr_d = din("wrr", (128, 256))
    bv_d = din("bvr", (1, 1024))
    bias_d = din("biases", (128, 113))
    ident_d = din("identc", (128, 128), BF16)
    mask4_d = din("mask4c", (128, 128), BF16)
    onec_d = din("onecolc", (128, 1), BF16)
    oner_d = din("onerowc", (1, 128), BF16)
    emat_d = din("ematc", (8, 128), BF16)
    out_d = nc.dram_tensor("out", [DM, T], F32, kind="ExternalOutput").ap()
    tapd = {}
    if taps:
        tapd["t1"] = nc.dram_tensor("tap_t1", [128, 32 * 1026], BF16, kind="ExternalOutput").ap()
        tapd["t2"] = nc.dram_tensor("tap_t2", [128, 32 * 1024], BF16, kind="ExternalOutput").ap()
        tapd["rows"] = nc.dram_tensor("tap_rows", [1, 4 * TH], F32, kind="ExternalOutput").ap()
        tapd["lgtk"] = nc.dram_tensor("tap_lgtk", [128, 64], F32, kind="ExternalOutput").ap()
        tapd["comb2"] = nc.dram_tensor("tap_comb2", [128, 64], BF16, kind="ExternalOutput").ap()
        tapd["hw"] = nc.dram_tensor("tap_hw", [128, 1024], BF16, kind="ExternalOutput").ap()
        tapd["isbc"] = nc.dram_tensor("tap_isbc", [128, 1026], BF16, kind="ExternalOutput").ap()
        tapd["att"] = nc.dram_tensor("tap_att", [128, 32], F32, kind="ExternalOutput").ap()
        tapd["SA"] = nc.dram_tensor("tap_SA", [128, 1024], BF16, kind="ExternalOutput").ap()
        tapd["PA"] = nc.dram_tensor("tap_PA", [128, 1024], BF16, kind="ExternalOutput").ap()
        tapd["PB"] = nc.dram_tensor("tap_PB", [128, 512], BF16, kind="ExternalOutput").ap()
        tapd["PnT"] = nc.dram_tensor("tap_PnT", [128, 512], BF16, kind="ExternalOutput").ap()
        tapd["mobc"] = nc.dram_tensor("tap_mobc", [128, 1026], BF16, kind="ExternalOutput").ap()

    from contextlib import ExitStack
    with ExitStack() as st:
        def sb(name, shape, dt):
            return st.enter_context(nc.sbuf_tensor(name, shape, dt))

        T1 = sb("T1", [128, 32 * TH], BF16)
        T2 = sb("T2", [128, 32 * 1024], BF16)
        WA = sb("WA", [128, 12544], BF16)
        x2b = sb("x2b", [128, 2 * TH], BF16)
        RR = sb("RR", [128, 4 * TH], F32)
        rbf = sb("rbf", [128, 2 * TH], BF16)
        isbc = sb("isbc", [128, TH], BF16)
        mobc = sb("mobc", [128, TH], BF16)
        nc._isbc_tap = isbc
        nc._mobc_tap = mobc
        att4 = sb("att4", [128, 16], F32)
        att4b = sb("att4b", [128, 20], F32)
        PA = sb("PA", [128, 2 * 512], BF16)
        SA = sb("SA", [128, 2 * 512], BF16)
        PB = sb("PB", [128, 512], BF16)
        PnT = sb("PnT", [128, 512], BF16)
        xr = sb("xr", [128, 2 * 1024], F32)
        outb = sb("outb", [128, 2 * 1024], F32)
        lgtk = sb("lgtk", [128, 64], F32)
        cw = sb("cwX", [128, 52], F32)
        cwk = sb("cwk", [128, 320], F32)
        comb2 = sb("comb2", [128, 64], BF16)
        wrsb = sb("wrsb", [128, 256], BF16)
        bvbf = rbf[0:1, 0:1024]
        bvbc = isbc[:, 0:1024]
        cbc = mobc[:, 0:1024]
        hwsb = PA[:, 0:1024]
        lgfm = PA[0:8, 0:1024]
        bia = sb("bia", [128, 113], F32)
        ident = sb("ident", [128, 128], BF16)
        mask4 = sb("mask4", [128, 128], BF16)
        onec = sb("onec", [128, 1], BF16)
        oner = sb("oner", [1, 128], BF16)
        emat = sb("emat", [8, 128], BF16)

        def ps(name, shape, dt):
            return st.enter_context(nc.psum_tensor(name, shape, dt))

        pb = [ps(f"pb{i}", [128, 512], F32) for i in range(6)]
        pbt = [ps(f"pbt{i}", [128, 1024], BF16) for i in range(2)]

        sems = {k: st.enter_context(nc.semaphore(k))
                for k in ("tp", "vp", "sp", *DMASEMS)}
        block = st.enter_context(nc.Block())

        p = Prog()
        bank_free = {}

        def bfree(bids):
            return [bank_free[b] for b in bids if b in bank_free]

        def brel(bids, evname):
            for b in bids:
                bank_free[b] = evname

        def vset(name, waits=()):
            p.op("V", lambda e: e.tensor_copy(cw[:, 48:52], cw[:, 44:48]),
                 waits=waits, name=name)

        def sset(name, waits=()):
            p.op("S", lambda e: e.copy(cw[0:1, 48:52], cw[0:1, 44:48]),
                 waits=waits, name=name)

        def x1ap(m, half=None, width=1024):
            if half is None:
                return T1[:, m * 1024: m * 1024 + width]
            return T1[:, m * 1024 + half * 512: m * 1024 + half * 512 + 512]

        def x0b(c, width=TH):
            return T1[:, c * TH: c * TH + width]

        def xl1(dk, lo, width):
            base = dk * TH
            return T1[:, base + lo: base + lo + width]

        p.op("Y", lambda e: e.dma_start(ident[:], ident_d[:]), dma=True)
        p.op("Y", lambda e: e.dma_start(mask4[:], mask4_d[:]), dma=True)
        p.op("Y", lambda e: e.dma_start(onec[:], onec_d[:]), dma=True)
        p.op("Y", lambda e: e.dma_start(oner[:], oner_d[:]), dma=True)
        p.op("Y", lambda e: e.dma_start(emat[:], emat_d[:]), dma=True)
        p.op("Y", lambda e: e.dma_start(bia[:], bias_d[:]), dma=True, name="consts0")
        consts_w = None
        p.op("Q", lambda e: e.dma_start(wrsb[:], wr_d[:]), dma=True, name="wr_ld")
        consts_w = p.qsnap("Y")

        for g in range(8):
            dst = T1[:, g * 4 * TH: (g * 4 + 4) * TH].rearrange("p (c t) -> p c t", c=4)
            src = xt.rearrange("(c p) t -> p c t", p=128)[:, 4 * g: 4 * g + 4, :]
            p.op("Q", lambda e, d=dst, s=src: e.dma_start(d, s), dma=True, name=f"x0b{g}")

        for c in range(DC):
            xc = x0b(c)
            sq = x2b[:, (c % 2) * TH: (c % 2) * TH + TH]
            wv_ = [f"x0b{c // 4}"]
            if c >= 2:
                wv_.append(f"st{c - 2}")
            p.op("V", lambda e, a=sq, b=xc: e.tensor_tensor(a, b, b, ALU.mult),
                 waits=wv_, name=f"sq{c}")
            st_ = (c == 0)
            sp_ = (c == DC - 1)
            wt = consts_w + [f"x0b{c // 4}", f"sq{c}"]
            p.op("T", lambda e, r=xc, s=st_, q=sp_: e.matmul(pb[0][0:1, 0:512], onec[:], r[:, 0:512], start=s, stop=q), waits=wt)
            p.op("T", lambda e, r=xc, s=st_, q=sp_: e.matmul(pb[1][0:1, 0:512], onec[:], r[:, 512:1024], start=s, stop=q))
            p.op("T", lambda e, r=xc, s=st_, q=sp_: e.matmul(pb[2][0:1, 0:2], onec[:], r[:, 1024:1026], start=s, stop=q))
            p.op("T", lambda e, r=sq, s=st_, q=sp_: e.matmul(pb[3][0:1, 0:512], onec[:], r[:, 0:512], start=s, stop=q))
            p.op("T", lambda e, r=sq, s=st_, q=sp_: e.matmul(pb[4][0:1, 0:512], onec[:], r[:, 512:1024], start=s, stop=q))
            p.op("T", lambda e, r=sq, s=st_, q=sp_: e.matmul(pb[5][0:1, 0:2], onec[:], r[:, 1024:1026], start=s, stop=q),
                 name=f"st{c}")

        def ln_post(tag, wid, statev):
            if wid == TH:
                srcs = [(pb[0], 0, 512), (pb[1], 512, 512), (pb[2], 1024, 2)]
                sqs = [(pb[3], 0, 512), (pb[4], 512, 512), (pb[5], 1024, 2)]
            else:
                srcs = [(pb[2], 0, 512), (pb[3], 512, 512)]
                sqs = [(pb[4], 0, 512), (pb[5], 512, 512)]
            rA, rB, rC, rD = 0, TH, 2 * TH, 3 * TH
            for (bk, lo, w) in srcs:
                p.op("V", lambda e, b=bk, l=lo, w=w: e.tensor_scalar_mul(
                    RR[0:1, rA + l: rA + l + w], b[0:1, 0:w], INV_D), waits=[statev])
            for (bk, lo, w) in sqs:
                p.op("V", lambda e, b=bk, l=lo, w=w: e.tensor_scalar_mul(
                    RR[0:1, rB + l: rB + l + w], b[0:1, 0:w], INV_D))
            p.op("V", lambda e, w=wid: e.tensor_tensor(
                RR[0:1, rC: rC + w], RR[0:1, rA: rA + w], RR[0:1, rA: rA + w], ALU.mult))
            p.op("V", lambda e, w=wid: e.tensor_tensor(
                RR[0:1, rD: rD + w], RR[0:1, rB: rB + w], RR[0:1, rC: rC + w], ALU.subtract),
                 name=f"{tag}_var")
            p.op("S", lambda e, w=wid: e.activation(
                RR[0:1, rB: rB + w], RR[0:1, rD: rD + w], AF.Sqrt, bias=bia[0:1, 112:113]),
                 waits=[f"{tag}_var"], name=f"{tag}_sig")
            p.op("V", lambda e, w=wid: e.reciprocal(
                RR[0:1, rC: rC + w], RR[0:1, rB: rB + w]), waits=[f"{tag}_sig"])
            p.op("V", lambda e, w=wid: e.tensor_tensor(
                RR[0:1, rD: rD + w], RR[0:1, rA: rA + w], RR[0:1, rC: rC + w], ALU.mult),
                 name=f"{tag}_rows")
            p.op("S", lambda e, w=wid: e.copy(rbf[0:1, 0:w], RR[0:1, rC: rC + w]), waits=[f"{tag}_rows"])
            p.op("S", lambda e, w=wid: e.copy(rbf[0:1, TH: TH + w], RR[0:1, rD: rD + w]))
            sset(f"{tag}_rbf")
            if wid == TH:
                regs = [(0, 0, 512), (1, 512, 512), (2, 1024, 2)]
                mregs = [(3, 0, 512), (4, 512, 512), (5, 1024, 2)]
            else:
                regs = [(2, 0, 512), (3, 512, 512)]
                mregs = [(4, 0, 512), (5, 512, 512)]
            wt = [f"{tag}_rbf", f"{tag}_rows"]
            for (bi, lo, w) in regs:
                p.op("T", lambda e, b=bi, l=lo, w=w: e.matmul(
                    pb[b][:, 0:w], oner[:], rbf[0:1, l:l + w], start=True, stop=True), waits=wt)
            for (bi, lo, w) in mregs:
                p.op("T", lambda e, b=bi, l=lo, w=w: e.matmul(
                    pb[b][:, 0:w], oner[:], rbf[0:1, TH + l: TH + l + w], start=True, stop=True))
            p.op("T", lambda e: e.nop(), name=f"{tag}_bcmm")
            for (bi, lo, w) in regs:
                p.op("V", lambda e, b=bi, l=lo, w=w: e.tensor_copy(isbc[:, l:l + w], pb[b][:, 0:w]),
                     waits=[f"{tag}_bcmm"])
            for i, (bi, lo, w) in enumerate(mregs):
                p.op("V", lambda e, b=bi, l=lo, w=w: e.tensor_copy(mobc[:, l:l + w], pb[b][:, 0:w]),
                     name=(f"{tag}_bc" if i == len(mregs) - 1 else None))

        ln_post("ln1", TH, "st31")
        brel(["b0", "b1", "b2", "b3", "b4", "b5"], "ln1_bc")
        if upto <= 0:
            return _finish(nc, p, block, sems, taps, tapd, T1, T2, RR, lgtk, comb2, hwsb, out_d, outb)

        for c in range(DC):
            xc = x0b(c)
            tmp = x2b[:, (c % 2) * TH: (c % 2) * TH + TH]
            dst = T1[:, c * TH: (c + 1) * TH]
            p.op("V", lambda e, t=tmp, a=xc: e.tensor_tensor(t, a, isbc[:], ALU.mult),
                 waits=["ln1_bc"])
            p.op("V", lambda e, d=dst, t=tmp: e.tensor_tensor(d, t, mobc[:], ALU.subtract))
            vset(f"xln{c}")
        if upto <= 1:
            return _finish(nc, p, block, sems, taps, tapd, T1, T2, RR, lgtk, comb2, hwsb, out_d, outb)

        for oc in range(24):
            for g in range(4):
                slot = (oc * 4 + g) % 4
                wt = []
                if oc >= 1:
                    wt.append(f"cvslot{oc - 1}_{slot}")
                dst = WA[:, slot * 3072: slot * 3072 + 3072]
                srcd = cw_d[oc, :, g * 3072: (g + 1) * 3072]
                p.op("Q", lambda e, d=dst, s=srcd: e.dma_start(d, s), dma=True,
                     waits=wt, name=f"cw{oc}_{g}")
            pr = ("b0", "b1") if oc % 2 == 0 else ("b2", "b3")
            pA = pb[0] if oc % 2 == 0 else pb[2]
            pB = pb[1] if oc % 2 == 0 else pb[3]
            wt0 = bfree(pr)
            for k in range(96):
                dk, tap = k // 3, k % 3
                slot = k // 24
                g = slot
                lhs = WA[:, ((oc * 4 + g) % 4) * 3072 + (k % 24) * 128:
                         ((oc * 4 + g) % 4) * 3072 + (k % 24) * 128 + 128]
                wt = list(wt0) if k == 0 else []
                wt.append(f"cw{oc}_{g}")
                wt.append(f"xln{dk}")
                st_, sp_ = (k == 0), (k == 95)
                p.op("T", lambda e, l=lhs, d=dk, tp_=tap, s=st_, q=sp_, P=pA:
                     e.matmul(P[:], l, xl1(d, tp_, 512), start=s, stop=q), waits=wt)
                nm = f"cvslot{oc}_{slot}" if k % 24 == 23 else None
                p.op("T", lambda e, l=lhs, d=dk, tp_=tap, s=st_, q=sp_, P=pB:
                     e.matmul(P[:], l, xl1(d, tp_ + 512, 512), start=s, stop=q), name=nm)
                if k == 95:
                    p.op("T", lambda e: e.nop(), name=f"conv{oc}")
            for half, P in ((0, pA), (1, pB)):
                p.op("S", lambda e, o=oc, h=half, P=P: e.add(
                    T2[:, o * 1024 + h * 512: o * 1024 + h * 512 + 512], P[:],
                    bia[:, o:o + 1]),
                    waits=[f"conv{oc}"])
            sset(f"qkvev{oc}")
            brel(pr, f"qkvev{oc}")
        if upto <= 2:
            return _finish(nc, p, block, sems, taps, tapd, T1, T2, RR, lgtk, comb2, hwsb, out_d, outb)

        p.op("Q", lambda e: e.dma_start(bvbf[:], bv_d[:]), dma=True,
             waits=["ln1_bcmm"], name="bv_ld")

        wt = bfree(("b0", "b1"))
        p.op("T", lambda e: e.matmul(pb[0][:], oner[:], bvbf[0:1, 0:512], start=True, stop=True),
             waits=wt + ["bv_ld", "wr_ld"])
        p.op("T", lambda e: e.matmul(pb[1][:], oner[:], bvbf[0:1, 512:1024], start=True, stop=True))
        p.op("T", lambda e: e.nop(), name="bvmm")
        p.op("V", lambda e: e.tensor_copy(bvbc[:, 0:512], pb[0][:]), waits=["bvmm"])
        p.op("V", lambda e: e.tensor_copy(bvbc[:, 512:1024], pb[1][:]))
        vset("bvbc_ok")
        brel(("b0", "b1"), "bvbc_ok")

        pairs = [("b0", "b1"), ("b2", "b3"), ("b4", "b5")]
        pairt = [(pb[0], pb[1]), (pb[2], pb[3]), (pb[4], pb[5])]
        gi = 0

        def pgroup():
            nonlocal gi
            r = (pairs[gi % 3], pairt[gi % 3])
            gi += 1
            return r

        for which, wdcol, evtag, btag in (("q", 0, "qh", 24), ("k", 2048, "kh", 32)):
            wdmat = wq_d if which == "q" else wk_d
            for m in range(8):
                wtd = ["conv23"] if m < 2 else [f"{evtag}_m{m - 2}"]
                p.op("Q", lambda e, m=m, wd=wdmat, col=wdcol: e.dma_start(
                    WA[:, col + (m % 2) * 1024: col + (m % 2) * 1024 + 1024], wd[m]),
                    dma=True, waits=wtd, name=f"w{which}d{m}")
                (prn, (pA, pB)) = pgroup()
                wt = bfree(prn) + [f"w{which}d{m}", "qkvev7" if which == "q" else "qkvev15"]
                for k in range(8):
                    lhs = WA[:, wdcol + (m % 2) * 1024 + k * 128: wdcol + (m % 2) * 1024 + k * 128 + 128]
                    rhsbase = (0 if which == "q" else 8192) + k * 1024
                    st_, sp_ = (k == 0), (k == 7)
                    p.op("T", lambda e, l=lhs, rb=rhsbase, s=st_, q=sp_, P=pA:
                         e.matmul(P[:], l, T2[:, rb: rb + 512], start=s, stop=q),
                         waits=(wt if k == 0 else []))
                    p.op("T", lambda e, l=lhs, rb=rhsbase, s=st_, q=sp_, P=pB:
                         e.matmul(P[:], l, T2[:, rb + 512: rb + 1024], start=s, stop=q))
                    if k == 7:
                        p.op("T", lambda e: e.nop(), name=f"{evtag}_m{m}")
                dstb = (0 if which == "q" else 8192) + m * 1024
                for half, P in ((0, pA), (1, pB)):
                    p.op("S", lambda e, db=dstb, h=half, P=P, m=m, bt=btag: e.add(
                        T1[:, db + h * 512: db + h * 512 + 512], P[:],
                        bia[:, bt + m: bt + m + 1]),
                        waits=[f"{evtag}_m{m}"])
                sset(f"{evtag}ev{m}")
                brel(prn, f"{evtag}ev{m}")

        for k in range(8):
            p.op("Q", lambda e, k=k: e.dma_start(WA[:, 4096 + k * 1024: 4096 + k * 1024 + 1024], wv_d[k]),
                 dma=True, waits=["conv23"], name=f"wvd{k}")
        for tb in range(8):
            (prn, (pA, pB)) = pgroup()
            wt = bfree(prn) + ["qkvev23", "bvbc_ok"] + [f"wvd{k}" for k in range(8)]
            for k in range(8):
                lhs = T2[:, (16 + k) * 1024 + tb * 128: (16 + k) * 1024 + tb * 128 + 128]
                st_, sp_ = (k == 0), (k == 7)
                p.op("T", lambda e, l=lhs, k=k, s=st_, q=sp_, P=pA:
                     e.matmul(P[:], l, WA[:, 4096 + k * 1024: 4096 + k * 1024 + 512], start=s, stop=q),
                     waits=(wt if k == 0 else []))
                p.op("T", lambda e, l=lhs, k=k, s=st_, q=sp_, P=pB:
                     e.matmul(P[:], l, WA[:, 4096 + k * 1024 + 512: 4096 + k * 1024 + 1024], start=s, stop=q))
                if k == 7:
                    p.op("T", lambda e: e.nop(), name=f"vh_tb{tb}")
            for half, P in ((0, pA), (1, pB)):
                p.op("V", lambda e, t=tb, h=half, P=P: e.tensor_tensor(
                    T1[:, 16384 + t * 1024 + h * 512: 16384 + t * 1024 + h * 512 + 512],
                    P[:], bvbc[:, h * 512: h * 512 + 512], ALU.add),
                    waits=[f"vh_tb{tb}"])
            vset(f"vhev{tb}")
            brel(prn, f"vhev{tb}")
        if upto <= 3:
            return _finish(nc, p, block, sems, taps, tapd, T1, T2, RR, lgtk, comb2, hwsb, out_d, outb)

        NG = min(32, agroups)
        sc_free = [bank_free.get("b4"), bank_free.get("b5")]
        pv_free = [bank_free.get("b2"), bank_free.get("b3")]
        tr_free = [None, None]

        def att_A(g):
            tb, hg, par = g // 4, g % 4, g % 2
            pS = pb[4 + par]
            wt = ["qhev7", "khev7", f"vhev{tb}"]
            if sc_free[par] is not None:
                wt.append(sc_free[par])
            for pr_ in range(2):
                hc = hg * 2 + pr_
                qf = T1[:, hc * 1024 + tb * 128: hc * 1024 + tb * 128 + 128]
                kf = T1[:, 8192 + hc * 1024 + tb * 128: 8192 + hc * 1024 + tb * 128 + 128]
                p.op("T", lambda e, a=qf, b=kf, r=2 * pr_, P=pS:
                     e.matmul(P[:, r * 128:(r + 1) * 128], a, b, start=True, stop=True),
                     waits=(wt if pr_ == 0 else []))
                p.op("T", lambda e, a=qf[0:64, :], b=kf[0:64, :], r=2 * pr_ + 1, P=pS:
                     e.matmul(P[:, r * 128:(r + 1) * 128], a, b, start=True, stop=True))
            p.op("T", lambda e: e.nop(), name=f"smm{g}")
            wts = [f"smm{g}"]
            if g >= 2:
                wts.append(f"pexp{g - 2}")
            for pr_ in range(2):
                p.op("V", lambda e, pr_=pr_, par=par, P=pS: e.tensor_copy(
                    SA[:, par * 512 + (2 * pr_) * 128: par * 512 + (2 * pr_) * 128 + 128],
                    P[:, (2 * pr_ + 1) * 128: (2 * pr_ + 2) * 128]),
                    waits=(wts if pr_ == 0 else []))
            for pr_ in range(2):
                p.op("V", lambda e, pr_=pr_, par=par, P=pS: e.tensor_tensor(
                    SA[:, par * 512 + (2 * pr_ + 1) * 128: par * 512 + (2 * pr_ + 1) * 128 + 128],
                    P[:, (2 * pr_) * 128: (2 * pr_ + 1) * 128],
                    SA[:, par * 512 + (2 * pr_) * 128: par * 512 + (2 * pr_) * 128 + 128],
                    ALU.subtract),
                    name=(f"sa{g}" if pr_ == 1 else None))
            sc_free[par] = f"sa{g}"
            wtm = []
            if g >= 2:
                wtm.append(f"pexp{g - 2}")
            p.op("V", lambda e, par=par: e.reduce_max(
                att4[:, par * 8: par * 8 + 4],
                SA[:, par * 512:(par + 1) * 512].rearrange("p (a b) -> p a b", a=4),
                axis=AX.X), waits=wtm)
            vset(f"mx{g}")
            p.op("S", lambda e, par=par: e.mul(
                att4[:, par * 8 + 4: par * 8 + 8], att4[:, par * 8: par * 8 + 4], -0.125),
                waits=[f"mx{g}"])
            wte = [f"ptr{g - 2}"] if g >= 2 else []
            for r in range(4):
                p.op("S", lambda e, par=par, r=r: e.activation(
                    PA[:, par * 512 + r * 128: par * 512 + (r + 1) * 128],
                    SA[:, par * 512 + r * 128: par * 512 + (r + 1) * 128], AF.Exp,
                    bias=att4[:, par * 8 + 4 + r: par * 8 + 5 + r], scale=0.125),
                    waits=(wte if r == 0 else []))
            sset(f"pexp{g}")

        def att_C(g):
            par = g % 2
            p.op("V", lambda e, par=par: e.tensor_tensor(
                PB[:, 0:512], PA[:, par * 512:(par + 1) * 512],
                mask4.rearrange("p (u k) -> p u k", u=1).broadcast_to([128, 4, 128]),
                ALU.mult), waits=[f"pexp{g}"])
            p.op("V", lambda e, par=par: e.reduce_sum(
                att4b[:, par * 8: par * 8 + 4],
                PB[:, 0:512].rearrange("p (a b) -> p a b", a=4), axis=AX.X))
            p.op("V", lambda e: e.tensor_copy(att4b[:, 16:20], att4[:, 0:4]))
            p.op("V", lambda e, par=par: e.reciprocal(
                att4b[:, par * 8 + 4: par * 8 + 8], att4b[:, par * 8: par * 8 + 4]))
            p.op("V", lambda e: e.tensor_copy(att4b[:, 16:20], att4[:, 4:8]))
            sset(f"recq{g}")
            for r in range(4):
                p.op("V", lambda e, par=par, r=r: e.tensor_scalar_mul(
                    PA[:, par * 512 + r * 128: par * 512 + (r + 1) * 128],
                    PB[:, r * 128: (r + 1) * 128],
                    att4b[:, par * 8 + 4 + r: par * 8 + 5 + r]))
            vset(f"pnv{g}")

        def att_D(g):
            tb, hg, par = g // 4, g % 4, g % 2
            pO = pb[2 + par]
            pT = pbt[par]
            wt = [f"pnv{g}"]
            if tr_free[par] is not None:
                wt.append(tr_free[par])
            for r in range(4):
                p.op("T", lambda e, par=par, r=r, P=pT: e.transpose(
                    P[:, r * 128:(r + 1) * 128],
                    PA[:, par * 512 + r * 128: par * 512 + (r + 1) * 128], ident[:]),
                    waits=(wt if r == 0 else []))
            p.op("T", lambda e: e.nop(), name=f"ptr{g}")
            wtp = [f"ptr{g}"]
            if g >= 1:
                wtp.append(f"pv{g - 1}")
            p.op("S", lambda e, P=pT: e.copy(PnT[:, 0:512], P[:, 0:512]), waits=wtp)
            sset(f"pnt{g}")
            tr_free[par] = f"pnt{g}"
            wtv = [f"pnt{g}"]
            if pv_free[par] is not None:
                wtv.append(pv_free[par])
            for r in range(4):
                h = hg * 4 + r
                vl = T1[:, 16384 + tb * 1024 + h * 64: 16384 + tb * 1024 + h * 64 + 64]
                colr = (r // 2) * 128
                rowo = (h % 2) * 64
                tz = (0, 64) if rowo else None
                p.op("T", lambda e, v=vl, r=r, cr=colr, ro=rowo, tz=tz, P=pO:
                     e.matmul(P[ro:ro + 64, cr:cr + 128], v,
                              PnT[:, r * 128: (r + 1) * 128],
                              start=True, stop=True, tile_position=tz),
                     waits=(wtv if r == 0 else []))
            p.op("T", lambda e: e.nop(), name=f"pv{g}")
            for cc in range(2):
                hc = hg * 2 + cc
                p.op("S", lambda e, hc=hc, tb=tb, cc=cc, P=pO: e.copy(
                    T2[:, hc * 1024 + tb * 128: hc * 1024 + tb * 128 + 128],
                    P[:, cc * 128:(cc + 1) * 128]),
                    waits=([f"pv{g}"] if cc == 0 else []))
            sset(f"ofm{g}")
            pv_free[par] = f"ofm{g}"

        for it in range(NG + 2):
            if 0 <= it - 2 < NG:
                att_D(it - 2)
            if 0 <= it - 1 < NG:
                att_C(it - 1)
            if it < NG:
                att_A(it)
        gidx = NG
        for b_, e_ in (("b4", sc_free[0]), ("b5", sc_free[1]), ("b2", pv_free[0]),
                       ("b3", pv_free[1]), ("t0", tr_free[0]), ("t1", tr_free[1])):
            if e_ is not None:
                brel((b_,), e_)
        nc._att_taps = (att4, att4b, SA, PA, PB, PnT)
        if upto <= 4:
            return _finish(nc, p, block, sems, taps, tapd, T1, T2, RR, lgtk, comb2, hwsb, out_d, outb)

        gi = 0
        for m in range(8):
            wtd = ["qh_m7"] if m < 2 else [f"wo_m{m - 2}"]
            p.op("Q", lambda e, m=m: e.dma_start(
                WA[:, (m % 2) * 1024: (m % 2) * 1024 + 1024], wo_d[m]),
                dma=True, waits=wtd, name=f"wod{m}")
            (prn, (pA, pB)) = pgroup()
            wt = bfree(prn) + [f"wod{m}", f"ofm{gidx - 1}"]
            for k in range(8):
                lhs = WA[:, (m % 2) * 1024 + k * 128: (m % 2) * 1024 + k * 128 + 128]
                st_, sp_ = (k == 0), (k == 7)
                p.op("T", lambda e, l=lhs, k=k, s=st_, q=sp_, P=pA:
                     e.matmul(P[:], l, T2[:, k * 1024: k * 1024 + 512], start=s, stop=q),
                     waits=(wt if k == 0 else []))
                p.op("T", lambda e, l=lhs, k=k, s=st_, q=sp_, P=pB:
                     e.matmul(P[:], l, T2[:, k * 1024 + 512: k * 1024 + 1024], start=s, stop=q))
                if k == 7:
                    p.op("T", lambda e: e.nop(), name=f"wo_m{m}")
            for half, P in ((0, pA), (1, pB)):
                p.op("S", lambda e, m=m, h=half, P=P: e.add(
                    T2[:, 8192 + m * 1024 + h * 512: 8192 + m * 1024 + h * 512 + 512],
                    P[:], bia[:, 40 + m: 41 + m]),
                    waits=[f"wo_m{m}"])
            sset(f"woev{m}")
            brel(prn, f"woev{m}")
        if upto <= 5:
            return _finish(nc, p, block, sems, taps, tapd, T1, T2, RR, lgtk, comb2, hwsb, out_d, outb)

        p.op("Q", lambda e: e.dma_start(WA[:, 4096: 8192], a_d[:]),
             dma=True, waits=["vh_tb7"], name="a_ld")

        wt_first = bfree(("b0", "b1")) + ["woev7"]
        wt_stats = bfree(("b2", "b3", "b4", "b5"))
        def emit_st2(c_):
            st_, sp_ = (c_ == 0), (c_ == 31)
            wt2 = (list(wt_stats) if c_ == 0 else []) + [f"x1sq{c_}"]
            p.op("T", lambda e, c_=c_, s=st_, q=sp_: e.matmul(
                pb[2][0:1, 0:512], onec[:], x1ap(c_, 0), start=s, stop=q), waits=wt2)
            p.op("T", lambda e, c_=c_, s=st_, q=sp_: e.matmul(
                pb[3][0:1, 0:512], onec[:], x1ap(c_, 1), start=s, stop=q))
            p.op("T", lambda e, c_=c_, s=st_, q=sp_: e.matmul(
                pb[4][0:1, 0:512], onec[:],
                x2b[:, (c_ % 2) * TH: (c_ % 2) * TH + 512], start=s, stop=q))
            p.op("T", lambda e, c_=c_, s=st_, q=sp_: e.matmul(
                pb[5][0:1, 0:512], onec[:],
                x2b[:, (c_ % 2) * TH + 512: (c_ % 2) * TH + 1024], start=s, stop=q))
            p.op("T", lambda e, c_=c_: e.nop(), name=f"st2_{c_}")

        for m in range(32):
            wtd = ["kh_m7"] if m < 2 else [f"proj_m{m - 2}"]
            p.op("Q", lambda e, m=m: e.dma_start(
                WA[:, 2048 + (m % 2) * 1024: 2048 + (m % 2) * 1024 + 1024], pj_d[m]),
                dma=True, waits=wtd, name=f"pjd{m}")
            wtx = [] if m < 2 else [f"x1c{m - 2}"]
            p.op("Y", lambda e, m=m: e.dma_start(
                xr[:, (m % 2) * 1024: (m % 2) * 1024 + 1024],
                xt[m * 128:(m + 1) * 128, 1: 1025]), dma=True, waits=wtx, name=f"xr{m}")
            wt = list(wt_first) if m == 0 else [f"x1ch0_{m - 1}"]
            wt.append(f"pjd{m}")
            for k in range(8):
                lhs = WA[:, 2048 + (m % 2) * 1024 + k * 128: 2048 + (m % 2) * 1024 + k * 128 + 128]
                st_, sp_ = (k == 0), (k == 7)
                p.op("T", lambda e, l=lhs, k=k, s=st_, q=sp_:
                     e.matmul(pb[0][:], l, T2[:, 8192 + k * 1024: 8192 + k * 1024 + 512], start=s, stop=q),
                     waits=(wt if k == 0 else []))
                p.op("T", lambda e, l=lhs, k=k, s=st_, q=sp_:
                     e.matmul(pb[1][:], l, T2[:, 8192 + k * 1024 + 512: 8192 + k * 1024 + 1024], start=s, stop=q),
                     waits=([f"x1c{m - 1}"] if (k == 0 and m >= 1) else []))
                if k == 7:
                    p.op("T", lambda e: e.nop(), name=f"proj_m{m}")
            if m >= 2:
                emit_st2(m - 2)
            for half in range(2):
                p.op("V", lambda e, m=m, h=half: e.scalar_tensor_tensor(
                    x1ap(m, h), pb[h][:], bia[:, 48 + m: 49 + m],
                    xr[:, (m % 2) * 1024 + h * 512: (m % 2) * 1024 + h * 512 + 512],
                    ALU.add, ALU.add),
                    waits=[f"proj_m{m}", f"xr{m}"],
                    name=(f"x1ch0_{m}" if half == 0 else None))
            vset(f"x1c{m}")
            p.op("V", lambda e, m=m: e.tensor_tensor(
                x2b[:, (m % 2) * TH: (m % 2) * TH + 1024], x1ap(m), x1ap(m), ALU.mult),
                waits=([f"st2_{m - 2}"] if m >= 2 else []))
            vset(f"x1sq{m}")


        emit_st2(30)
        emit_st2(31)
        ln_post("ln2", 1024, "st2_31")
        brel(("b2", "b3", "b4", "b5"), "ln2_bc")
        brel(("b0", "b1"), "x1c31")
        if upto <= 6:
            return _finish(nc, p, block, sems, taps, tapd, T1, T2, RR, lgtk, comb2, hwsb, out_d, outb)

        for c in range(DC):
            tmp = x2b[:, (c % 2) * TH: (c % 2) * TH + 1024]
            wt = ["ln2_bc"]
            if c == 0:
                wt.append("proj_m31")
            p.op("V", lambda e, t=tmp, c=c: e.tensor_tensor(t, x1ap(c), isbc[:, 0:1024], ALU.mult),
                 waits=wt)
            p.op("V", lambda e, t=tmp, c=c: e.tensor_tensor(
                T2[:, c * 1024:(c + 1) * 1024], t, mobc[:, 0:1024], ALU.subtract))
            vset(f"xln2_{c}")
        if upto <= 7:
            return _finish(nc, p, block, sems, taps, tapd, T1, T2, RR, lgtk, comb2, hwsb, out_d, outb)

        def emit_ffn_dmas(o):
            wtb = ["vh_tb7"] if o < 2 else [f"base_o{o - 2}"]
            p.op("Q", lambda e, o=o: e.dma_start(
                WA[:, 8192 + (o % 2) * 128: 8192 + (o % 2) * 128 + 128],
                b_d[:, o * 128: (o + 1) * 128]), dma=True, waits=wtb, name=f"bld{o}")
            if o == 0:
                wt = ["vh_tb7"]
            elif o == 1:
                wt = ["proj_m31"]
            else:
                wt = [f"base_o{o - 2}"]
            base_c = 8448 if o % 2 == 0 else 0
            p.op("Q", lambda e, o=o, bc=base_c: e.dma_start(
                WA[:, bc: bc + 4096], wb_d[o]),
                dma=True, waits=wt, name=f"wbd{o}")

        emit_ffn_dmas(0)
        emit_ffn_dmas(1)
        wrtA = bfree(("b0", "b1"))
        wrtH = bfree(("b2", "b3"))
        for k in range(DC):
            st_, sp_ = (k == 0), (k == DC - 1)
            wt = [f"xln2_{k}", "wr_ld", "a_ld"]
            if k == 0:
                wt += wrtA + wrtH
            p.op("T", lambda e, k=k, s=st_, q=sp_: e.matmul(
                pb[0][0:8, 0:512], wrsb[:, k * 8:(k + 1) * 8],
                T2[:, k * 1024: k * 1024 + 512], start=s, stop=q), waits=wt)
            p.op("T", lambda e, k=k, s=st_, q=sp_: e.matmul(
                pb[1][0:8, 0:512], wrsb[:, k * 8:(k + 1) * 8],
                T2[:, k * 1024 + 512: k * 1024 + 1024], start=s, stop=q))
            if sp_:
                p.op("T", lambda e: e.nop(), name="router_mm")
            p.op("T", lambda e, k=k, s=st_, q=sp_: e.matmul(
                pb[2][:], WA[:, 4096 + k * 128: 4096 + k * 128 + 128],
                T2[:, k * 1024: k * 1024 + 512], start=s, stop=q))
            p.op("T", lambda e, k=k, s=st_, q=sp_: e.matmul(
                pb[3][:], WA[:, 4096 + k * 128: 4096 + k * 128 + 128],
                T2[:, k * 1024 + 512: k * 1024 + 1024], start=s, stop=q))
            if sp_:
                p.op("T", lambda e: e.nop(), name="a_mm")

        def base_k_mms(o, pA, pB, wt0):
            for k in range(DC):
                bc = 8448 if o % 2 == 0 else 0
                lhs = WA[:, bc + k * 128: bc + k * 128 + 128]
                st_ = (k == 0)
                p.op("T", lambda e, l=lhs, k=k, s=st_, P=pA:
                     e.matmul(P[:], l, T2[:, k * 1024: k * 1024 + 512], start=s, stop=False),
                     waits=(wt0 if k == 0 else [f"xln2_{k}"]))
                p.op("T", lambda e, l=lhs, k=k, s=st_, P=pB:
                     e.matmul(P[:], l, T2[:, k * 1024 + 512: k * 1024 + 1024], start=s, stop=False))

        def base_lora(o, pA, pB, extra):
            lhs = b_d
            p.op("T", lambda e, o=o, P=pA: e.matmul(
                P[:], WA[:, 8192 + (o % 2) * 128: 8192 + (o % 2) * 128 + 128],
                hwsb[:, 0:512], start=False, stop=True),
                waits=["hw_ok", f"bld{o}"] + extra)
            p.op("T", lambda e, o=o, P=pB: e.matmul(
                P[:], WA[:, 8192 + (o % 2) * 128: 8192 + (o % 2) * 128 + 128],
                hwsb[:, 512:1024], start=False, stop=True))
            p.op("T", lambda e: e.nop(), name=f"base_o{o}")

        wt0 = bfree(("b4", "b5")) + ["wbd0", "xln2_0"]
        base_k_mms(0, pb[4], pb[5], wt0)

        p.op("S", lambda e: e.copy(lgfm[0:8, 0:512], pb[0][0:8, 0:512]), waits=["router_mm"])
        p.op("S", lambda e: e.copy(lgfm[0:8, 512:1024], pb[1][0:8, 0:512]))
        sset("lgfm_ok")
        wt = ["lgfm_ok"]
        if "t0" in bank_free:
            wt.append(bank_free["t0"])
        for tb in range(8):
            p.op("T", lambda e, tb=tb: e.transpose(
                pbt[0][:, tb * 8:(tb + 1) * 8], lgfm[0:8, tb * 128:(tb + 1) * 128],
                ident[0:8, 0:8]),
                waits=(wt if tb == 0 else []))
        p.op("T", lambda e: e.nop(), name="ltr_ok")
        p.op("V", lambda e: e.tensor_copy(lgtk[:], pbt[0][:, 0:64]), waits=["ltr_ok"])
        p.op("V", lambda e: e.tensor_copy(cw[:, 48:52], cw[:, 44:48]), name="lgtk_ok")

        def bc8(ap8):
            return ap8.rearrange("p (e u) -> p e u", u=1).broadcast_to([128, 8, 8])

        cA = cwk[:, 0:64]; cB = cwk[:, 64:128]; cC = cwk[:, 128:192]
        cD = cwk[:, 192:256]; cE = cwk[:, 256:320]
        p.op("V", lambda e: e.reduce_max(cw[:, 0:8], lgtk.rearrange("p (a b) -> p a b", a=8), axis=AX.X),
             waits=["lgtk_ok"])
        p.op("V", lambda e: e.tensor_copy(cw[:, 48:52], lgtk[:, 0:4]))
        p.op("V", lambda e: e.tensor_tensor(cA, lgtk[:], bc8(cw[:, 0:8]), ALU.is_ge))
        p.op("V", lambda e: e.tensor_copy(cw[:, 48:52], cw[:, 44:48]))
        p.op("V", lambda e: e.scalar_tensor_tensor(cB, cA, -1e9, lgtk[:], ALU.mult, ALU.add))
        p.op("V", lambda e: e.tensor_copy(cw[:, 48:52], cw[:, 44:48]))
        p.op("V", lambda e: e.reduce_max(cw[:, 8:16], cB.rearrange("p (a b) -> p a b", a=8), axis=AX.X))
        p.op("V", lambda e: e.tensor_tensor(cC, lgtk[:], bc8(cw[:, 0:8]), ALU.subtract))
        vset("combv1")
        p.op("S", lambda e: e.activation(cD, cC, AF.Exp), waits=["combv1"], name="combs1")
        p.op("V", lambda e: e.tensor_tensor(cE, lgtk[:], bc8(cw[:, 8:16]), ALU.is_ge))
        p.op("V", lambda e: e.tensor_copy(cw[:, 48:52], cw[:, 44:48]))
        p.op("V", lambda e: e.tensor_tensor(cA, cD, cE, ALU.mult), waits=["combs1"])
        p.op("V", lambda e: e.tensor_tensor(cw[:, 16:24], cw[:, 8:16], cw[:, 0:8], ALU.subtract))
        vset("combv2")
        p.op("S", lambda e: e.activation(cw[:, 24:32], cw[:, 16:24], AF.Exp),
             waits=["combv2"])
        sset("combs2")
        p.op("V", lambda e: e.tensor_scalar(cw[:, 32:40], cw[:, 24:32], 1.0, None, ALU.add),
             waits=["combs2"])
        p.op("V", lambda e: e.tensor_copy(cw[:, 48:52], cw[:, 44:48]))
        p.op("V", lambda e: e.reciprocal(cw[:, 40:48], cw[:, 32:40]))
        p.op("V", lambda e: e.tensor_copy(cw[:, 48:52], cw[:, 44:48]))
        p.op("V", lambda e: e.tensor_tensor(comb2[:], cA, bc8(cw[:, 40:48]), ALU.mult))
        vset("comb2_ok")

        wt = ["comb2_ok"]
        if "t1" in bank_free:
            wt.append(bank_free["t1"])
        for tb in range(8):
            p.op("T", lambda e, tb=tb: e.transpose(
                pbt[1][0:8, tb * 128:(tb + 1) * 128], comb2[:, tb * 8:(tb + 1) * 8], ident[:]),
                waits=(wt if tb == 0 else []))
        p.op("T", lambda e: e.nop(), name="ctr_ok")
        p.op("S", lambda e: e.copy(lgfm[0:8, 0:1024], pbt[1][0:8, 0:1024]),
             waits=["ctr_ok"])
        sset("cfm_ok")
        p.op("T", lambda e: e.matmul(pb[0][:], emat[:], lgfm[0:8, 0:512], start=True, stop=True),
             waits=["cfm_ok", "lgfm_ok"])
        p.op("T", lambda e: e.matmul(pb[1][:], emat[:], lgfm[0:8, 512:1024], start=True, stop=True))
        p.op("T", lambda e: e.nop(), name="expand_ok")
        p.op("V", lambda e: e.tensor_copy(cbc[:, 0:512], pb[0][:]), waits=["expand_ok"])
        p.op("V", lambda e: e.tensor_copy(cbc[:, 512:1024], pb[1][:]), name="cbc_ok")
        p.op("V", lambda e: e.tensor_tensor(hwsb[:, 0:512], pb[2][:], cbc[:, 0:512], ALU.mult),
             waits=["cbc_ok", "a_mm"])
        p.op("V", lambda e: e.tensor_tensor(hwsb[:, 512:1024], pb[3][:], cbc[:, 512:1024], ALU.mult))
        vset("hw_ok")

        base_lora(0, pb[4], pb[5], [])
        for half in range(2):
            p.op("V", lambda e, h=half: e.scalar_tensor_tensor(
                outb[:, h * 512: h * 512 + 512], pb[4 + h][:], bia[:, 80: 81],
                x1ap(0, h), ALU.add, ALU.add),
                waits=["base_o0"])
        vset("outev0")
        p.op("Y", lambda e: e.dma_start(out_d[0:128, :], outb[:, 0:1024]),
             dma=True, waits=["outev0"], name="outdma0")

        for o in range(1, 32):
            if o + 1 < 32:
                emit_ffn_dmas(o + 1)
            ev = o % 2
            pA = pb[4] if ev == 0 else pb[2]
            pB = pb[5] if ev == 0 else pb[3]
            wt0 = [f"wbd{o}"]
            if o == 1:
                wt0.append("hw_ok")
            else:
                wt0.append(f"outev{o - 2}")
            base_k_mms(o, pA, pB, wt0)
            base_lora(o, pA, pB, [])
            for half in range(2):
                P = pA if half == 0 else pB
                wt = [f"base_o{o}"]
                if o >= 2:
                    wt.append(f"outdma{o - 2}")
                p.op("V", lambda e, o=o, h=half, P=P: e.scalar_tensor_tensor(
                    outb[:, (o % 2) * 1024 + h * 512: (o % 2) * 1024 + h * 512 + 512],
                    P[:], bia[:, 80 + o: 81 + o], x1ap(o, h), ALU.add, ALU.add),
                    waits=wt)
            vset(f"outev{o}")
            p.op("Y", lambda e, o=o: e.dma_start(
                out_d[o * 128:(o + 1) * 128, :], outb[:, (o % 2) * 1024: (o % 2) * 1024 + 1024]),
                dma=True, waits=[f"outev{o}"], name=f"outdma{o}")

        p.op("Y", lambda e: e.nop(), waits=p.qsnap("Y"), inc=False)

        return _finish(nc, p, block, sems, taps, tapd, T1, T2, RR, lgtk, comb2, hwsb, out_d, outb)


def _finish(nc, p, block, sems, taps, tapd, T1, T2, RR, lgtk, comb2, hwsb, out_d, outb):
    if taps:
        wt = [("tp", p.cnt["tp"]), ("vp", p.cnt["vp"]), ("sp", p.cnt["sp"])]
        wt += p.qsnap("Q") + p.qsnap("Y")
        if "outdma31" not in p.ev:
            p.op("Y", lambda e: e.dma_start(out_d[0:128, :], outb[:, 0:1024]),
                 dma=True, waits=wt)
        p.op("Y", lambda e: e.dma_start(tapd["t1"][:], T1[:]), dma=True, waits=wt)
        p.op("Y", lambda e: e.dma_start(tapd["t2"][:], T2[:]), dma=True)
        p.op("Y", lambda e: e.dma_start(tapd["rows"][:], RR[0:1, :]), dma=True)
        p.op("Y", lambda e: e.dma_start(tapd["lgtk"][:], lgtk[:]), dma=True)
        p.op("Y", lambda e: e.dma_start(tapd["isbc"][:], nc._isbc_tap[:]), dma=True)
        if hasattr(nc, "_att_taps"):
            att4x, att4bx, SAx, PAx, PBx, PnTx = nc._att_taps
            p.op("Y", lambda e: e.dma_start(tapd["att"][:, 0:16], att4x[:]), dma=True)
            p.op("Y", lambda e: e.dma_start(tapd["att"][:, 16:32], att4bx[:, 0:16]), dma=True)
            p.op("Y", lambda e: e.dma_start(tapd["SA"][:], SAx[:]), dma=True)
            p.op("Y", lambda e: e.dma_start(tapd["PA"][:], PAx[:]), dma=True)
            p.op("Y", lambda e: e.dma_start(tapd["PB"][:], PBx[:]), dma=True)
            p.op("Y", lambda e: e.dma_start(tapd["PnT"][:], PnTx[:]), dma=True)
        p.op("Y", lambda e: e.dma_start(tapd["mobc"][:], nc._mobc_tap[:]), dma=True)
        p.op("Y", lambda e: e.dma_start(tapd["comb2"][:], comb2[:]), dma=True)
        p.op("Y", lambda e: e.dma_start(tapd["hw"][:], hwsb[:]), dma=True, name="tapsdone")
        p.op("Y", lambda e: e.nop(), waits=["tapsdone"], inc=False)
    p.emit(block, sems)
    return nc



def _prep_weights(inputs):
    f = np.float32
    conv_w = np.ascontiguousarray(inputs["L0_attn_conv_w"], dtype=f)
    cw = np.ascontiguousarray(
        conv_w.reshape(3, 32, 128, 24, 128).transpose(3, 2, 1, 0, 4)).reshape(24, 128, 12288)

    def proj_repack(w, mk, kk):
        return np.ascontiguousarray(
            w.reshape(kk, 128, mk, 128).transpose(2, 1, 0, 3)).reshape(mk, 128, kk * 128)

    wq = proj_repack(np.asarray(inputs["L0_attn_wq"], f).reshape(1024, 1024), 8, 8)
    wk = proj_repack(np.asarray(inputs["L0_attn_wk"], f).reshape(1024, 1024), 8, 8)
    wv = np.ascontiguousarray(np.asarray(inputs["L0_attn_wv"], f).reshape(8, 128, 1024))
    wo = proj_repack(np.asarray(inputs["L0_attn_wo"], f).reshape(1024, 1024), 8, 8)
    pj = proj_repack(np.asarray(inputs["L0_attn_proj_w"], f), 32, 8)
    wb = np.ascontiguousarray(
        np.asarray(inputs["L0_ffn_W_base"], f).reshape(32, 128, 32, 128).transpose(0, 3, 2, 1)
    ).reshape(32, 128, 4096)
    A2 = np.asarray(inputs["L0_ffn_A"], f).reshape(128, 4096)
    ar = np.ascontiguousarray(A2.T.reshape(32, 128, 128).transpose(1, 0, 2)).reshape(128, 4096)
    br = np.ascontiguousarray(np.asarray(inputs["L0_ffn_B"], f).transpose(0, 2, 1)).reshape(128, 4096)
    wrr = np.ascontiguousarray(
        np.asarray(inputs["L0_ffn_W_router"], f).T.reshape(32, 128, 8).transpose(1, 0, 2)
    ).reshape(128, 256)
    bvr = np.ascontiguousarray(np.asarray(inputs["L0_attn_bv"], f).reshape(1, 1024))

    biases = np.zeros((128, 113), f)
    biases[:, 112] = 1e-5
    biases[:, 0:24] = np.asarray(inputs["L0_attn_conv_b"], f).reshape(24, 128).T
    biases[:, 24:32] = np.asarray(inputs["L0_attn_bq"], f).reshape(8, 128).T
    biases[:, 32:40] = np.asarray(inputs["L0_attn_bk"], f).reshape(8, 128).T
    biases[:, 40:48] = np.asarray(inputs["L0_attn_bo"], f).reshape(8, 128).T
    biases[:, 48:80] = np.asarray(inputs["L0_attn_proj_b"], f).reshape(32, 128).T
    biases[:, 80:112] = np.asarray(inputs["L0_ffn_b_base"], f).reshape(32, 128).T

    bf = ml_dtypes.bfloat16
    ident = np.eye(128, dtype=bf)
    mask4 = np.tril(np.ones((128, 128), np.float32)).astype(bf)
    onec = np.ones((128, 1), bf)
    oner = np.ones((1, 128), bf)
    emat = np.zeros((8, 128), np.float32)
    for e_ in range(8):
        emat[e_, e_ * 16:(e_ + 1) * 16] = 2.0
    emat = emat.astype(bf)
    return dict(cw=cw, wqr=wq, wkr=wk, wvr=wv, wor=wo, pjr=pj, wbr=wb, ar=ar, br=br,
                wrr=wrr, bvr=bvr, biases=biases, identc=ident, mask4c=mask4,
                onecolc=onec, onerowc=oner, ematc=emat)


def _prep_x(inputs):
    f = np.float32
    x = np.asarray(inputs["L0_attn_inputs"], f)
    xts = []
    for c in range(NC):
        s, q0 = c // 4, (c % 4) * 1024
        xt = np.zeros((DM, TH), f)
        xt[:, 1:1025] = np.ascontiguousarray(x[s, q0:q0 + 1024, :].T)
        if c % 4 > 0:
            xt[:, 0] = x[s, q0 - 1, :]
        if c % 4 < 3:
            xt[:, 1025] = x[s, q0 + 1024, :]
        xts.append(np.ascontiguousarray(xt))
    return xts


def kernel(**inputs):
    w = _prep_weights(inputs)
    xts = _prep_x(inputs)
    nc = build()
    in_maps = [{**w, "xt": xts[c]} for c in range(NC)]
    res = run_bass_kernel_spmd(nc, in_maps, core_ids=list(range(NC)))
    out = np.empty((2, 4096, 4096), np.float32)
    for c in range(NC):
        s, q0 = c // 4, (c % 4) * 1024
        out[s, q0:q0 + 1024, :] = np.asarray(res.results[c]["out"]).T
    return out


# version 09 (final)
# speedup vs baseline: 1.0855x; 1.0026x over previous
import numpy as np
import ml_dtypes
import concourse.bass as bass
import concourse.mybir as mybir
from concourse.bass_utils import run_bass_kernel_spmd

F32 = mybir.dt.float32
BF16 = mybir.dt.bfloat16
AF = mybir.ActivationFunctionType
ALU = mybir.AluOpType
AX = mybir.AxisListType

NC = 8
T = 1024
TH = 1026
DM = 4096
DC = 32
W1 = 1024
H = 16
DH = 64
EPS = 1e-5
INV_D = 1.0 / DM

ENGMAP = {"T": "tensor", "V": "vector", "S": "scalar", "Q": "gpsimd", "Y": "sync"}
ENGSEM = {"T": "tp", "V": "vp", "S": "sp", "Q": None, "Y": None}
KD = 4
DMASEMS = [f"q{i}" for i in range(KD)] + [f"y{i}" for i in range(KD)]


class Prog:

    def __init__(self):
        self.ops = []
        self.cnt = {k: 0 for k in ("tp", "vp", "sp", *DMASEMS)}
        self.ev = {}
        self.dq = {"Q": 0, "Y": 0}

    def op(self, eng, fn, waits=(), name=None, inc=True, dma=False):
        wl = []
        for w in waits:
            if w is None:
                continue
            if isinstance(w, str):
                wl.append(self.ev[w])
            else:
                wl.append(w)
        if dma:
            pref = "q" if eng == "Q" else "y"
            sem = f"{pref}{self.dq[eng] % KD}"
            self.dq[eng] += 1
            if self.cnt[sem] > 0:
                wl.append((sem, self.cnt[sem]))
            self.cnt[sem] += 16
            if name is not None:
                self.ev[name] = (sem, self.cnt[sem])
            self.ops.append((eng, fn, wl, (sem, 16)))
            return
        sem = ENGSEM[eng]
        if inc:
            self.cnt[sem] += 1
            if name is not None:
                self.ev[name] = (sem, self.cnt[sem])
        else:
            assert name is None
        self.ops.append((eng, fn, wl, (sem, 1) if inc else None))

    def qsnap(self, eng):
        pref = "q" if eng == "Q" else "y"
        return [(f"{pref}{i}", self.cnt[f"{pref}{i}"]) for i in range(KD)
                if self.cnt[f"{pref}{i}"] > 0]

    def emit(self, block, sems):
        per = {}
        for rec in self.ops:
            per.setdefault(rec[0], []).append(rec)
        for ename, recs in per.items():
            own = ENGSEM[ename]

            def body(eng, recs=recs, own=own):
                maxw = {}
                for (_e, fn, waits, inc) in recs:
                    for (s, v) in waits:
                        if own is not None and s == own:
                            continue
                        if maxw.get(s, 0) < v:
                            eng.wait_ge(sems[s], v)
                            maxw[s] = v
                    inst = fn(eng)
                    if inc is not None:
                        s, a = inc
                        inst.then_inc(sems[s], a)

            getattr(block, ENGMAP[ename])(body)


def build(upto=99, taps=False, agroups=64):
    nc = bass.Bass("TRN2", target_bir_lowering=False)

    def din(name, shape, dt=F32):
        return nc.dram_tensor(name, list(shape), dt, kind="ExternalInput").ap()

    xt = din("xt", (DM, TH))
    cw_d = din("cw", (24, 128, 12288))
    wq_d = din("wqr", (8, 128, 1024))
    wk_d = din("wkr", (8, 128, 1024))
    wv_d = din("wvr", (8, 128, 1024))
    wo_d = din("wor", (8, 128, 1024))
    pj_d = din("pjr", (32, 128, 1024))
    wb_d = din("wbr", (32, 128, 4096))
    a_d = din("ar", (128, 4096))
    b_d = din("br", (128, 4096))
    wr_d = din("wrr", (128, 256))
    bv_d = din("bvr", (1, 1024))
    bias_d = din("biases", (128, 113))
    ident_d = din("identc", (128, 128), BF16)
    mask4_d = din("mask4c", (128, 128), BF16)
    onec_d = din("onecolc", (128, 1), BF16)
    oner_d = din("onerowc", (1, 128), BF16)
    emat_d = din("ematc", (8, 128), BF16)
    out_d = nc.dram_tensor("out", [DM, T], F32, kind="ExternalOutput").ap()
    tapd = {}
    if taps:
        tapd["t1"] = nc.dram_tensor("tap_t1", [128, 32 * 1026], BF16, kind="ExternalOutput").ap()
        tapd["t2"] = nc.dram_tensor("tap_t2", [128, 32 * 1024], BF16, kind="ExternalOutput").ap()
        tapd["rows"] = nc.dram_tensor("tap_rows", [1, 4 * TH], F32, kind="ExternalOutput").ap()
        tapd["lgtk"] = nc.dram_tensor("tap_lgtk", [128, 64], F32, kind="ExternalOutput").ap()
        tapd["comb2"] = nc.dram_tensor("tap_comb2", [128, 64], BF16, kind="ExternalOutput").ap()
        tapd["hw"] = nc.dram_tensor("tap_hw", [128, 1024], BF16, kind="ExternalOutput").ap()
        tapd["isbc"] = nc.dram_tensor("tap_isbc", [128, 1026], BF16, kind="ExternalOutput").ap()
        tapd["att"] = nc.dram_tensor("tap_att", [128, 32], F32, kind="ExternalOutput").ap()
        tapd["SA"] = nc.dram_tensor("tap_SA", [128, 1024], BF16, kind="ExternalOutput").ap()
        tapd["PA"] = nc.dram_tensor("tap_PA", [128, 1024], BF16, kind="ExternalOutput").ap()
        tapd["PB"] = nc.dram_tensor("tap_PB", [128, 512], BF16, kind="ExternalOutput").ap()
        tapd["PnT"] = nc.dram_tensor("tap_PnT", [128, 512], BF16, kind="ExternalOutput").ap()
        tapd["mobc"] = nc.dram_tensor("tap_mobc", [128, 1026], BF16, kind="ExternalOutput").ap()

    from contextlib import ExitStack
    with ExitStack() as st:
        def sb(name, shape, dt):
            return st.enter_context(nc.sbuf_tensor(name, shape, dt))

        T1 = sb("T1", [128, 32 * TH], BF16)
        T2 = sb("T2", [128, 32 * 1024], BF16)
        WA = sb("WA", [128, 12544], BF16)
        x2b = sb("x2b", [128, 2 * TH], BF16)
        RR = sb("RR", [128, 4 * TH], F32)
        rbf = sb("rbf", [128, 2 * TH], BF16)
        isbc = sb("isbc", [128, TH], BF16)
        mobc = sb("mobc", [128, TH], BF16)
        nc._isbc_tap = isbc
        nc._mobc_tap = mobc
        att4 = sb("att4", [128, 16], F32)
        att4b = sb("att4b", [128, 20], F32)
        PA = sb("PA", [128, 2 * 512], BF16)
        SA = sb("SA", [128, 2 * 512], BF16)
        PB = sb("PB", [128, 512], BF16)
        PnT = sb("PnT", [128, 512], BF16)
        xr = sb("xr", [128, 2 * 1024], F32)
        outb = sb("outb", [128, 2 * 1024], F32)
        lgtk = sb("lgtk", [128, 64], F32)
        cw = sb("cwX", [128, 52], F32)
        cwk = sb("cwk", [128, 320], F32)
        comb2 = sb("comb2", [128, 64], BF16)
        wrsb = sb("wrsb", [128, 256], BF16)
        bvbf = rbf[0:1, 0:1024]
        bvbc = isbc[:, 0:1024]
        cbc = mobc[:, 0:1024]
        hwsb = PA[:, 0:1024]
        lgfm = PA[0:8, 0:1024]
        bia = sb("bia", [128, 113], F32)
        ident = sb("ident", [128, 128], BF16)
        mask4 = sb("mask4", [128, 128], BF16)
        onec = sb("onec", [128, 1], BF16)
        oner = sb("oner", [1, 128], BF16)
        emat = sb("emat", [8, 128], BF16)

        def ps(name, shape, dt):
            return st.enter_context(nc.psum_tensor(name, shape, dt))

        pb = [ps(f"pb{i}", [128, 512], F32) for i in range(6)]
        pbt = [ps(f"pbt{i}", [128, 1024], BF16) for i in range(2)]

        sems = {k: st.enter_context(nc.semaphore(k))
                for k in ("tp", "vp", "sp", *DMASEMS)}
        block = st.enter_context(nc.Block())

        p = Prog()
        bank_free = {}

        def bfree(bids):
            return [bank_free[b] for b in bids if b in bank_free]

        def brel(bids, evname):
            for b in bids:
                bank_free[b] = evname

        def vset(name, waits=()):
            p.op("V", lambda e: e.tensor_copy(cw[:, 48:52], cw[:, 44:48]),
                 waits=waits, name=name)

        def sset(name, waits=()):
            p.op("S", lambda e: e.copy(cw[0:1, 48:52], cw[0:1, 44:48]),
                 waits=waits, name=name)

        def x1ap(m, half=None, width=1024):
            if half is None:
                return T1[:, m * 1024: m * 1024 + width]
            return T1[:, m * 1024 + half * 512: m * 1024 + half * 512 + 512]

        def x0b(c, width=TH):
            return T1[:, c * TH: c * TH + width]

        def xl1(dk, lo, width):
            base = dk * TH
            return T1[:, base + lo: base + lo + width]

        p.op("Y", lambda e: e.dma_start(ident[:], ident_d[:]), dma=True)
        p.op("Y", lambda e: e.dma_start(mask4[:], mask4_d[:]), dma=True)
        p.op("Y", lambda e: e.dma_start(onec[:], onec_d[:]), dma=True)
        p.op("Y", lambda e: e.dma_start(oner[:], oner_d[:]), dma=True)
        p.op("Y", lambda e: e.dma_start(emat[:], emat_d[:]), dma=True)
        p.op("Y", lambda e: e.dma_start(bia[:], bias_d[:]), dma=True, name="consts0")
        consts_w = None
        p.op("Q", lambda e: e.dma_start(wrsb[:], wr_d[:]), dma=True, name="wr_ld")
        consts_w = p.qsnap("Y")

        for g in range(8):
            dst = T1[:, g * 4 * TH: (g * 4 + 4) * TH].rearrange("p (c t) -> p c t", c=4)
            src = xt.rearrange("(c p) t -> p c t", p=128)[:, 4 * g: 4 * g + 4, :]
            p.op("Q", lambda e, d=dst, s=src: e.dma_start(d, s), dma=True, name=f"x0b{g}")

        for c in range(DC):
            xc = x0b(c)
            sq = x2b[:, (c % 2) * TH: (c % 2) * TH + TH]
            wv_ = [f"x0b{c // 4}"]
            if c >= 2:
                wv_.append(f"st{c - 2}")
            p.op("V", lambda e, a=sq, b=xc: e.tensor_tensor(a, b, b, ALU.mult),
                 waits=wv_, name=f"sq{c}")
            st_ = (c == 0)
            sp_ = (c == DC - 1)
            wt = consts_w + [f"x0b{c // 4}", f"sq{c}"]
            p.op("T", lambda e, r=xc, s=st_, q=sp_: e.matmul(pb[0][0:1, 0:512], onec[:], r[:, 0:512], start=s, stop=q), waits=wt)
            p.op("T", lambda e, r=xc, s=st_, q=sp_: e.matmul(pb[1][0:1, 0:512], onec[:], r[:, 512:1024], start=s, stop=q))
            p.op("T", lambda e, r=xc, s=st_, q=sp_: e.matmul(pb[2][0:1, 0:2], onec[:], r[:, 1024:1026], start=s, stop=q))
            p.op("T", lambda e, r=sq, s=st_, q=sp_: e.matmul(pb[3][0:1, 0:512], onec[:], r[:, 0:512], start=s, stop=q))
            p.op("T", lambda e, r=sq, s=st_, q=sp_: e.matmul(pb[4][0:1, 0:512], onec[:], r[:, 512:1024], start=s, stop=q))
            p.op("T", lambda e, r=sq, s=st_, q=sp_: e.matmul(pb[5][0:1, 0:2], onec[:], r[:, 1024:1026], start=s, stop=q),
                 name=f"st{c}")

        def ln_post(tag, wid, statev):
            if wid == TH:
                srcs = [(pb[0], 0, 512), (pb[1], 512, 512), (pb[2], 1024, 2)]
                sqs = [(pb[3], 0, 512), (pb[4], 512, 512), (pb[5], 1024, 2)]
            else:
                srcs = [(pb[2], 0, 512), (pb[3], 512, 512)]
                sqs = [(pb[4], 0, 512), (pb[5], 512, 512)]
            rA, rB, rC, rD = 0, TH, 2 * TH, 3 * TH
            for (bk, lo, w) in srcs:
                p.op("V", lambda e, b=bk, l=lo, w=w: e.tensor_scalar_mul(
                    RR[0:1, rA + l: rA + l + w], b[0:1, 0:w], INV_D), waits=[statev])
            for (bk, lo, w) in sqs:
                p.op("V", lambda e, b=bk, l=lo, w=w: e.tensor_scalar_mul(
                    RR[0:1, rB + l: rB + l + w], b[0:1, 0:w], INV_D))
            p.op("V", lambda e, w=wid: e.tensor_tensor(
                RR[0:1, rC: rC + w], RR[0:1, rA: rA + w], RR[0:1, rA: rA + w], ALU.mult))
            p.op("V", lambda e, w=wid: e.tensor_tensor(
                RR[0:1, rD: rD + w], RR[0:1, rB: rB + w], RR[0:1, rC: rC + w], ALU.subtract),
                 name=f"{tag}_var")
            p.op("S", lambda e, w=wid: e.activation(
                RR[0:1, rB: rB + w], RR[0:1, rD: rD + w], AF.Sqrt, bias=bia[0:1, 112:113]),
                 waits=[f"{tag}_var"], name=f"{tag}_sig")
            p.op("V", lambda e, w=wid: e.reciprocal(
                RR[0:1, rC: rC + w], RR[0:1, rB: rB + w]), waits=[f"{tag}_sig"])
            p.op("V", lambda e, w=wid: e.tensor_tensor(
                RR[0:1, rD: rD + w], RR[0:1, rA: rA + w], RR[0:1, rC: rC + w], ALU.mult),
                 name=f"{tag}_rows")
            p.op("S", lambda e, w=wid: e.copy(rbf[0:1, 0:w], RR[0:1, rC: rC + w]), waits=[f"{tag}_rows"])
            p.op("S", lambda e, w=wid: e.copy(rbf[0:1, TH: TH + w], RR[0:1, rD: rD + w]))
            sset(f"{tag}_rbf")
            if wid == TH:
                regs = [(0, 0, 512), (1, 512, 512), (2, 1024, 2)]
                mregs = [(3, 0, 512), (4, 512, 512), (5, 1024, 2)]
            else:
                regs = [(2, 0, 512), (3, 512, 512)]
                mregs = [(4, 0, 512), (5, 512, 512)]
            wt = [f"{tag}_rbf", f"{tag}_rows"]
            for (bi, lo, w) in regs:
                p.op("T", lambda e, b=bi, l=lo, w=w: e.matmul(
                    pb[b][:, 0:w], oner[:], rbf[0:1, l:l + w], start=True, stop=True), waits=wt)
            for (bi, lo, w) in mregs:
                p.op("T", lambda e, b=bi, l=lo, w=w: e.matmul(
                    pb[b][:, 0:w], oner[:], rbf[0:1, TH + l: TH + l + w], start=True, stop=True))
            p.op("T", lambda e: e.nop(), name=f"{tag}_bcmm")
            for (bi, lo, w) in regs:
                p.op("V", lambda e, b=bi, l=lo, w=w: e.tensor_copy(isbc[:, l:l + w], pb[b][:, 0:w]),
                     waits=[f"{tag}_bcmm"])
            for i, (bi, lo, w) in enumerate(mregs):
                p.op("V", lambda e, b=bi, l=lo, w=w: e.tensor_copy(mobc[:, l:l + w], pb[b][:, 0:w]),
                     name=(f"{tag}_bc" if i == len(mregs) - 1 else None))

        ln_post("ln1", TH, "st31")
        brel(["b0", "b1", "b2", "b3", "b4", "b5"], "ln1_bc")
        if upto <= 0:
            return _finish(nc, p, block, sems, taps, tapd, T1, T2, RR, lgtk, comb2, hwsb, out_d, outb)

        for c in range(DC):
            xc = x0b(c)
            tmp = x2b[:, (c % 2) * TH: (c % 2) * TH + TH]
            dst = T1[:, c * TH: (c + 1) * TH]
            p.op("V", lambda e, t=tmp, a=xc: e.tensor_tensor(t, a, isbc[:], ALU.mult),
                 waits=["ln1_bc"])
            p.op("V", lambda e, d=dst, t=tmp: e.tensor_tensor(d, t, mobc[:], ALU.subtract))
            vset(f"xln{c}")
        if upto <= 1:
            return _finish(nc, p, block, sems, taps, tapd, T1, T2, RR, lgtk, comb2, hwsb, out_d, outb)

        for oc in range(24):
            for g in range(4):
                slot = (oc * 4 + g) % 4
                wt = []
                if oc >= 1:
                    wt.append(f"cvslot{oc - 1}_{slot}")
                dst = WA[:, slot * 3072: slot * 3072 + 3072]
                srcd = cw_d[oc, :, g * 3072: (g + 1) * 3072]
                p.op("Q", lambda e, d=dst, s=srcd: e.dma_start(d, s), dma=True,
                     waits=wt, name=f"cw{oc}_{g}")
            pr = ("b0", "b1") if oc % 2 == 0 else ("b2", "b3")
            pA = pb[0] if oc % 2 == 0 else pb[2]
            pB = pb[1] if oc % 2 == 0 else pb[3]
            wt0 = bfree(pr)
            for k in range(96):
                dk, tap = k // 3, k % 3
                slot = k // 24
                g = slot
                lhs = WA[:, ((oc * 4 + g) % 4) * 3072 + (k % 24) * 128:
                         ((oc * 4 + g) % 4) * 3072 + (k % 24) * 128 + 128]
                wt = list(wt0) if k == 0 else []
                wt.append(f"cw{oc}_{g}")
                wt.append(f"xln{dk}")
                st_, sp_ = (k == 0), (k == 95)
                p.op("T", lambda e, l=lhs, d=dk, tp_=tap, s=st_, q=sp_, P=pA:
                     e.matmul(P[:], l, xl1(d, tp_, 512), start=s, stop=q), waits=wt)
                nm = f"cvslot{oc}_{slot}" if k % 24 == 23 else None
                p.op("T", lambda e, l=lhs, d=dk, tp_=tap, s=st_, q=sp_, P=pB:
                     e.matmul(P[:], l, xl1(d, tp_ + 512, 512), start=s, stop=q), name=nm)
                if k == 95:
                    p.op("T", lambda e: e.nop(), name=f"conv{oc}")
            for half, P in ((0, pA), (1, pB)):
                p.op("S", lambda e, o=oc, h=half, P=P: e.add(
                    T2[:, o * 1024 + h * 512: o * 1024 + h * 512 + 512], P[:],
                    bia[:, o:o + 1]),
                    waits=[f"conv{oc}"])
            sset(f"qkvev{oc}")
            brel(pr, f"qkvev{oc}")
        if upto <= 2:
            return _finish(nc, p, block, sems, taps, tapd, T1, T2, RR, lgtk, comb2, hwsb, out_d, outb)

        p.op("Q", lambda e: e.dma_start(bvbf[:], bv_d[:]), dma=True,
             waits=["ln1_bcmm"], name="bv_ld")

        wt = bfree(("b0", "b1"))
        p.op("T", lambda e: e.matmul(pb[0][:], oner[:], bvbf[0:1, 0:512], start=True, stop=True),
             waits=wt + ["bv_ld", "wr_ld"])
        p.op("T", lambda e: e.matmul(pb[1][:], oner[:], bvbf[0:1, 512:1024], start=True, stop=True))
        p.op("T", lambda e: e.nop(), name="bvmm")
        p.op("V", lambda e: e.tensor_copy(bvbc[:, 0:512], pb[0][:]), waits=["bvmm"])
        p.op("V", lambda e: e.tensor_copy(bvbc[:, 512:1024], pb[1][:]))
        vset("bvbc_ok")
        brel(("b0", "b1"), "bvbc_ok")

        pairs = [("b0", "b1"), ("b2", "b3"), ("b4", "b5")]
        pairt = [(pb[0], pb[1]), (pb[2], pb[3]), (pb[4], pb[5])]
        gi = 0

        def pgroup():
            nonlocal gi
            r = (pairs[gi % 3], pairt[gi % 3])
            gi += 1
            return r

        for which, wdcol, evtag, btag in (("q", 0, "qh", 24), ("k", 2048, "kh", 32)):
            wdmat = wq_d if which == "q" else wk_d
            for m in range(8):
                wtd = ["conv23"] if m < 2 else [f"{evtag}_m{m - 2}"]
                p.op("Q", lambda e, m=m, wd=wdmat, col=wdcol: e.dma_start(
                    WA[:, col + (m % 2) * 1024: col + (m % 2) * 1024 + 1024], wd[m]),
                    dma=True, waits=wtd, name=f"w{which}d{m}")
                (prn, (pA, pB)) = pgroup()
                wt = bfree(prn) + [f"w{which}d{m}", "qkvev7" if which == "q" else "qkvev15"]
                for k in range(8):
                    lhs = WA[:, wdcol + (m % 2) * 1024 + k * 128: wdcol + (m % 2) * 1024 + k * 128 + 128]
                    rhsbase = (0 if which == "q" else 8192) + k * 1024
                    st_, sp_ = (k == 0), (k == 7)
                    p.op("T", lambda e, l=lhs, rb=rhsbase, s=st_, q=sp_, P=pA:
                         e.matmul(P[:], l, T2[:, rb: rb + 512], start=s, stop=q),
                         waits=(wt if k == 0 else []))
                    p.op("T", lambda e, l=lhs, rb=rhsbase, s=st_, q=sp_, P=pB:
                         e.matmul(P[:], l, T2[:, rb + 512: rb + 1024], start=s, stop=q))
                    if k == 7:
                        p.op("T", lambda e: e.nop(), name=f"{evtag}_m{m}")
                dstb = (0 if which == "q" else 8192) + m * 1024
                if which == "q":
                    for half, P in ((0, pA), (1, pB)):
                        p.op("V", lambda e, db=dstb, h=half, P=P, m=m, bt=btag: e.tensor_scalar(
                            T1[:, db + h * 512: db + h * 512 + 512], P[:],
                            bia[:, bt + m: bt + m + 1], 0.125, ALU.add, ALU.mult),
                            waits=[f"{evtag}_m{m}"])
                    vset(f"{evtag}ev{m}")
                else:
                    for half, P in ((0, pA), (1, pB)):
                        p.op("S", lambda e, db=dstb, h=half, P=P, m=m, bt=btag: e.add(
                            T1[:, db + h * 512: db + h * 512 + 512], P[:],
                            bia[:, bt + m: bt + m + 1]),
                            waits=[f"{evtag}_m{m}"])
                    sset(f"{evtag}ev{m}")
                brel(prn, f"{evtag}ev{m}")

        for k in range(8):
            p.op("Q", lambda e, k=k: e.dma_start(WA[:, 4096 + k * 1024: 4096 + k * 1024 + 1024], wv_d[k]),
                 dma=True, waits=["conv23"], name=f"wvd{k}")
        for tb in range(8):
            (prn, (pA, pB)) = pgroup()
            wt = bfree(prn) + ["qkvev23", "bvbc_ok"] + [f"wvd{k}" for k in range(8)]
            for k in range(8):
                lhs = T2[:, (16 + k) * 1024 + tb * 128: (16 + k) * 1024 + tb * 128 + 128]
                st_, sp_ = (k == 0), (k == 7)
                p.op("T", lambda e, l=lhs, k=k, s=st_, q=sp_, P=pA:
                     e.matmul(P[:], l, WA[:, 4096 + k * 1024: 4096 + k * 1024 + 512], start=s, stop=q),
                     waits=(wt if k == 0 else []))
                p.op("T", lambda e, l=lhs, k=k, s=st_, q=sp_, P=pB:
                     e.matmul(P[:], l, WA[:, 4096 + k * 1024 + 512: 4096 + k * 1024 + 1024], start=s, stop=q))
                if k == 7:
                    p.op("T", lambda e: e.nop(), name=f"vh_tb{tb}")
            for half, P in ((0, pA), (1, pB)):
                p.op("V", lambda e, t=tb, h=half, P=P: e.tensor_tensor(
                    T1[:, 16384 + t * 1024 + h * 512: 16384 + t * 1024 + h * 512 + 512],
                    P[:], bvbc[:, h * 512: h * 512 + 512], ALU.add),
                    waits=[f"vh_tb{tb}"])
            vset(f"vhev{tb}")
            brel(prn, f"vhev{tb}")
        if upto <= 3:
            return _finish(nc, p, block, sems, taps, tapd, T1, T2, RR, lgtk, comb2, hwsb, out_d, outb)

        NG = min(32, agroups)
        sc_free = [bank_free.get("b4"), bank_free.get("b5")]
        pv_free = [bank_free.get("b2"), bank_free.get("b3")]
        tr_free = [None, None]

        def att_A(g):
            tb, hg, par = g // 4, g % 4, g % 2
            pS = pb[4 + par]
            wt = ["qhev7", "khev7", f"vhev{tb}"]
            if sc_free[par] is not None:
                wt.append(sc_free[par])
            for pr_ in range(2):
                hc = hg * 2 + pr_
                qf = T1[:, hc * 1024 + tb * 128: hc * 1024 + tb * 128 + 128]
                kf = T1[:, 8192 + hc * 1024 + tb * 128: 8192 + hc * 1024 + tb * 128 + 128]
                p.op("T", lambda e, a=qf, b=kf, r=2 * pr_, P=pS:
                     e.matmul(P[:, r * 128:(r + 1) * 128], a, b, start=True, stop=True),
                     waits=(wt if pr_ == 0 else []))
                p.op("T", lambda e, a=qf[0:64, :], b=kf[0:64, :], r=2 * pr_ + 1, P=pS:
                     e.matmul(P[:, r * 128:(r + 1) * 128], a, b, start=True, stop=True))
            p.op("T", lambda e: e.nop(), name=f"smm{g}")
            wts = [f"smm{g}"]
            if g >= 2:
                wts.append(f"pexp{g - 2}")
            for pr_ in range(2):
                p.op("V", lambda e, pr_=pr_, par=par, P=pS: e.tensor_copy(
                    SA[:, par * 512 + (2 * pr_) * 128: par * 512 + (2 * pr_) * 128 + 128],
                    P[:, (2 * pr_ + 1) * 128: (2 * pr_ + 2) * 128]),
                    waits=(wts if pr_ == 0 else []))
            for pr_ in range(2):
                p.op("V", lambda e, pr_=pr_, par=par, P=pS: e.tensor_tensor(
                    SA[:, par * 512 + (2 * pr_ + 1) * 128: par * 512 + (2 * pr_ + 1) * 128 + 128],
                    P[:, (2 * pr_) * 128: (2 * pr_ + 1) * 128],
                    SA[:, par * 512 + (2 * pr_) * 128: par * 512 + (2 * pr_) * 128 + 128],
                    ALU.subtract),
                    name=(f"sa{g}" if pr_ == 1 else None))
            sc_free[par] = f"sa{g}"
            wtm = []
            if g >= 2:
                wtm.append(f"pexp{g - 2}")
            p.op("V", lambda e, par=par: e.reduce_max(
                att4[:, par * 8: par * 8 + 4],
                SA[:, par * 512:(par + 1) * 512].rearrange("p (a b) -> p a b", a=4),
                axis=AX.X, negate=True), waits=wtm)
            vset(f"mx{g}")
            wte = [f"mx{g}"] + ([f"ptr{g - 2}"] if g >= 2 else [])
            for r in range(4):
                p.op("S", lambda e, par=par, r=r: e.activation(
                    PA[:, par * 512 + r * 128: par * 512 + (r + 1) * 128],
                    SA[:, par * 512 + r * 128: par * 512 + (r + 1) * 128], AF.Exp,
                    bias=att4[:, par * 8 + r: par * 8 + r + 1], scale=1.0),
                    waits=(wte if r == 0 else []))
            sset(f"pexp{g}")

        def att_C(g):
            par = g % 2
            p.op("V", lambda e, par=par: e.tensor_tensor(
                PB[:, 0:512], PA[:, par * 512:(par + 1) * 512],
                mask4.rearrange("p (u k) -> p u k", u=1).broadcast_to([128, 4, 128]),
                ALU.mult), waits=[f"pexp{g}"])
            p.op("V", lambda e, par=par: e.reduce_sum(
                att4b[:, par * 8: par * 8 + 4],
                PB[:, 0:512].rearrange("p (a b) -> p a b", a=4), axis=AX.X))
            p.op("V", lambda e: e.tensor_copy(att4b[:, 16:20], att4[:, 0:4]))
            p.op("V", lambda e, par=par: e.reciprocal(
                att4b[:, par * 8 + 4: par * 8 + 8], att4b[:, par * 8: par * 8 + 4]))
            p.op("V", lambda e: e.tensor_copy(att4b[:, 16:20], att4[:, 4:8]))
            sset(f"recq{g}")
            for r in range(4):
                p.op("V", lambda e, par=par, r=r: e.tensor_scalar_mul(
                    PA[:, par * 512 + r * 128: par * 512 + (r + 1) * 128],
                    PB[:, r * 128: (r + 1) * 128],
                    att4b[:, par * 8 + 4 + r: par * 8 + 5 + r]))
            vset(f"pnv{g}")

        def att_D(g):
            tb, hg, par = g // 4, g % 4, g % 2
            pO = pb[2 + par]
            pT = pbt[par]
            wt = [f"pnv{g}"]
            if tr_free[par] is not None:
                wt.append(tr_free[par])
            for r in range(4):
                p.op("T", lambda e, par=par, r=r, P=pT: e.transpose(
                    P[:, r * 128:(r + 1) * 128],
                    PA[:, par * 512 + r * 128: par * 512 + (r + 1) * 128], ident[:]),
                    waits=(wt if r == 0 else []))
            p.op("T", lambda e: e.nop(), name=f"ptr{g}")
            wtp = [f"ptr{g}"]
            if g >= 1:
                wtp.append(f"pv{g - 1}")
            p.op("S", lambda e, P=pT: e.copy(PnT[:, 0:512], P[:, 0:512]), waits=wtp)
            sset(f"pnt{g}")
            tr_free[par] = f"pnt{g}"
            wtv = [f"pnt{g}"]
            if pv_free[par] is not None:
                wtv.append(pv_free[par])
            for r in range(4):
                h = hg * 4 + r
                vl = T1[:, 16384 + tb * 1024 + h * 64: 16384 + tb * 1024 + h * 64 + 64]
                colr = (r // 2) * 128
                rowo = (h % 2) * 64
                tz = (0, 64) if rowo else None
                p.op("T", lambda e, v=vl, r=r, cr=colr, ro=rowo, tz=tz, P=pO:
                     e.matmul(P[ro:ro + 64, cr:cr + 128], v,
                              PnT[:, r * 128: (r + 1) * 128],
                              start=True, stop=True, tile_position=tz),
                     waits=(wtv if r == 0 else []))
            p.op("T", lambda e: e.nop(), name=f"pv{g}")
            for cc in range(2):
                hc = hg * 2 + cc
                p.op("S", lambda e, hc=hc, tb=tb, cc=cc, P=pO: e.copy(
                    T2[:, hc * 1024 + tb * 128: hc * 1024 + tb * 128 + 128],
                    P[:, cc * 128:(cc + 1) * 128]),
                    waits=([f"pv{g}"] if cc == 0 else []))
            sset(f"ofm{g}")
            pv_free[par] = f"ofm{g}"

        for it in range(NG + 2):
            if 0 <= it - 2 < NG:
                att_D(it - 2)
            if 0 <= it - 1 < NG:
                att_C(it - 1)
            if it < NG:
                att_A(it)
        gidx = NG
        for b_, e_ in (("b4", sc_free[0]), ("b5", sc_free[1]), ("b2", pv_free[0]),
                       ("b3", pv_free[1]), ("t0", tr_free[0]), ("t1", tr_free[1])):
            if e_ is not None:
                brel((b_,), e_)
        nc._att_taps = (att4, att4b, SA, PA, PB, PnT)
        if upto <= 4:
            return _finish(nc, p, block, sems, taps, tapd, T1, T2, RR, lgtk, comb2, hwsb, out_d, outb)

        gi = 0
        for m in range(8):
            wtd = ["qh_m7"] if m < 2 else [f"wo_m{m - 2}"]
            p.op("Q", lambda e, m=m: e.dma_start(
                WA[:, (m % 2) * 1024: (m % 2) * 1024 + 1024], wo_d[m]),
                dma=True, waits=wtd, name=f"wod{m}")
            (prn, (pA, pB)) = pgroup()
            wt = bfree(prn) + [f"wod{m}", f"ofm{gidx - 1}"]
            for k in range(8):
                lhs = WA[:, (m % 2) * 1024 + k * 128: (m % 2) * 1024 + k * 128 + 128]
                st_, sp_ = (k == 0), (k == 7)
                p.op("T", lambda e, l=lhs, k=k, s=st_, q=sp_, P=pA:
                     e.matmul(P[:], l, T2[:, k * 1024: k * 1024 + 512], start=s, stop=q),
                     waits=(wt if k == 0 else []))
                p.op("T", lambda e, l=lhs, k=k, s=st_, q=sp_, P=pB:
                     e.matmul(P[:], l, T2[:, k * 1024 + 512: k * 1024 + 1024], start=s, stop=q))
                if k == 7:
                    p.op("T", lambda e: e.nop(), name=f"wo_m{m}")
            for half, P in ((0, pA), (1, pB)):
                p.op("S", lambda e, m=m, h=half, P=P: e.add(
                    T2[:, 8192 + m * 1024 + h * 512: 8192 + m * 1024 + h * 512 + 512],
                    P[:], bia[:, 40 + m: 41 + m]),
                    waits=[f"wo_m{m}"])
            sset(f"woev{m}")
            brel(prn, f"woev{m}")
        if upto <= 5:
            return _finish(nc, p, block, sems, taps, tapd, T1, T2, RR, lgtk, comb2, hwsb, out_d, outb)

        p.op("Q", lambda e: e.dma_start(WA[:, 4096: 8192], a_d[:]),
             dma=True, waits=["vh_tb7"], name="a_ld")

        wt_first = bfree(("b0", "b1")) + ["woev7"]
        wt_stats = bfree(("b2", "b3", "b4", "b5"))
        def emit_st2(c_):
            st_, sp_ = (c_ == 0), (c_ == 31)
            wt2 = (list(wt_stats) if c_ == 0 else []) + [f"x1sq{c_}"]
            p.op("T", lambda e, c_=c_, s=st_, q=sp_: e.matmul(
                pb[2][0:1, 0:512], onec[:], x1ap(c_, 0), start=s, stop=q), waits=wt2)
            p.op("T", lambda e, c_=c_, s=st_, q=sp_: e.matmul(
                pb[3][0:1, 0:512], onec[:], x1ap(c_, 1), start=s, stop=q))
            p.op("T", lambda e, c_=c_, s=st_, q=sp_: e.matmul(
                pb[4][0:1, 0:512], onec[:],
                x2b[:, (c_ % 2) * TH: (c_ % 2) * TH + 512], start=s, stop=q))
            p.op("T", lambda e, c_=c_, s=st_, q=sp_: e.matmul(
                pb[5][0:1, 0:512], onec[:],
                x2b[:, (c_ % 2) * TH + 512: (c_ % 2) * TH + 1024], start=s, stop=q))
            p.op("T", lambda e, c_=c_: e.nop(), name=f"st2_{c_}")

        for m in range(32):
            wtd = ["kh_m7"] if m < 2 else [f"proj_m{m - 2}"]
            p.op("Q", lambda e, m=m: e.dma_start(
                WA[:, 2048 + (m % 2) * 1024: 2048 + (m % 2) * 1024 + 1024], pj_d[m]),
                dma=True, waits=wtd, name=f"pjd{m}")
            wtx = [] if m < 2 else [f"x1c{m - 2}"]
            p.op("Y", lambda e, m=m: e.dma_start(
                xr[:, (m % 2) * 1024: (m % 2) * 1024 + 1024],
                xt[m * 128:(m + 1) * 128, 1: 1025]), dma=True, waits=wtx, name=f"xr{m}")
            wt = list(wt_first) if m == 0 else [f"x1ch0_{m - 1}"]
            wt.append(f"pjd{m}")
            for k in range(8):
                lhs = WA[:, 2048 + (m % 2) * 1024 + k * 128: 2048 + (m % 2) * 1024 + k * 128 + 128]
                st_, sp_ = (k == 0), (k == 7)
                p.op("T", lambda e, l=lhs, k=k, s=st_, q=sp_:
                     e.matmul(pb[0][:], l, T2[:, 8192 + k * 1024: 8192 + k * 1024 + 512], start=s, stop=q),
                     waits=(wt if k == 0 else []))
                p.op("T", lambda e, l=lhs, k=k, s=st_, q=sp_:
                     e.matmul(pb[1][:], l, T2[:, 8192 + k * 1024 + 512: 8192 + k * 1024 + 1024], start=s, stop=q),
                     waits=([f"x1c{m - 1}"] if (k == 0 and m >= 1) else []))
                if k == 7:
                    p.op("T", lambda e: e.nop(), name=f"proj_m{m}")
            if m >= 2:
                emit_st2(m - 2)
            for half in range(2):
                p.op("V", lambda e, m=m, h=half: e.scalar_tensor_tensor(
                    x1ap(m, h), pb[h][:], bia[:, 48 + m: 49 + m],
                    xr[:, (m % 2) * 1024 + h * 512: (m % 2) * 1024 + h * 512 + 512],
                    ALU.add, ALU.add),
                    waits=[f"proj_m{m}", f"xr{m}"],
                    name=(f"x1ch0_{m}" if half == 0 else None))
            vset(f"x1c{m}")
            p.op("V", lambda e, m=m: e.tensor_tensor(
                x2b[:, (m % 2) * TH: (m % 2) * TH + 1024], x1ap(m), x1ap(m), ALU.mult),
                waits=([f"st2_{m - 2}"] if m >= 2 else []))
            vset(f"x1sq{m}")


        emit_st2(30)
        emit_st2(31)
        ln_post("ln2", 1024, "st2_31")
        brel(("b2", "b3", "b4", "b5"), "ln2_bc")
        brel(("b0", "b1"), "x1c31")
        if upto <= 6:
            return _finish(nc, p, block, sems, taps, tapd, T1, T2, RR, lgtk, comb2, hwsb, out_d, outb)

        for c in range(DC):
            tmp = x2b[:, (c % 2) * TH: (c % 2) * TH + 1024]
            wt = ["ln2_bc"]
            if c == 0:
                wt.append("proj_m31")
            p.op("V", lambda e, t=tmp, c=c: e.tensor_tensor(t, x1ap(c), isbc[:, 0:1024], ALU.mult),
                 waits=wt)
            p.op("V", lambda e, t=tmp, c=c: e.tensor_tensor(
                T2[:, c * 1024:(c + 1) * 1024], t, mobc[:, 0:1024], ALU.subtract))
            vset(f"xln2_{c}")
        if upto <= 7:
            return _finish(nc, p, block, sems, taps, tapd, T1, T2, RR, lgtk, comb2, hwsb, out_d, outb)

        def emit_ffn_dmas(o):
            wtb = ["vh_tb7"] if o < 2 else [f"base_o{o - 2}"]
            p.op("Q", lambda e, o=o: e.dma_start(
                WA[:, 8192 + (o % 2) * 128: 8192 + (o % 2) * 128 + 128],
                b_d[:, o * 128: (o + 1) * 128]), dma=True, waits=wtb, name=f"bld{o}")
            if o == 0:
                wt = ["vh_tb7"]
            elif o == 1:
                wt = ["proj_m31"]
            else:
                wt = [f"base_o{o - 2}"]
            base_c = 8448 if o % 2 == 0 else 0
            p.op("Q", lambda e, o=o, bc=base_c: e.dma_start(
                WA[:, bc: bc + 4096], wb_d[o]),
                dma=True, waits=wt, name=f"wbd{o}")

        emit_ffn_dmas(0)
        emit_ffn_dmas(1)
        wrtA = bfree(("b0", "b1"))
        wrtH = bfree(("b2", "b3"))
        for k in range(DC):
            st_, sp_ = (k == 0), (k == DC - 1)
            wt = [f"xln2_{k}", "wr_ld", "a_ld"]
            if k == 0:
                wt += wrtA + wrtH
            p.op("T", lambda e, k=k, s=st_, q=sp_: e.matmul(
                pb[0][0:8, 0:512], wrsb[:, k * 8:(k + 1) * 8],
                T2[:, k * 1024: k * 1024 + 512], start=s, stop=q), waits=wt)
            p.op("T", lambda e, k=k, s=st_, q=sp_: e.matmul(
                pb[1][0:8, 0:512], wrsb[:, k * 8:(k + 1) * 8],
                T2[:, k * 1024 + 512: k * 1024 + 1024], start=s, stop=q))
            if sp_:
                p.op("T", lambda e: e.nop(), name="router_mm")
            p.op("T", lambda e, k=k, s=st_, q=sp_: e.matmul(
                pb[2][:], WA[:, 4096 + k * 128: 4096 + k * 128 + 128],
                T2[:, k * 1024: k * 1024 + 512], start=s, stop=q))
            p.op("T", lambda e, k=k, s=st_, q=sp_: e.matmul(
                pb[3][:], WA[:, 4096 + k * 128: 4096 + k * 128 + 128],
                T2[:, k * 1024 + 512: k * 1024 + 1024], start=s, stop=q))
            if sp_:
                p.op("T", lambda e: e.nop(), name="a_mm")

        def base_k_mms(o, pA, pB, wt0):
            for k in range(DC):
                bc = 8448 if o % 2 == 0 else 0
                lhs = WA[:, bc + k * 128: bc + k * 128 + 128]
                st_ = (k == 0)
                p.op("T", lambda e, l=lhs, k=k, s=st_, P=pA:
                     e.matmul(P[:], l, T2[:, k * 1024: k * 1024 + 512], start=s, stop=False),
                     waits=(wt0 if k == 0 else [f"xln2_{k}"]))
                p.op("T", lambda e, l=lhs, k=k, s=st_, P=pB:
                     e.matmul(P[:], l, T2[:, k * 1024 + 512: k * 1024 + 1024], start=s, stop=False))

        def base_lora(o, pA, pB, extra):
            lhs = b_d
            p.op("T", lambda e, o=o, P=pA: e.matmul(
                P[:], WA[:, 8192 + (o % 2) * 128: 8192 + (o % 2) * 128 + 128],
                hwsb[:, 0:512], start=False, stop=True),
                waits=["hw_ok", f"bld{o}"] + extra)
            p.op("T", lambda e, o=o, P=pB: e.matmul(
                P[:], WA[:, 8192 + (o % 2) * 128: 8192 + (o % 2) * 128 + 128],
                hwsb[:, 512:1024], start=False, stop=True))
            p.op("T", lambda e: e.nop(), name=f"base_o{o}")

        wt0 = bfree(("b4", "b5")) + ["wbd0", "xln2_0"]
        base_k_mms(0, pb[4], pb[5], wt0)

        p.op("S", lambda e: e.copy(lgfm[0:8, 0:512], pb[0][0:8, 0:512]), waits=["router_mm"])
        p.op("S", lambda e: e.copy(lgfm[0:8, 512:1024], pb[1][0:8, 0:512]))
        sset("lgfm_ok")
        wt = ["lgfm_ok"]
        if "t0" in bank_free:
            wt.append(bank_free["t0"])
        for tb in range(8):
            p.op("T", lambda e, tb=tb: e.transpose(
                pbt[0][:, tb * 8:(tb + 1) * 8], lgfm[0:8, tb * 128:(tb + 1) * 128],
                ident[0:8, 0:8]),
                waits=(wt if tb == 0 else []))
        p.op("T", lambda e: e.nop(), name="ltr_ok")
        p.op("V", lambda e: e.tensor_copy(lgtk[:], pbt[0][:, 0:64]), waits=["ltr_ok"])
        p.op("V", lambda e: e.tensor_copy(cw[:, 48:52], cw[:, 44:48]), name="lgtk_ok")

        def bc8(ap8):
            return ap8.rearrange("p (e u) -> p e u", u=1).broadcast_to([128, 8, 8])

        cA = cwk[:, 0:64]; cB = cwk[:, 64:128]; cC = cwk[:, 128:192]
        cD = cwk[:, 192:256]; cE = cwk[:, 256:320]
        p.op("V", lambda e: e.reduce_max(cw[:, 0:8], lgtk.rearrange("p (a b) -> p a b", a=8), axis=AX.X),
             waits=["lgtk_ok"])
        p.op("V", lambda e: e.tensor_copy(cw[:, 48:52], lgtk[:, 0:4]))
        p.op("V", lambda e: e.tensor_tensor(cA, lgtk[:], bc8(cw[:, 0:8]), ALU.is_ge))
        p.op("V", lambda e: e.tensor_copy(cw[:, 48:52], cw[:, 44:48]))
        p.op("V", lambda e: e.scalar_tensor_tensor(cB, cA, -1e9, lgtk[:], ALU.mult, ALU.add))
        p.op("V", lambda e: e.tensor_copy(cw[:, 48:52], cw[:, 44:48]))
        p.op("V", lambda e: e.reduce_max(cw[:, 8:16], cB.rearrange("p (a b) -> p a b", a=8), axis=AX.X))
        p.op("V", lambda e: e.tensor_tensor(cC, lgtk[:], bc8(cw[:, 0:8]), ALU.subtract))
        vset("combv1")
        p.op("S", lambda e: e.activation(cD, cC, AF.Exp), waits=["combv1"], name="combs1")
        p.op("V", lambda e: e.tensor_tensor(cE, lgtk[:], bc8(cw[:, 8:16]), ALU.is_ge))
        p.op("V", lambda e: e.tensor_copy(cw[:, 48:52], cw[:, 44:48]))
        p.op("V", lambda e: e.tensor_tensor(cA, cD, cE, ALU.mult), waits=["combs1"])
        p.op("V", lambda e: e.tensor_tensor(cw[:, 16:24], cw[:, 8:16], cw[:, 0:8], ALU.subtract))
        vset("combv2")
        p.op("S", lambda e: e.activation(cw[:, 24:32], cw[:, 16:24], AF.Exp),
             waits=["combv2"])
        sset("combs2")
        p.op("V", lambda e: e.tensor_scalar(cw[:, 32:40], cw[:, 24:32], 1.0, None, ALU.add),
             waits=["combs2"])
        p.op("V", lambda e: e.tensor_copy(cw[:, 48:52], cw[:, 44:48]))
        p.op("V", lambda e: e.reciprocal(cw[:, 40:48], cw[:, 32:40]))
        p.op("V", lambda e: e.tensor_copy(cw[:, 48:52], cw[:, 44:48]))
        p.op("V", lambda e: e.tensor_tensor(comb2[:], cA, bc8(cw[:, 40:48]), ALU.mult))
        vset("comb2_ok")

        wt = ["comb2_ok"]
        if "t1" in bank_free:
            wt.append(bank_free["t1"])
        for tb in range(8):
            p.op("T", lambda e, tb=tb: e.transpose(
                pbt[1][0:8, tb * 128:(tb + 1) * 128], comb2[:, tb * 8:(tb + 1) * 8], ident[:]),
                waits=(wt if tb == 0 else []))
        p.op("T", lambda e: e.nop(), name="ctr_ok")
        p.op("S", lambda e: e.copy(lgfm[0:8, 0:1024], pbt[1][0:8, 0:1024]),
             waits=["ctr_ok"])
        sset("cfm_ok")
        p.op("T", lambda e: e.matmul(pb[0][:], emat[:], lgfm[0:8, 0:512], start=True, stop=True),
             waits=["cfm_ok", "lgfm_ok"])
        p.op("T", lambda e: e.matmul(pb[1][:], emat[:], lgfm[0:8, 512:1024], start=True, stop=True))
        p.op("T", lambda e: e.nop(), name="expand_ok")
        p.op("V", lambda e: e.tensor_copy(cbc[:, 0:512], pb[0][:]), waits=["expand_ok"])
        p.op("V", lambda e: e.tensor_copy(cbc[:, 512:1024], pb[1][:]), name="cbc_ok")
        p.op("V", lambda e: e.tensor_tensor(hwsb[:, 0:512], pb[2][:], cbc[:, 0:512], ALU.mult),
             waits=["cbc_ok", "a_mm"])
        p.op("V", lambda e: e.tensor_tensor(hwsb[:, 512:1024], pb[3][:], cbc[:, 512:1024], ALU.mult))
        vset("hw_ok")

        base_lora(0, pb[4], pb[5], [])
        for half in range(2):
            p.op("V", lambda e, h=half: e.scalar_tensor_tensor(
                outb[:, h * 512: h * 512 + 512], pb[4 + h][:], bia[:, 80: 81],
                x1ap(0, h), ALU.add, ALU.add),
                waits=["base_o0"])
        vset("outev0")
        p.op("Y", lambda e: e.dma_start(out_d[0:128, :], outb[:, 0:1024]),
             dma=True, waits=["outev0"], name="outdma0")

        for o in range(1, 32):
            if o + 1 < 32:
                emit_ffn_dmas(o + 1)
            ev = o % 2
            pA = pb[4] if ev == 0 else pb[2]
            pB = pb[5] if ev == 0 else pb[3]
            wt0 = [f"wbd{o}"]
            if o == 1:
                wt0.append("hw_ok")
            else:
                wt0.append(f"outev{o - 2}")
            base_k_mms(o, pA, pB, wt0)
            base_lora(o, pA, pB, [])
            for half in range(2):
                P = pA if half == 0 else pB
                wt = [f"base_o{o}"]
                if o >= 2:
                    wt.append(f"outdma{o - 2}")
                p.op("V", lambda e, o=o, h=half, P=P: e.scalar_tensor_tensor(
                    outb[:, (o % 2) * 1024 + h * 512: (o % 2) * 1024 + h * 512 + 512],
                    P[:], bia[:, 80 + o: 81 + o], x1ap(o, h), ALU.add, ALU.add),
                    waits=wt)
            vset(f"outev{o}")
            p.op("Y", lambda e, o=o: e.dma_start(
                out_d[o * 128:(o + 1) * 128, :], outb[:, (o % 2) * 1024: (o % 2) * 1024 + 1024]),
                dma=True, waits=[f"outev{o}"], name=f"outdma{o}")

        p.op("Y", lambda e: e.nop(), waits=p.qsnap("Y"), inc=False)

        return _finish(nc, p, block, sems, taps, tapd, T1, T2, RR, lgtk, comb2, hwsb, out_d, outb)


def _finish(nc, p, block, sems, taps, tapd, T1, T2, RR, lgtk, comb2, hwsb, out_d, outb):
    if taps:
        wt = [("tp", p.cnt["tp"]), ("vp", p.cnt["vp"]), ("sp", p.cnt["sp"])]
        wt += p.qsnap("Q") + p.qsnap("Y")
        if "outdma31" not in p.ev:
            p.op("Y", lambda e: e.dma_start(out_d[0:128, :], outb[:, 0:1024]),
                 dma=True, waits=wt)
        p.op("Y", lambda e: e.dma_start(tapd["t1"][:], T1[:]), dma=True, waits=wt)
        p.op("Y", lambda e: e.dma_start(tapd["t2"][:], T2[:]), dma=True)
        p.op("Y", lambda e: e.dma_start(tapd["rows"][:], RR[0:1, :]), dma=True)
        p.op("Y", lambda e: e.dma_start(tapd["lgtk"][:], lgtk[:]), dma=True)
        p.op("Y", lambda e: e.dma_start(tapd["isbc"][:], nc._isbc_tap[:]), dma=True)
        if hasattr(nc, "_att_taps"):
            att4x, att4bx, SAx, PAx, PBx, PnTx = nc._att_taps
            p.op("Y", lambda e: e.dma_start(tapd["att"][:, 0:16], att4x[:]), dma=True)
            p.op("Y", lambda e: e.dma_start(tapd["att"][:, 16:32], att4bx[:, 0:16]), dma=True)
            p.op("Y", lambda e: e.dma_start(tapd["SA"][:], SAx[:]), dma=True)
            p.op("Y", lambda e: e.dma_start(tapd["PA"][:], PAx[:]), dma=True)
            p.op("Y", lambda e: e.dma_start(tapd["PB"][:], PBx[:]), dma=True)
            p.op("Y", lambda e: e.dma_start(tapd["PnT"][:], PnTx[:]), dma=True)
        p.op("Y", lambda e: e.dma_start(tapd["mobc"][:], nc._mobc_tap[:]), dma=True)
        p.op("Y", lambda e: e.dma_start(tapd["comb2"][:], comb2[:]), dma=True)
        p.op("Y", lambda e: e.dma_start(tapd["hw"][:], hwsb[:]), dma=True, name="tapsdone")
        p.op("Y", lambda e: e.nop(), waits=["tapsdone"], inc=False)
    p.emit(block, sems)
    return nc



def _prep_weights(inputs):
    f = np.float32
    conv_w = np.ascontiguousarray(inputs["L0_attn_conv_w"], dtype=f)
    cw = np.ascontiguousarray(
        conv_w.reshape(3, 32, 128, 24, 128).transpose(3, 2, 1, 0, 4)).reshape(24, 128, 12288)

    def proj_repack(w, mk, kk):
        return np.ascontiguousarray(
            w.reshape(kk, 128, mk, 128).transpose(2, 1, 0, 3)).reshape(mk, 128, kk * 128)

    wq = proj_repack(np.asarray(inputs["L0_attn_wq"], f).reshape(1024, 1024), 8, 8)
    wk = proj_repack(np.asarray(inputs["L0_attn_wk"], f).reshape(1024, 1024), 8, 8)
    wv = np.ascontiguousarray(np.asarray(inputs["L0_attn_wv"], f).reshape(8, 128, 1024))
    wo = proj_repack(np.asarray(inputs["L0_attn_wo"], f).reshape(1024, 1024), 8, 8)
    pj = proj_repack(np.asarray(inputs["L0_attn_proj_w"], f), 32, 8)
    wb = np.ascontiguousarray(
        np.asarray(inputs["L0_ffn_W_base"], f).reshape(32, 128, 32, 128).transpose(0, 3, 2, 1)
    ).reshape(32, 128, 4096)
    A2 = np.asarray(inputs["L0_ffn_A"], f).reshape(128, 4096)
    ar = np.ascontiguousarray(A2.T.reshape(32, 128, 128).transpose(1, 0, 2)).reshape(128, 4096)
    br = np.ascontiguousarray(np.asarray(inputs["L0_ffn_B"], f).transpose(0, 2, 1)).reshape(128, 4096)
    wrr = np.ascontiguousarray(
        np.asarray(inputs["L0_ffn_W_router"], f).T.reshape(32, 128, 8).transpose(1, 0, 2)
    ).reshape(128, 256)
    bvr = np.ascontiguousarray(np.asarray(inputs["L0_attn_bv"], f).reshape(1, 1024))

    biases = np.zeros((128, 113), f)
    biases[:, 112] = 1e-5
    biases[:, 0:24] = np.asarray(inputs["L0_attn_conv_b"], f).reshape(24, 128).T
    biases[:, 24:32] = np.asarray(inputs["L0_attn_bq"], f).reshape(8, 128).T
    biases[:, 32:40] = np.asarray(inputs["L0_attn_bk"], f).reshape(8, 128).T
    biases[:, 40:48] = np.asarray(inputs["L0_attn_bo"], f).reshape(8, 128).T
    biases[:, 48:80] = np.asarray(inputs["L0_attn_proj_b"], f).reshape(32, 128).T
    biases[:, 80:112] = np.asarray(inputs["L0_ffn_b_base"], f).reshape(32, 128).T

    bf = ml_dtypes.bfloat16
    ident = np.eye(128, dtype=bf)
    mask4 = np.tril(np.ones((128, 128), np.float32)).astype(bf)
    onec = np.ones((128, 1), bf)
    oner = np.ones((1, 128), bf)
    emat = np.zeros((8, 128), np.float32)
    for e_ in range(8):
        emat[e_, e_ * 16:(e_ + 1) * 16] = 2.0
    emat = emat.astype(bf)
    return dict(cw=cw, wqr=wq, wkr=wk, wvr=wv, wor=wo, pjr=pj, wbr=wb, ar=ar, br=br,
                wrr=wrr, bvr=bvr, biases=biases, identc=ident, mask4c=mask4,
                onecolc=onec, onerowc=oner, ematc=emat)


def _prep_x(inputs):
    f = np.float32
    x = np.asarray(inputs["L0_attn_inputs"], f)
    xts = []
    for c in range(NC):
        s, q0 = c // 4, (c % 4) * 1024
        xt = np.zeros((DM, TH), f)
        xt[:, 1:1025] = np.ascontiguousarray(x[s, q0:q0 + 1024, :].T)
        if c % 4 > 0:
            xt[:, 0] = x[s, q0 - 1, :]
        if c % 4 < 3:
            xt[:, 1025] = x[s, q0 + 1024, :]
        xts.append(np.ascontiguousarray(xt))
    return xts


def kernel(**inputs):
    w = _prep_weights(inputs)
    xts = _prep_x(inputs)
    nc = build()
    in_maps = [{**w, "xt": xts[c]} for c in range(NC)]
    res = run_bass_kernel_spmd(nc, in_maps, core_ids=list(range(NC)))
    out = np.empty((2, 4096, 4096), np.float32)
    for c in range(NC):
        s, q0 = c // 4, (c % 4) * 1024
        out[s, q0:q0 + 1024, :] = np.asarray(res.results[c]["out"]).T
    return out
